# version 01
# baseline (speedup 1.0000x reference)
import numpy as np
import ml_dtypes
import concourse.bass as bass
import concourse.mybir as mybir
from concourse.bass_utils import run_bass_kernel_spmd
from concourse.alu_op_type import AluOpType
from contextlib import ExitStack

F32 = mybir.dt.float32
BF16 = mybir.dt.bfloat16
I32 = mybir.dt.int32
AF = mybir.ActivationFunctionType
OP = AluOpType
AX = mybir.AxisListType.X

S = 8192
D = 4096
H = 32
NOWN = 1024
HID = 11008
NHT = HID // 128
SCALE = float(1.0 / np.sqrt(np.float32(128.0)))
EPS = 1e-5
NEG = -66504.0

_DBG_KIND = {}
_PHASES = 99


class Ctr:
    def __init__(self, sem):
        self.sem = sem
        self.n = 0


def _build():
    nc = bass.Bass("TRN2", target_bir_lowering=False)
    ap = lambda t: t.ap()

    x_full = ap(nc.dram_tensor("x_full", [S, D], F32, kind="ExternalInput"))
    x_own = ap(nc.dram_tensor("x_own", [NOWN, D], F32, kind="ExternalInput"))
    qkv_w = ap(nc.dram_tensor("qkv_w", [D, 3 * D], F32, kind="ExternalInput"))
    qkv_bT = ap(nc.dram_tensor("qkv_bT", [128, 96], F32, kind="ExternalInput"))
    vb_row = ap(nc.dram_tensor("vb_row", [1, D], F32, kind="ExternalInput"))
    out_w = ap(nc.dram_tensor("out_w", [D, D], F32, kind="ExternalInput"))
    ob_row = ap(nc.dram_tensor("ob_row", [1, D], F32, kind="ExternalInput"))
    w1p = ap(nc.dram_tensor("w1p", [352256, 64], I32, kind="ExternalInput"))
    s1v = ap(nc.dram_tensor("s1v", [704512], F32, kind="ExternalInput"))
    w2p = ap(nc.dram_tensor("w2p", [352256, 64], I32, kind="ExternalInput"))
    s2v = ap(nc.dram_tensor("s2v", [704512], F32, kind="ExternalInput"))
    w3p = ap(nc.dram_tensor("w3p", [352256, 64], I32, kind="ExternalInput"))
    s3v = ap(nc.dram_tensor("s3v", [704512], F32, kind="ExternalInput"))
    maskin = ap(nc.dram_tensor("maskin", [8, 128, 1024], F32, kind="ExternalInput"))
    cosk = ap(nc.dram_tensor("cosk", [128, S], F32, kind="ExternalInput"))
    sink = ap(nc.dram_tensor("sink", [128, S], F32, kind="ExternalInput"))
    cosq = ap(nc.dram_tensor("cosq", [128, NOWN], F32, kind="ExternalInput"))
    sinq = ap(nc.dram_tensor("sinq", [128, NOWN], F32, kind="ExternalInput"))
    identb = ap(nc.dram_tensor("identb", [128, 128], BF16, kind="ExternalInput"))
    permT = ap(nc.dram_tensor("permT", [128, 128], BF16, kind="ExternalInput"))

    out_own = ap(nc.dram_tensor("out_own", [NOWN, D], F32, kind="ExternalOutput"))

    def scr(name, shape, dt):
        return ap(nc.dram_tensor(name, shape, dt, kind=_DBG_KIND.get(name, "Internal")))

    xfm = scr("xfm", [D, S], BF16)
    kscr = scr("kscr", [D, S], BF16)
    vscr = scr("vscr", [S, D], BF16)
    qscr = scr("qscr", [D, NOWN], BF16)
    ctxscr = scr("ctxscr", [NOWN, D], BF16)
    x2scr = scr("x2scr", [NOWN, D], F32)
    r1scr = scr("r1scr", [NOWN, D], F32)
    r2scr = scr("r2scr", [NOWN, D], F32)
    hactscr = scr("hactscr", [HID, NOWN], BF16)
    w2tscr = scr("w2tscr", [HID, D], BF16)

    es = ExitStack()
    with es:
        LS = [Ctr(es.enter_context(nc.semaphore(f"LS{i}"))) for i in range(2)]
        LC = [Ctr(es.enter_context(nc.semaphore(f"LC{i}"))) for i in range(2)]
        LG = [Ctr(es.enter_context(nc.semaphore(f"LG{i}"))) for i in range(2)]
        TE = Ctr(es.enter_context(nc.semaphore("TEs")))
        VE = Ctr(es.enter_context(nc.semaphore("VEs")))
        SE = Ctr(es.enter_context(nc.semaphore("SEs")))
        BAR = Ctr(es.enter_context(nc.semaphore("BARs")))

        sb_qkvbT = es.enter_context(nc.sbuf_tensor("sb_qkvbT", [128, 96], F32))
        sb_vbrow = es.enter_context(nc.sbuf_tensor("sb_vbrow", [1, D], BF16))
        sb_obrow = es.enter_context(nc.sbuf_tensor("sb_obrow", [1, D], BF16))
        sb_vbf = es.enter_context(nc.sbuf_tensor("sb_vbf", [1, D], F32))
        sb_obf = es.enter_context(nc.sbuf_tensor("sb_obf", [1, D], F32))
        sb_id = es.enter_context(nc.sbuf_tensor("sb_id", [128, 128], BF16))
        sb_perm = es.enter_context(nc.sbuf_tensor("sb_perm", [128, 128], BF16))
        sb_ones = es.enter_context(nc.sbuf_tensor("sb_ones", [1, 512], BF16))

        blk = es.enter_context(nc.Block())

        @blk.gpsimd
        def _(_g):
            sy, sc_, gp, ve, te = nc.sync, nc.scalar, nc.gpsimd, nc.vector, nc.tensor

            def dmaS(out, in_, par=0):
                sy.dma_start(out=out, in_=in_).then_inc(LS[par].sem, 16)
                LS[par].n += 16
                return (LS[par], LS[par].n)

            def dmaC(out, in_, par=0):
                sc_.dma_start(out=out, in_=in_).then_inc(LC[par].sem, 16)
                LC[par].n += 16
                return (LC[par], LC[par].n)

            def dmaG(out, in_, par=0):
                gp.dma_start(out=out, in_=in_).then_inc(LG[par].sem, 16)
                LG[par].n += 16
                return (LG[par], LG[par].n)

            def wq_(eng, tok):
                eng.wait_ge(tok[0].sem, tok[1])

            def te_mark(instr):
                instr.then_inc(TE.sem, 1)
                TE.n += 1
                return TE.n

            def vstart():
                ve.wait_ge(VE.sem, VE.n)

            tg_state = {}

            def emit_tgroups(nblk, g, psl, src_fn, dst_fn):
                st = tg_state.setdefault(psl[0].name, {0: 0, 1: 0})
                ngr = (nblk + g - 1) // g
                for jg in range(ngr):
                    gi = min(g, nblk - jg * g)
                    bank = jg % 2
                    if st[bank]:
                        te.wait_ge(VE.sem, st[bank])
                    ti = None
                    for jj in range(gi):
                        ti = te.transpose(psl[bank][:, jj * 1024: jj * 1024 + 128],
                                          src_fn(jg * g + jj), sb_id[:])
                    t = te_mark(ti)
                    ve.wait_ge(TE.sem, t)
                    vstart()
                    ve.tensor_copy(
                        out=dst_fn(jg, gi),
                        in_=psl[bank][:].rearrange("p (a q) -> p a q", q=1024)[:, 0:gi, 0:128],
                    ).then_inc(VE.sem, 1)
                    VE.n += 1
                    st[bank] = VE.n

            def barrier():
                for c in LS:
                    sy.wait_ge(c.sem, c.n)
                for c in LC:
                    sc_.wait_ge(c.sem, c.n)
                for c in LG:
                    gp.wait_ge(c.sem, c.n)
                for e in (te, ve, sc_, gp, sy):
                    e.sem_inc(BAR.sem, 1)
                BAR.n += 5
                for e in (te, ve, sc_, gp, sy):
                    e.wait_ge(BAR.sem, BAR.n)

            dmaS(sb_qkvbT[:], qkv_bT[:])
            dmaS(sb_vbf[:], vb_row[:])
            dmaS(sb_obf[:], ob_row[:])
            dmaS(sb_id[:], identb[:])
            dmaS(sb_perm[:], permT[:])
            ve.wait_ge(LS[0].sem, LS[0].n)
            ve.memset(sb_ones[:], 1.0)
            ve.tensor_copy(out=sb_vbrow[:], in_=sb_vbf[:])
            ve.tensor_copy(out=sb_obrow[:], in_=sb_obf[:]).then_inc(VE.sem, 1)
            VE.n += 1
            barrier()
            if _PHASES <= 0:
                return

            with ExitStack() as ph:
                xtok = [ph.enter_context(nc.sbuf_tensor(f"xtok{b}", [128, 4 * D], BF16)) for b in range(2)]
                stg = [ph.enter_context(nc.sbuf_tensor(f"stg{b}", [128, 32 * 512], BF16)) for b in range(2)]
                psT = [ph.enter_context(nc.psum_tensor(f"psT{b}", [128, 4096], BF16)) for b in range(2)]
                te_chunk, ls_chunk, ve_chunk = {}, {}, {}
                for i in range(16):
                    b = i % 2
                    if i >= 2:
                        gp.wait_ge(TE.sem, te_chunk[i - 2])
                    for p in range(4):
                        dmaG(xtok[b][:, p * D:(p + 1) * D],
                             x_full[i * 512 + p * 128: i * 512 + (p + 1) * 128, :], par=b)
                    te.wait_ge(LG[b].sem, LG[b].n)
                    if i >= 2:
                        te.wait_ge(LS[b].sem, ls_chunk[i - 2])
                    emit_tgroups(
                        128, 4, psT,
                        lambda k, b=b: xtok[b][:, (k % 4) * D + (k // 4) * 128: (k % 4) * D + (k // 4) * 128 + 128],
                        lambda jg, gi, b=b: stg[b][:, jg * 512: jg * 512 + gi * 128].rearrange("p (a q) -> p a q", q=128))
                    te_chunk[i] = TE.n
                    sy.wait_ge(VE.sem, VE.n)
                    for j in range(32):
                        dmaS(xfm[j * 128:(j + 1) * 128, i * 512:(i + 1) * 512],
                             stg[b][:, j * 512:(j + 1) * 512], par=b)
                    ls_chunk[i] = LS[b].n
                    ve_chunk[i] = VE.n
            barrier()
            if _PHASES <= 1:
                return

            with ExitStack() as ph:
                wband = ph.enter_context(nc.sbuf_tensor("wband", [128, 32 * 1024], BF16))
                xfc = [ph.enter_context(nc.sbuf_tensor(f"xfc{b}", [128, 32 * 512], BF16)) for b in range(2)]
                ckv = [ph.enter_context(nc.sbuf_tensor(f"ckv{b}", [128, 512], F32)) for b in range(2)]
                skv = [ph.enter_context(nc.sbuf_tensor(f"skv{b}", [128, 512], F32)) for b in range(2)]
                kb = [ph.enter_context(nc.sbuf_tensor(f"kb{b}", [128, 512], BF16)) for b in range(2)]
                m1 = ph.enter_context(nc.sbuf_tensor("m1t", [128, 512], F32))
                m2 = ph.enter_context(nc.sbuf_tensor("m2t", [128, 512], F32))
                kro = [ph.enter_context(nc.sbuf_tensor(f"kro{b}", [128, 512], BF16)) for b in range(2)]
                psK = [ph.enter_context(nc.psum_tensor(f"psK{b}", [128, 512], F32)) for b in range(2)]
                psP = [ph.enter_context(nc.psum_tensor(f"psP{b}", [128, 512], F32)) for b in range(2)]

                ik = 0
                ev_psK = {}
                ve_rope = {}
                lc_st = {}
                se_kb = {}
                te_chunkb = {}
                ve_chunkb = {}
                for band in range(8):
                    is_k = band < 4
                    colbase = D + band * 1024 if is_k else 2 * D + (band - 4) * 1024
                    gp.wait_ge(TE.sem, TE.n)
                    for j in range(32):
                        dmaG(wband[:, j * 1024:(j + 1) * 1024],
                             qkv_w[j * 128:(j + 1) * 128, colbase:colbase + 1024], par=0)
                    for i in range(16):
                        b = i % 2
                        if (band, i) != (0, 0):
                            key = (band, i - 2) if i >= 2 else (band - 1, 14 + i)
                            if key in te_chunkb:
                                sy.wait_ge(TE.sem, te_chunkb[key])
                            if key in ve_chunkb:
                                sy.wait_ge(VE.sem, ve_chunkb[key])
                        dmaS(xfc[b][:].rearrange("p (j t) -> p j t", t=512),
                             xfm.rearrange("(j p) t -> p j t", p=128)[:, :, i * 512:(i + 1) * 512], par=b)
                        if is_k:
                            dmaS(ckv[b][:], cosk[:, i * 512:(i + 1) * 512], par=b)
                            dmaS(skv[b][:], sink[:, i * 512:(i + 1) * 512], par=b)
                        te.wait_ge(LG[0].sem, LG[0].n)
                        te.wait_ge(LS[b].sem, LS[b].n)
                        if is_k:
                            for oc in range(8):
                                head = band * 8 + oc
                                pb = ik % 2
                                if ik >= 2:
                                    ctr, n = ev_psK[ik - 2]
                                    te.wait_ge(ctr.sem, n)
                                for ds in range(32):
                                    mm = te.matmul(psK[pb][:],
                                                   wband[:, ds * 1024 + oc * 128: ds * 1024 + oc * 128 + 128],
                                                   xfc[b][:, ds * 512:(ds + 1) * 512],
                                                   start=(ds == 0), stop=(ds == 31))
                                t = te_mark(mm)
                                sc_.wait_ge(TE.sem, t)
                                if ik >= 2:
                                    sc_.wait_ge(VE.sem, ve_rope[ik - 2])
                                sc_.activation(out=kb[pb][:], in_=psK[pb][:], func=AF.Identity,
                                               bias=sb_qkvbT[:, 32 + head:33 + head],
                                               scale=1.0).then_inc(SE.sem, 1)
                                SE.n += 1
                                ev_psK[ik] = (SE, SE.n)
                                se_kb[ik] = SE.n
                                te.wait_ge(SE.sem, SE.n)
                                if ik >= 2:
                                    te.wait_ge(VE.sem, ve_rope[ik - 2])
                                t = te_mark(te.matmul(psP[pb][:], sb_perm[:], kb[pb][:],
                                                      start=True, stop=True))
                                ve.wait_ge(TE.sem, t)
                                vstart()
                                ve.tensor_tensor(out=m1[:], in0=kb[pb][:], in1=ckv[b][:],
                                                 op=OP.mult)
                                ve.tensor_tensor(out=m2[:], in0=psP[pb][:], in1=skv[b][:],
                                                 op=OP.mult).then_inc(VE.sem, 1)
                                VE.n += 1
                                ve.wait_ge(VE.sem, VE.n)
                                if ik >= 2:
                                    ve.wait_ge(LC[pb].sem, lc_st[ik - 2])
                                ve.tensor_tensor(out=kro[pb][:], in0=m1[:], in1=m2[:],
                                                 op=OP.add).then_inc(VE.sem, 1)
                                VE.n += 1
                                ve_rope[ik] = VE.n
                                sc_.wait_ge(VE.sem, VE.n)
                                dmaC(kscr[head * 128:(head + 1) * 128, i * 512:(i + 1) * 512],
                                     kro[pb][:], par=pb)
                                lc_st[ik] = LC[pb].n
                                ik += 1
                        else:
                            vs = (band - 4) * 2
                            for sl in range(2):
                                for p in range(4):
                                    pb = ik % 2
                                    if ik >= 2:
                                        ctr, n = ev_psK[ik - 2]
                                        te.wait_ge(ctr.sem, n)
                                    for ds in range(32):
                                        te.matmul(psK[pb][:],
                                                  xfc[b][:, ds * 512 + p * 128: ds * 512 + (p + 1) * 128],
                                                  wband[:, ds * 1024 + sl * 512:(ds * 1024) + (sl + 1) * 512],
                                                  start=(ds == 0), stop=False)
                                    t = te_mark(te.matmul(psK[pb][:], sb_ones[:, 0:128],
                                                          sb_vbrow[:, (vs + sl) * 512:(vs + sl + 1) * 512],
                                                          start=False, stop=True))
                                    sc_.wait_ge(TE.sem, t)
                                    if ik >= 2:
                                        sc_.wait_ge(LC[pb].sem, lc_st[ik - 2])
                                    sc_.activation(out=kro[pb][:], in_=psK[pb][:],
                                                   func=AF.Copy).then_inc(SE.sem, 1)
                                    SE.n += 1
                                    ev_psK[ik] = (SE, SE.n)
                                    sc_.wait_ge(SE.sem, SE.n)
                                    dmaC(vscr[i * 512 + p * 128: i * 512 + (p + 1) * 128,
                                              (vs + sl) * 512:(vs + sl + 1) * 512],
                                         kro[pb][:], par=pb)
                                    lc_st[ik] = LC[pb].n
                                    ik += 1
                        te_chunkb[(band, i)] = TE.n
                        ve_chunkb[(band, i)] = VE.n
            barrier()
            if _PHASES <= 2:
                return

            with ExitStack() as ph:
                xof = ph.enter_context(nc.sbuf_tensor("xof", [128, 32 * NOWN], BF16))
                xot = ph.enter_context(nc.sbuf_tensor("xot", [128, 4 * D], BF16))
                wq = ph.enter_context(nc.sbuf_tensor("wq", [128, 32 * 512], BF16))
                cq = ph.enter_context(nc.sbuf_tensor("cq", [128, NOWN], F32))
                sqt = ph.enter_context(nc.sbuf_tensor("sqt", [128, NOWN], F32))
                qb = [ph.enter_context(nc.sbuf_tensor(f"qb{b}", [128, 512], BF16)) for b in range(2)]
                qro = [ph.enter_context(nc.sbuf_tensor(f"qro{b}", [128, 512], BF16)) for b in range(2)]
                qm1 = ph.enter_context(nc.sbuf_tensor("qm1", [128, 512], F32))
                qm2 = ph.enter_context(nc.sbuf_tensor("qm2", [128, 512], F32))
                psQ = [ph.enter_context(nc.psum_tensor(f"psQ{b}", [128, 512], F32)) for b in range(2)]
                psQP = [ph.enter_context(nc.psum_tensor(f"psQP{b}", [128, 512], F32)) for b in range(2)]
                psXB = [ph.enter_context(nc.psum_tensor(f"psXB{b}", [128, 2048], BF16)) for b in range(2)]
                dmaS(cq[:], cosq[:])
                dmaS(sqt[:], sinq[:])
                ve.wait_ge(LS[0].sem, LS[0].n)
                for i in range(2):
                    if i == 1:
                        gp.wait_ge(TE.sem, TE.n)
                    for p in range(4):
                        dmaG(xot[:, p * D:(p + 1) * D],
                             x_own[i * 512 + p * 128: i * 512 + (p + 1) * 128, :])
                    te.wait_ge(LG[0].sem, LG[0].n)
                    te.wait_ge(LS[0].sem, LS[0].n)
                    def _xof_dst(jg, gi, i=i):
                        j = (2 * jg) // 4
                        p0 = (2 * jg) % 4
                        base = j * NOWN + i * 512 + p0 * 128
                        return xof[:, base: base + gi * 128].rearrange("p (a q) -> p a q", q=128)
                    emit_tgroups(
                        128, 2, psXB,
                        lambda k: xot[:, (k % 4) * D + (k // 4) * 128: (k % 4) * D + (k // 4) * 128 + 128],
                        _xof_dst)
                te.wait_ge(VE.sem, VE.n)
                iq = 0
                ev_psQ, ve_ropeq, lc_q, se_qb = {}, {}, {}, {}
                for quarter in range(8):
                    gp.wait_ge(TE.sem, TE.n)
                    for j in range(32):
                        dmaG(wq[:, j * 512:(j + 1) * 512],
                             qkv_w[j * 128:(j + 1) * 128, quarter * 512:(quarter + 1) * 512])
                    te.wait_ge(LG[0].sem, LG[0].n)
                    for oc in range(4):
                        head = quarter * 4 + oc
                        for qc in range(2):
                            pb = iq % 2
                            if iq >= 2:
                                ctr, n = ev_psQ[iq - 2]
                                te.wait_ge(ctr.sem, n)
                            for ds in range(32):
                                mm = te.matmul(psQ[pb][:],
                                               wq[:, ds * 512 + oc * 128: ds * 512 + oc * 128 + 128],
                                               xof[:, ds * NOWN + qc * 512: ds * NOWN + (qc + 1) * 512],
                                               start=(ds == 0), stop=(ds == 31))
                            t = te_mark(mm)
                            sc_.wait_ge(TE.sem, t)
                            if iq >= 2:
                                sc_.wait_ge(VE.sem, ve_ropeq[iq - 2])
                            sc_.activation(out=qb[pb][:], in_=psQ[pb][:], func=AF.Identity,
                                           bias=sb_qkvbT[:, head:head + 1],
                                           scale=1.0).then_inc(SE.sem, 1)
                            SE.n += 1
                            ev_psQ[iq] = (SE, SE.n)
                            te.wait_ge(SE.sem, SE.n)
                            if iq >= 2:
                                te.wait_ge(VE.sem, ve_ropeq[iq - 2])
                            t = te_mark(te.matmul(psQP[pb][:], sb_perm[:], qb[pb][:],
                                                  start=True, stop=True))
                            ve.wait_ge(TE.sem, t)
                            vstart()
                            ve.tensor_tensor(out=qm1[:], in0=qb[pb][:],
                                             in1=cq[:, qc * 512:(qc + 1) * 512], op=OP.mult)
                            ve.tensor_tensor(out=qm2[:], in0=psQP[pb][:],
                                             in1=sqt[:, qc * 512:(qc + 1) * 512],
                                             op=OP.mult).then_inc(VE.sem, 1)
                            VE.n += 1
                            ve.wait_ge(VE.sem, VE.n)
                            if iq >= 2:
                                ve.wait_ge(LC[pb].sem, lc_q[iq - 2])
                            ve.tensor_tensor(out=qro[pb][:], in0=qm1[:], in1=qm2[:],
                                             op=OP.add).then_inc(VE.sem, 1)
                            VE.n += 1
                            ve_ropeq[iq] = VE.n
                            sc_.wait_ge(VE.sem, VE.n)
                            dmaC(qscr[head * 128:(head + 1) * 128, qc * 512:(qc + 1) * 512],
                                 qro[pb][:], par=pb)
                            lc_q[iq] = LC[pb].n
                            iq += 1
            barrier()
            if _PHASES <= 3:
                return

            with ExitStack() as ph:
                masks = ph.enter_context(nc.sbuf_tensor("masks", [128, 8 * 1024], F32))
                kh = [ph.enter_context(nc.sbuf_tensor(f"kh{b}", [128, S], BF16)) for b in range(2)]
                vh = [ph.enter_context(nc.sbuf_tensor(f"vh{b}", [128, 64 * 128], BF16)) for b in range(2)]
                qrow = [ph.enter_context(nc.sbuf_tensor(f"qrow{b}", [128, NOWN], BF16)) for b in range(2)]
                scf = ph.enter_context(nc.sbuf_tensor("scf", [128, S], F32))
                prb = ph.enter_context(nc.sbuf_tensor("prb", [128, S], BF16))
                ptb = [ph.enter_context(nc.sbuf_tensor(f"ptb{b}", [128, 128], BF16)) for b in range(2)]
                ctxo = ph.enter_context(nc.sbuf_tensor("ctxo", [128, 128], BF16))
                nmax = ph.enter_context(nc.sbuf_tensor("nmax", [128, 1], F32))
                ssum = ph.enter_context(nc.sbuf_tensor("ssum", [128, 1], F32))
                rsum = ph.enter_context(nc.sbuf_tensor("rsum", [128, 1], F32))
                psSc = [ph.enter_context(nc.psum_tensor(f"psSc{b}", [128, 512], F32)) for b in range(2)]
                psPT = [ph.enter_context(nc.psum_tensor(f"psPT{b}", [128, 128], BF16)) for b in range(2)]
                psCtx = ph.enter_context(nc.psum_tensor("psCtx", [128, 128], F32))

                dmaS(masks[:].rearrange("p (j t) -> p j t", t=1024),
                     maskin.transpose([1, 0, 2]))

                def load_head(h, b):
                    dmaS(kh[b][:], kscr[h * 128:(h + 1) * 128, :], par=b)
                    dmaS(vh[b][:].rearrange("p (m d) -> p m d", d=128),
                         vscr.rearrange("(m p) (hh d) -> p m hh d", p=128, d=128)[:, :, h, :], par=b)
                    dmaS(qrow[b][:], qscr[h * 128:(h + 1) * 128, :], par=b)
                    return LS[b].n

                hd_ls = {0: load_head(0, 0)}
                te_head = {}
                se_ctx_prev = 0
                te_probsT_prev = 0
                for h in range(H):
                    b = h % 2
                    if h + 1 < H:
                        if h >= 1:
                            sy.wait_ge(TE.sem, te_head[h - 1])
                        hd_ls[h + 1] = load_head(h + 1, (h + 1) % 2)
                    te.wait_ge(LS[b].sem, hd_ls[h])
                    for j in range(8):
                        E = 1024 * (j + 1)
                        nch = E // 512
                        ev = {}
                        for c in range(nch):
                            pb = c % 2
                            if c >= 2:
                                ctr, n = ev[c - 2]
                                te.wait_ge(ctr.sem, n)
                            t = te_mark(te.matmul(psSc[pb][:], qrow[b][:, j * 128:(j + 1) * 128],
                                                  kh[b][:, c * 512:(c + 1) * 512],
                                                  start=True, stop=True))
                            if c < nch - 2:
                                sc_.wait_ge(TE.sem, t)
                                sc_.activation(out=scf[:, c * 512:(c + 1) * 512],
                                               in_=psSc[pb][:], func=AF.Copy,
                                               scale=SCALE).then_inc(SE.sem, 1)
                                SE.n += 1
                                ev[c] = (SE, SE.n)
                            else:
                                mi = c - (nch - 2)
                                ve.wait_ge(TE.sem, t)
                                vstart()
                                ve.scalar_tensor_tensor(
                                    out=scf[:, c * 512:(c + 1) * 512], in0=psSc[pb][:],
                                    scalar=SCALE,
                                    in1=masks[:, j * 1024 + mi * 512: j * 1024 + (mi + 1) * 512],
                                    op0=OP.mult, op1=OP.add).then_inc(VE.sem, 1)
                                VE.n += 1
                                ev[c] = (VE, VE.n)
                        ve.wait_ge(SE.sem, SE.n)
                        vstart()
                        ve.reduce_max(out=nmax[:], in_=scf[:, 0:E], axis=AX,
                                      negate=True).then_inc(VE.sem, 1)
                        VE.n += 1
                        sc_.wait_ge(VE.sem, VE.n)
                        sc_.wait_ge(TE.sem, te_probsT_prev)
                        sc_.activation(out=prb[:, 0:E], in_=scf[:, 0:E], func=AF.Exp,
                                       bias=nmax[:, 0:1], scale=1.0,
                                       accum_out=ssum[:, 0:1]).then_inc(SE.sem, 1)
                        SE.n += 1
                        ve.wait_ge(SE.sem, SE.n)
                        vstart()
                        ve.reciprocal(out=rsum[:], in_=ssum[:]).then_inc(VE.sem, 1)
                        VE.n += 1
                        ve.wait_ge(VE.sem, VE.n)
                        ve.tensor_scalar(out=prb[:, 0:E], in0=prb[:, 0:E],
                                         scalar1=rsum[:, 0:1], scalar2=None,
                                         op0=OP.mult).then_inc(VE.sem, 1)
                        VE.n += 1
                        te.wait_ge(VE.sem, VE.n)
                        te.wait_ge(SE.sem, se_ctx_prev)
                        nkt = E // 128
                        ve_pt = {}
                        for kt in range(nkt):
                            pb = kt % 2
                            if kt >= 2:
                                te.wait_ge(VE.sem, ve_pt[kt - 2])
                            t = te_mark(te.transpose(psPT[pb][:], prb[:, kt * 128:(kt + 1) * 128],
                                                     sb_id[:]))
                            ve.wait_ge(TE.sem, t)
                            vstart()
                            ve.tensor_copy(out=ptb[pb][:], in_=psPT[pb][:]).then_inc(VE.sem, 1)
                            VE.n += 1
                            ve_pt[kt] = VE.n
                            te.wait_ge(VE.sem, VE.n)
                            mm = te.matmul(psCtx[:], ptb[pb][:], vh[b][:, kt * 128:(kt + 1) * 128],
                                           start=(kt == 0), stop=(kt == nkt - 1))
                        t = te_mark(mm)
                        te_probsT_prev = t
                        sc_.wait_ge(TE.sem, t)
                        sc_.wait_ge(LC[0].sem, LC[0].n)
                        sc_.activation(out=ctxo[:], in_=psCtx[:],
                                       func=AF.Copy).then_inc(SE.sem, 1)
                        SE.n += 1
                        se_ctx_prev = SE.n
                        sc_.wait_ge(SE.sem, SE.n)
                        dmaC(ctxscr[j * 128:(j + 1) * 128, h * 128:(h + 1) * 128], ctxo[:])
                    te_head[h] = TE.n
            barrier()
            if _PHASES <= 4:
                return

            with ExitStack() as ph:
                wo = ph.enter_context(nc.sbuf_tensor("wo", [128, 32 * 512], BF16))
                ctt = ph.enter_context(nc.sbuf_tensor("ctt", [128, D], BF16))
                cfm = ph.enter_context(nc.sbuf_tensor("cfm", [128, 32 * 128], BF16))
                xos = [ph.enter_context(nc.sbuf_tensor(f"xos{b}", [128, 512], F32)) for b in range(2)]
                r1o = [ph.enter_context(nc.sbuf_tensor(f"r1o{b}", [128, 512], F32)) for b in range(2)]
                psO = [ph.enter_context(nc.psum_tensor(f"psO{b}", [128, 512], F32)) for b in range(2)]
                psCT = [ph.enter_context(nc.psum_tensor(f"psCT{b}", [128, 2048], BF16)) for b in range(2)]
                io = 0
                ve_o, lc_o = {}, {}
                for os_ in range(8):
                    gp.wait_ge(TE.sem, TE.n)
                    for j in range(32):
                        dmaG(wo[:, j * 512:(j + 1) * 512],
                             out_w[j * 128:(j + 1) * 128, os_ * 512:(os_ + 1) * 512], par=0)
                    for tt in range(8):
                        sy.wait_ge(TE.sem, TE.n)
                        dmaS(ctt[:], ctxscr[tt * 128:(tt + 1) * 128, :], par=tt % 2)
                        if io >= 2:
                            gp.wait_ge(VE.sem, ve_o[io - 2])
                        dmaG(xos[io % 2][:],
                             x_own[tt * 128:(tt + 1) * 128, os_ * 512:(os_ + 1) * 512],
                             par=io % 2)
                        te.wait_ge(LS[tt % 2].sem, LS[tt % 2].n)
                        te.wait_ge(LG[0].sem, LG[0].n)
                        emit_tgroups(
                            32, 2, psCT,
                            lambda k: ctt[:, k * 128:(k + 1) * 128],
                            lambda jg, gi: cfm[:, jg * 256: jg * 256 + gi * 128].rearrange("p (a q) -> p a q", q=128))
                        te.wait_ge(VE.sem, VE.n)
                        pb = io % 2
                        if io >= 2:
                            te.wait_ge(VE.sem, ve_o[io - 2])
                        for ds in range(32):
                            te.matmul(psO[pb][:], cfm[:, ds * 128:(ds + 1) * 128],
                                      wo[:, ds * 512:(ds + 1) * 512],
                                      start=(ds == 0), stop=False)
                        t = te_mark(te.matmul(psO[pb][:], sb_ones[:, 0:128],
                                              sb_obrow[:, os_ * 512:(os_ + 1) * 512],
                                              start=False, stop=True))
                        ve.wait_ge(TE.sem, t)
                        ve.wait_ge(LG[pb].sem, LG[pb].n)
                        vstart()
                        if io >= 2:
                            ve.wait_ge(LC[pb].sem, lc_o[io - 2])
                        ve.tensor_tensor(out=r1o[pb][:], in0=psO[pb][:], in1=xos[pb][:],
                                         op=OP.add).then_inc(VE.sem, 1)
                        VE.n += 1
                        ve_o[io] = VE.n
                        sc_.wait_ge(VE.sem, VE.n)
                        dmaC(r1scr[tt * 128:(tt + 1) * 128, os_ * 512:(os_ + 1) * 512],
                             r1o[pb][:], par=pb)
                        lc_o[io] = LC[pb].n
                        io += 1
            barrier()
            if _PHASES <= 5:
                return

            x2fm = es.enter_context(nc.sbuf_tensor("x2fm", [128, 32 * NOWN], BF16))

            def ln_rows(src, dst, ph):
                rrow, xc, sqr, x2r = ph["rrow"], ph["xc"], ph["sqr"], ph["x2r"]
                st1, st2 = ph["st1"], ph["st2"]
                sy.wait_ge(VE.sem, VE.n)
                dmaS(rrow[:], src)
                ve.wait_ge(LS[0].sem, LS[0].n)
                vstart()
                ve.reduce_sum(out=st1[:], in_=rrow[:], axis=AX).then_inc(VE.sem, 1)
                VE.n += 1
                ve.wait_ge(VE.sem, VE.n)
                ve.tensor_scalar(out=st1[:], in0=st1[:], scalar1=1.0 / D, scalar2=None,
                                 op0=OP.mult).then_inc(VE.sem, 1)
                VE.n += 1
                ve.wait_ge(VE.sem, VE.n)
                ve.tensor_scalar(out=xc[:], in0=rrow[:], scalar1=st1[:, 0:1], scalar2=None,
                                 op0=OP.subtract).then_inc(VE.sem, 1)
                VE.n += 1
                sc_.wait_ge(VE.sem, VE.n)
                sc_.activation(out=sqr[:], in_=xc[:], func=AF.Square).then_inc(SE.sem, 1)
                SE.n += 1
                ve.wait_ge(SE.sem, SE.n)
                vstart()
                ve.reduce_sum(out=st2[:], in_=sqr[:], axis=AX).then_inc(VE.sem, 1)
                VE.n += 1
                ve.wait_ge(VE.sem, VE.n)
                ve.tensor_scalar(out=st2[:], in0=st2[:], scalar1=1.0 / D, scalar2=float(EPS),
                                 op0=OP.mult, op1=OP.add).then_inc(VE.sem, 1)
                VE.n += 1
                sc_.wait_ge(VE.sem, VE.n)
                sc_.activation(out=st2[:], in_=st2[:], func=AF.Sqrt).then_inc(SE.sem, 1)
                SE.n += 1
                ve.wait_ge(SE.sem, SE.n)
                vstart()
                ve.reciprocal(out=st2[:], in_=st2[:]).then_inc(VE.sem, 1)
                VE.n += 1
                ve.wait_ge(VE.sem, VE.n)
                ve.wait_ge(LC[0].sem, LC[0].n)
                ve.tensor_scalar(out=x2r[:], in0=xc[:], scalar1=st2[:, 0:1], scalar2=None,
                                 op0=OP.mult).then_inc(VE.sem, 1)
                VE.n += 1
                sc_.wait_ge(VE.sem, VE.n)
                if dst is not None:
                    dmaC(dst, x2r[:])

            with ExitStack() as ph0:
                ph = {k: ph0.enter_context(nc.sbuf_tensor(k, [128, D], F32))
                      for k in ("rrow", "xc", "sqr", "x2r")}
                ph["st1"] = ph0.enter_context(nc.sbuf_tensor("st1", [128, 1], F32))
                ph["st2"] = ph0.enter_context(nc.sbuf_tensor("st2", [128, 1], F32))
                x2b = ph0.enter_context(nc.sbuf_tensor("x2b", [128, D], BF16))
                psX2 = [ph0.enter_context(nc.psum_tensor(f"psX2{b}", [128, 4096], BF16)) for b in range(2)]
                for tt in range(8):
                    ln_rows(r1scr[tt * 128:(tt + 1) * 128, :],
                            x2scr[tt * 128:(tt + 1) * 128, :], ph)
                    sc_.wait_ge(TE.sem, TE.n)
                    sc_.activation(out=x2b[:], in_=ph["x2r"][:], func=AF.Copy).then_inc(SE.sem, 1)
                    SE.n += 1
                    te.wait_ge(SE.sem, SE.n)
                    x2v = x2fm[:].rearrange("p (j t) -> p j t", t=NOWN)
                    emit_tgroups(
                        32, 4, psX2,
                        lambda k: x2b[:, k * 128:(k + 1) * 128],
                        lambda jg, gi, tt=tt: x2v[:, jg * 4: jg * 4 + gi, tt * 128:(tt + 1) * 128])
            barrier()
            if _PHASES <= 6:
                return

            with ExitStack() as ph:
                pk1 = [ph.enter_context(nc.sbuf_tensor(f"pk1{b}", [128, 2048], I32)) for b in range(2)]
                pk3 = [ph.enter_context(nc.sbuf_tensor(f"pk3{b}", [128, 2048], I32)) for b in range(2)]
                sc1 = [ph.enter_context(nc.sbuf_tensor(f"sc1{b}", [128, 64], F32)) for b in range(2)]
                sc3 = [ph.enter_context(nc.sbuf_tensor(f"sc3{b}", [128, 64], F32)) for b in range(2)]
                msb = ph.enter_context(nc.sbuf_tensor("msb", [128, 2048], I32))
                lsb = ph.enter_context(nc.sbuf_tensor("lsb", [128, 2048], I32))
                nat = [ph.enter_context(nc.sbuf_tensor(f"nat{b}", [128, D], BF16)) for b in range(2)]
                w1t = ph.enter_context(nc.sbuf_tensor("w1t", [128, D], BF16))
                w3t = ph.enter_context(nc.sbuf_tensor("w3t", [128, D], BF16))
                sil = [ph.enter_context(nc.sbuf_tensor(f"sil{b}", [128, 512], F32)) for b in range(2)]
                hao = [ph.enter_context(nc.sbuf_tensor(f"hao{b}", [128, 512], BF16)) for b in range(2)]
                psA = [ph.enter_context(nc.psum_tensor(f"psA{b}", [128, 512], F32)) for b in range(2)]
                psB = [ph.enter_context(nc.psum_tensor(f"psB{b}", [128, 512], F32)) for b in range(2)]
                psW = [ph.enter_context(nc.psum_tensor(f"psW{b}", [128, 2048], BF16)) for b in range(2)]

                w1r = w1p.rearrange("(h r) b -> h (r b)", r=32)
                w3r = w3p.rearrange("(h r) b -> h (r b)", r=32)
                s1r = s1v.rearrange("(h g) -> h g", g=64)
                s3r = s3v.rearrange("(h g) -> h g", g=64)

                def deq_tile(pk, scv, outT, slot, par):
                    ve.wait_ge(LS[par].sem, LS[par].n)
                    vstart()
                    ve.tensor_scalar(out=msb[:], in0=pk[:], scalar1=4, scalar2=None,
                                     op0=OP.arith_shift_right)
                    ve.tensor_scalar(out=lsb[:], in0=pk[:], scalar1=28, scalar2=28,
                                     op0=OP.arith_shift_left,
                                     op1=OP.arith_shift_right).then_inc(VE.sem, 1)
                    VE.n += 1
                    ve.wait_ge(VE.sem, VE.n)
                    nb = nat[slot]
                    natv = nb[:].rearrange("p (m z c) -> p z m c", z=2, c=64)
                    mv = msb[:].rearrange("p (m c) -> p m c", c=64)
                    lv = lsb[:].rearrange("p (m c) -> p m c", c=64)
                    sev = scv[:].rearrange("p (m z) -> p z m", z=2)[:, 0].unsqueeze(2).broadcast_to([128, 32, 64])
                    sov = scv[:].rearrange("p (m z) -> p z m", z=2)[:, 1].unsqueeze(2).broadcast_to([128, 32, 64])
                    ve.tensor_tensor(out=natv[:, 0], in0=mv, in1=sev, op=OP.mult)
                    ve.tensor_tensor(out=natv[:, 1], in0=lv, in1=sov, op=OP.mult).then_inc(VE.sem, 1)
                    VE.n += 1
                    te.wait_ge(VE.sem, VE.n)
                    emit_tgroups(
                        32, 2, psW,
                        lambda k, nb=nb: nb[:, k * 128:(k + 1) * 128],
                        lambda jg, gi, outT=outT: outT[:, jg * 256: jg * 256 + gi * 128].rearrange("p (a q) -> p a q", q=128))

                te_ht, se_ab, ve_ab, lc_ab = {}, {}, {}, {}
                for ht in range(NHT):
                    b = ht % 2
                    if ht >= 2:
                        sy.wait_ge(VE.sem, ve_ab.get(2 * (ht - 2) + 1, 0))
                    dmaS(pk1[b][:], w1r[ht * 128:(ht + 1) * 128, :], par=b)
                    dmaS(sc1[b][:], s1r[ht * 128:(ht + 1) * 128, :], par=b)
                    dmaS(pk3[b][:], w3r[ht * 128:(ht + 1) * 128, :], par=b)
                    dmaS(sc3[b][:], s3r[ht * 128:(ht + 1) * 128, :], par=b)
                    if ht >= 1:
                        ve.wait_ge(TE.sem, te_ht[ht - 1])
                    deq_tile(pk1[b], sc1[b], w1t, 0, b)
                    deq_tile(pk3[b], sc3[b], w3t, 1, b)
                    te.wait_ge(VE.sem, VE.n)
                    for c in range(2):
                        it = ht * 2 + c
                        pb = c
                        if it >= 2:
                            te.wait_ge(SE.sem, se_ab[it - 2])
                            te.wait_ge(VE.sem, ve_ab[it - 2])
                        for ds in range(32):
                            te.matmul(psA[pb][:], w1t[:, ds * 128:(ds + 1) * 128],
                                      x2fm[:, ds * NOWN + c * 512: ds * NOWN + (c + 1) * 512],
                                      start=(ds == 0), stop=(ds == 31))
                        for ds in range(32):
                            mm = te.matmul(psB[pb][:], w3t[:, ds * 128:(ds + 1) * 128],
                                           x2fm[:, ds * NOWN + c * 512: ds * NOWN + (c + 1) * 512],
                                           start=(ds == 0), stop=(ds == 31))
                        t = te_mark(mm)
                        sc_.wait_ge(TE.sem, t)
                        if it >= 2:
                            sc_.wait_ge(VE.sem, ve_ab[it - 2])
                        sc_.activation(out=sil[pb][:], in_=psA[pb][:],
                                       func=AF.Silu).then_inc(SE.sem, 1)
                        SE.n += 1
                        se_ab[it] = SE.n
                        ve.wait_ge(SE.sem, SE.n)
                        vstart()
                        if it >= 2:
                            ve.wait_ge(LC[pb].sem, lc_ab[it - 2])
                        ve.tensor_tensor(out=hao[pb][:], in0=sil[pb][:], in1=psB[pb][:],
                                         op=OP.mult).then_inc(VE.sem, 1)
                        VE.n += 1
                        ve_ab[it] = VE.n
                        sc_.wait_ge(VE.sem, VE.n)
                        dmaC(hactscr[ht * 128:(ht + 1) * 128, c * 512:(c + 1) * 512],
                             hao[pb][:], par=pb)
                        lc_ab[it] = LC[pb].n
                    te_ht[ht] = TE.n
            barrier()
            if _PHASES <= 7:
                return

            with ExitStack() as ph:
                natS = ph.enter_context(nc.sbuf_tensor("natS", [128, 32 * 1024], BF16))
                pk2 = [ph.enter_context(nc.sbuf_tensor(f"pk2{b}", [128, 512], I32)) for b in range(2)]
                sc2 = [ph.enter_context(nc.sbuf_tensor(f"sc2{b}", [128, 172], F32)) for b in range(2)]
                msb2 = ph.enter_context(nc.sbuf_tensor("msb2", [128, 512], I32))
                lsb2 = ph.enter_context(nc.sbuf_tensor("lsb2", [128, 512], I32))
                w2st = ph.enter_context(nc.sbuf_tensor("w2st", [128, D], BF16))
                psW2 = [ph.enter_context(nc.psum_tensor(f"psW2{b}", [128, 4096], BF16)) for b in range(2)]
                w2r = w2p.rearrange("(d r) b -> d (r b)", r=86)
                s2r = s2v.rearrange("(d g) -> d g", g=172)
                ve_w2 = {}
                te_sup = {}
                for sup in range(11):
                    HS = 1024 if sup < 10 else 768
                    NM = HS // 128
                    for dt in range(32):
                        it = sup * 32 + dt
                        b = it % 2
                        if it >= 2:
                            sy.wait_ge(VE.sem, ve_w2[it - 2])
                        dmaS(pk2[b][:, 0:NM * 64],
                             w2r[dt * 128:(dt + 1) * 128, sup * 512: sup * 512 + NM * 64], par=b)
                        dmaS(sc2[b][:], s2r[dt * 128:(dt + 1) * 128, :], par=b)
                        ve.wait_ge(LS[b].sem, LS[b].n)
                        if dt == 0 and sup >= 1:
                            ve.wait_ge(TE.sem, te_sup[sup - 1])
                        vstart()
                        ve.tensor_scalar(out=msb2[:, 0:NM * 64], in0=pk2[b][:, 0:NM * 64],
                                         scalar1=4, scalar2=None,
                                         op0=OP.arith_shift_right)
                        ve.tensor_scalar(out=lsb2[:, 0:NM * 64], in0=pk2[b][:, 0:NM * 64],
                                         scalar1=28, scalar2=28,
                                         op0=OP.arith_shift_left,
                                         op1=OP.arith_shift_right).then_inc(VE.sem, 1)
                        VE.n += 1
                        ve.wait_ge(VE.sem, VE.n)
                        natv = natS[:, dt * 1024: dt * 1024 + HS].rearrange(
                            "p (m z c) -> p z m c", z=2, c=64)
                        mv = msb2[:, 0:NM * 64].rearrange("p (m c) -> p m c", c=64)
                        lv = lsb2[:, 0:NM * 64].rearrange("p (m c) -> p m c", c=64)
                        sall = sc2[b][:, 16 * sup: 16 * sup + 2 * NM]
                        sev = sall.rearrange("p (m z) -> p z m", z=2)[:, 0].unsqueeze(2).broadcast_to([128, NM, 64])
                        sov = sall.rearrange("p (m z) -> p z m", z=2)[:, 1].unsqueeze(2).broadcast_to([128, NM, 64])
                        ve.tensor_tensor(out=natv[:, 0], in0=mv, in1=sev, op=OP.mult)
                        ve.tensor_tensor(out=natv[:, 1], in0=lv, in1=sov, op=OP.mult).then_inc(VE.sem, 1)
                        VE.n += 1
                        ve_w2[it] = VE.n
                    te.wait_ge(VE.sem, VE.n)
                    for hb in range(NM):
                        ve.wait_ge(LC[0].sem, LC[0].n)
                        emit_tgroups(
                            32, 4, psW2,
                            lambda k, hb=hb: natS[:, k * 1024 + hb * 128: k * 1024 + (hb + 1) * 128],
                            lambda jg, gi: w2st[:, jg * 512: jg * 512 + gi * 128].rearrange("p (a q) -> p a q", q=128))
                        sc_.wait_ge(VE.sem, VE.n)
                        dmaC(w2tscr[sup * 1024 + hb * 128: sup * 1024 + (hb + 1) * 128, :],
                             w2st[:])
                    te_sup[sup] = TE.n
            barrier()
            if _PHASES <= 8:
                return

            with ExitStack() as ph:
                hab = [ph.enter_context(nc.sbuf_tensor(f"hab{b}", [128, NOWN], BF16)) for b in range(2)]
                w2b = [ph.enter_context(nc.sbuf_tensor(f"w2b{b}", [128, 512], BF16)) for b in range(2)]
                x2s = ph.enter_context(nc.sbuf_tensor("x2s", [128, 512], F32))
                r2o = ph.enter_context(nc.sbuf_tensor("r2o", [128, 512], F32))
                psF = [ph.enter_context(nc.psum_tensor(f"psF{t}", [128, 512], F32)) for t in range(8)]
                te_f = {}
                ve_ds = 0
                for ds in range(8):
                    for hs in range(NHT):
                        it = ds * NHT + hs
                        b = it % 2
                        if it >= 2:
                            sy.wait_ge(TE.sem, te_f[it - 2])
                        dmaS(hab[b][:], hactscr[hs * 128:(hs + 1) * 128, :], par=b)
                        dmaS(w2b[b][:], w2tscr[hs * 128:(hs + 1) * 128, ds * 512:(ds + 1) * 512], par=b)
                        te.wait_ge(LS[b].sem, LS[b].n)
                        if hs == 0:
                            te.wait_ge(VE.sem, ve_ds)
                        for tt in range(8):
                            mm = te.matmul(psF[tt][:], hab[b][:, tt * 128:(tt + 1) * 128],
                                           w2b[b][:], start=(hs == 0), stop=(hs == NHT - 1))
                        te_f[it] = te_mark(mm)
                    for tt in range(8):
                        gp.wait_ge(VE.sem, VE.n)
                        dmaG(x2s[:], x2scr[tt * 128:(tt + 1) * 128, ds * 512:(ds + 1) * 512])
                        ve.wait_ge(TE.sem, TE.n)
                        ve.wait_ge(LG[0].sem, LG[0].n)
                        vstart()
                        ve.wait_ge(LC[0].sem, LC[0].n)
                        ve.tensor_tensor(out=r2o[:], in0=psF[tt][:], in1=x2s[:],
                                         op=OP.add).then_inc(VE.sem, 1)
                        VE.n += 1
                        sc_.wait_ge(VE.sem, VE.n)
                        dmaC(r2scr[tt * 128:(tt + 1) * 128, ds * 512:(ds + 1) * 512],
                             r2o[:])
                    ve_ds = VE.n
            barrier()
            if _PHASES <= 9:
                return

            with ExitStack() as ph0:
                ph = {k: ph0.enter_context(nc.sbuf_tensor(k + "b", [128, D], F32))
                      for k in ("rrow", "xc", "sqr", "x2r")}
                ph["st1"] = ph0.enter_context(nc.sbuf_tensor("st1b", [128, 1], F32))
                ph["st2"] = ph0.enter_context(nc.sbuf_tensor("st2b", [128, 1], F32))
                for tt in range(8):
                    ln_rows(r2scr[tt * 128:(tt + 1) * 128, :],
                            out_own[tt * 128:(tt + 1) * 128, :], ph)
                for c in LC:
                    sy.wait_ge(c.sem, c.n)
                for c in LS:
                    sy.wait_ge(c.sem, c.n)
                for c in LG:
                    sy.wait_ge(c.sem, c.n)

    return nc


def _host_inputs(inputs, core):
    x = np.asarray(inputs["L0_attn_hidden_states"]).reshape(S, D)
    qkv_b = np.asarray(inputs["L0_attn_qkv_b"])
    out_b = np.asarray(inputs["L0_attn_out_b"])

    own = np.concatenate([np.arange(128 * (8 * j + core), 128 * (8 * j + core) + 128)
                          for j in range(8)])
    mask = np.zeros((8, 128, 1024), np.float32)
    for j in range(8):
        q = 128 * (8 * j + core) + np.arange(128)[:, None]
        k = 1024 * j + np.arange(1024)[None, :]
        mask[j] = np.where(k <= q, 0.0, np.float32(NEG))
    inv = 1.0 / (10000.0 ** (np.arange(64, dtype=np.float64) / 64.0))
    pos = np.arange(S, dtype=np.float64)
    ang = pos[None, :] * inv[:, None]
    cosk_ = np.cos(ang).astype(np.float32)
    sink_ = np.sin(ang).astype(np.float32)
    cosk_ = np.concatenate([cosk_, cosk_], axis=0)
    sink_ = np.concatenate([sink_, sink_], axis=0)

    perm = np.zeros((128, 128), np.float32)
    for p in range(64):
        perm[p, p + 64] = -1.0
        perm[p + 64, p] = 1.0

    return {
        "x_full": x,
        "x_own": np.ascontiguousarray(x[own]),
        "qkv_w": np.asarray(inputs["L0_attn_qkv_w"]),
        "qkv_bT": np.ascontiguousarray(qkv_b.reshape(96, 128).T),
        "vb_row": np.ascontiguousarray(qkv_b[8192:].reshape(1, D)),
        "out_w": np.asarray(inputs["L0_attn_out_w"]),
        "ob_row": np.ascontiguousarray(out_b.reshape(1, D)),
        "w1p": np.asarray(inputs["L0_ffn_w1"]),
        "s1v": np.asarray(inputs["L0_ffn_s1"]),
        "w2p": np.asarray(inputs["L0_ffn_w2"]),
        "s2v": np.asarray(inputs["L0_ffn_s2"]),
        "w3p": np.asarray(inputs["L0_ffn_w3"]),
        "s3v": np.asarray(inputs["L0_ffn_s3"]),
        "maskin": mask,
        "cosk": np.ascontiguousarray(cosk_),
        "sink": np.ascontiguousarray(sink_),
        "cosq": np.ascontiguousarray(cosk_[:, own]),
        "sinq": np.ascontiguousarray(sink_[:, own]),
        "identb": np.eye(128).astype(ml_dtypes.bfloat16),
        "permT": np.ascontiguousarray(perm.T).astype(ml_dtypes.bfloat16),
    }, own


def kernel(L0_attn_hidden_states, L0_attn_qkv_w, L0_attn_qkv_b, L0_attn_out_w,
           L0_attn_out_b, L0_ffn_w1, L0_ffn_s1, L0_ffn_w2, L0_ffn_s2,
           L0_ffn_w3, L0_ffn_s3):
    inputs = dict(
        L0_attn_hidden_states=L0_attn_hidden_states,
        L0_attn_qkv_w=L0_attn_qkv_w, L0_attn_qkv_b=L0_attn_qkv_b,
        L0_attn_out_w=L0_attn_out_w, L0_attn_out_b=L0_attn_out_b,
        L0_ffn_w1=L0_ffn_w1, L0_ffn_s1=L0_ffn_s1,
        L0_ffn_w2=L0_ffn_w2, L0_ffn_s2=L0_ffn_s2,
        L0_ffn_w3=L0_ffn_w3, L0_ffn_s3=L0_ffn_s3,
    )
    nc = _build()
    in_maps, owns = [], []
    for c in range(8):
        m, own = _host_inputs(inputs, c)
        in_maps.append(m)
        owns.append(own)
    res = run_bass_kernel_spmd(nc, in_maps, core_ids=list(range(8)))
    out = np.empty((1, S, D), np.float32)
    for c in range(8):
        out[0, owns[c]] = res.results[c]["out_own"]
    return out


# version 02
# speedup vs baseline: 1.2355x; 1.2355x over previous
import numpy as np
import ml_dtypes
import concourse.bass as bass
import concourse.mybir as mybir
from concourse.bass_utils import run_bass_kernel_spmd
from concourse.alu_op_type import AluOpType
from contextlib import ExitStack

F32 = mybir.dt.float32
BF16 = mybir.dt.bfloat16
I32 = mybir.dt.int32
AF = mybir.ActivationFunctionType
OP = AluOpType
AX = mybir.AxisListType.X

S = 8192
D = 4096
H = 32
NOWN = 1024
HID = 11008
NHT = HID // 128
SCALE = float(1.0 / np.sqrt(np.float32(128.0)))
EPS = 1e-5
NEG = -66504.0

_DBG_KIND = {}
_PHASES = 99


class Ctr:
    def __init__(self, sem):
        self.sem = sem
        self.n = 0


def _build():
    nc = bass.Bass("TRN2", target_bir_lowering=False)
    ap = lambda t: t.ap()

    x_full = ap(nc.dram_tensor("x_full", [S, D], F32, kind="ExternalInput"))
    x_own = ap(nc.dram_tensor("x_own", [NOWN, D], F32, kind="ExternalInput"))
    qkv_w = ap(nc.dram_tensor("qkv_w", [D, 3 * D], F32, kind="ExternalInput"))
    qkv_bT = ap(nc.dram_tensor("qkv_bT", [128, 96], F32, kind="ExternalInput"))
    vb_row = ap(nc.dram_tensor("vb_row", [1, D], F32, kind="ExternalInput"))
    out_w = ap(nc.dram_tensor("out_w", [D, D], F32, kind="ExternalInput"))
    ob_row = ap(nc.dram_tensor("ob_row", [1, D], F32, kind="ExternalInput"))
    w1p = ap(nc.dram_tensor("w1p", [352256, 64], I32, kind="ExternalInput"))
    s1v = ap(nc.dram_tensor("s1v", [704512], F32, kind="ExternalInput"))
    w2p = ap(nc.dram_tensor("w2p", [352256, 64], I32, kind="ExternalInput"))
    s2v = ap(nc.dram_tensor("s2v", [704512], F32, kind="ExternalInput"))
    w3p = ap(nc.dram_tensor("w3p", [352256, 64], I32, kind="ExternalInput"))
    s3v = ap(nc.dram_tensor("s3v", [704512], F32, kind="ExternalInput"))
    maskin = ap(nc.dram_tensor("maskin", [8, 128, 1024], BF16, kind="ExternalInput"))
    cosk = ap(nc.dram_tensor("cosk", [128, S], F32, kind="ExternalInput"))
    sink = ap(nc.dram_tensor("sink", [128, S], F32, kind="ExternalInput"))
    cosq = ap(nc.dram_tensor("cosq", [128, NOWN], F32, kind="ExternalInput"))
    sinq = ap(nc.dram_tensor("sinq", [128, NOWN], F32, kind="ExternalInput"))
    identb = ap(nc.dram_tensor("identb", [128, 128], BF16, kind="ExternalInput"))
    permT = ap(nc.dram_tensor("permT", [128, 128], BF16, kind="ExternalInput"))

    out_own = ap(nc.dram_tensor("out_own", [NOWN, D], F32, kind="ExternalOutput"))

    def scr(name, shape, dt):
        return ap(nc.dram_tensor(name, shape, dt, kind=_DBG_KIND.get(name, "Internal")))

    xfm = scr("xfm", [D, S], BF16)
    kscr = scr("kscr", [D, S], BF16)
    vscr = scr("vscr", [S, D], BF16)
    qscr = scr("qscr", [D, NOWN], BF16)
    ctxscr = scr("ctxscr", [NOWN, D], BF16)
    x2scr = scr("x2scr", [NOWN, D], F32)
    r1scr = scr("r1scr", [NOWN, D], F32)
    r2scr = scr("r2scr", [NOWN, D], F32)
    hactscr = scr("hactscr", [HID, NOWN], BF16)
    w2tscr = scr("w2tscr", [HID, D], BF16)

    es = ExitStack()
    with es:
        LS = [Ctr(es.enter_context(nc.semaphore(f"LS{i}"))) for i in range(2)]
        LC = [Ctr(es.enter_context(nc.semaphore(f"LC{i}"))) for i in range(2)]
        LG = [Ctr(es.enter_context(nc.semaphore(f"LG{i}"))) for i in range(2)]
        TE = Ctr(es.enter_context(nc.semaphore("TEs")))
        VE = Ctr(es.enter_context(nc.semaphore("VEs")))
        SE = Ctr(es.enter_context(nc.semaphore("SEs")))
        BAR = Ctr(es.enter_context(nc.semaphore("BARs")))

        sb_qkvbT = es.enter_context(nc.sbuf_tensor("sb_qkvbT", [128, 96], F32))
        sb_vbrow = es.enter_context(nc.sbuf_tensor("sb_vbrow", [1, D], BF16))
        sb_obrow = es.enter_context(nc.sbuf_tensor("sb_obrow", [1, D], BF16))
        sb_id = es.enter_context(nc.sbuf_tensor("sb_id", [128, 128], BF16))
        sb_perm = es.enter_context(nc.sbuf_tensor("sb_perm", [128, 128], BF16))
        sb_ones = es.enter_context(nc.sbuf_tensor("sb_ones", [1, 512], BF16))

        blk = es.enter_context(nc.Block())

        @blk.gpsimd
        def _(_g):
            sy, sc_, gp, ve, te = nc.sync, nc.scalar, nc.gpsimd, nc.vector, nc.tensor

            def dmaS(out, in_, par=0):
                sy.dma_start(out=out, in_=in_).then_inc(LS[par].sem, 16)
                LS[par].n += 16
                return (LS[par], LS[par].n)

            def dmaC(out, in_, par=0):
                sc_.dma_start(out=out, in_=in_).then_inc(LC[par].sem, 16)
                LC[par].n += 16
                return (LC[par], LC[par].n)

            def dmaG(out, in_, par=0):
                gp.dma_start(out=out, in_=in_).then_inc(LG[par].sem, 16)
                LG[par].n += 16
                return (LG[par], LG[par].n)

            def wq_(eng, tok):
                eng.wait_ge(tok[0].sem, tok[1])

            def te_mark(instr):
                instr.then_inc(TE.sem, 1)
                TE.n += 1
                return TE.n

            def vstart():
                ve.wait_ge(VE.sem, VE.n)

            tg_state = {}

            def emit_tgroups(nblk, g, psl, src_fn, dst_fn):
                st = tg_state.setdefault(psl[0].name, {0: 0, 1: 0})
                ngr = (nblk + g - 1) // g
                for jg in range(ngr):
                    gi = min(g, nblk - jg * g)
                    bank = jg % 2
                    if st[bank]:
                        te.wait_ge(VE.sem, st[bank])
                    ti = None
                    for jj in range(gi):
                        ti = te.transpose(psl[bank][:, jj * 1024: jj * 1024 + 128],
                                          src_fn(jg * g + jj), sb_id[:])
                    t = te_mark(ti)
                    ve.wait_ge(TE.sem, t)
                    vstart()
                    ve.tensor_copy(
                        out=dst_fn(jg, gi),
                        in_=psl[bank][:].rearrange("p (a q) -> p a q", q=1024)[:, 0:gi, 0:128],
                    ).then_inc(VE.sem, 1)
                    VE.n += 1
                    st[bank] = VE.n

            def barrier():
                for c in LS:
                    sy.wait_ge(c.sem, c.n)
                for c in LC:
                    sc_.wait_ge(c.sem, c.n)
                for c in LG:
                    gp.wait_ge(c.sem, c.n)
                te.wait_ge(TE.sem, TE.n)
                ve.wait_ge(VE.sem, VE.n)
                sc_.wait_ge(SE.sem, SE.n)
                for e in (te, ve, sc_, gp, sy):
                    e.sem_inc(BAR.sem, 1)
                BAR.n += 5
                for e in (te, ve, sc_, gp, sy):
                    e.wait_ge(BAR.sem, BAR.n)

            with ExitStack() as p0:
                sb_vbf = p0.enter_context(nc.sbuf_tensor("sb_vbf", [1, D], F32))
                sb_obf = p0.enter_context(nc.sbuf_tensor("sb_obf", [1, D], F32))
                dmaS(sb_qkvbT[:], qkv_bT[:])
                dmaS(sb_vbf[:], vb_row[:])
                dmaS(sb_obf[:], ob_row[:])
                dmaS(sb_id[:], identb[:])
                dmaS(sb_perm[:], permT[:])
                ve.wait_ge(LS[0].sem, LS[0].n)
                ve.memset(sb_ones[:], 1.0)
                ve.tensor_copy(out=sb_vbrow[:], in_=sb_vbf[:]).then_inc(VE.sem, 1)
                VE.n += 1
                ve.tensor_copy(out=sb_obrow[:], in_=sb_obf[:]).then_inc(VE.sem, 1)
                VE.n += 1
                barrier()
            if _PHASES <= 0:
                return

            with ExitStack() as ph:
                xtok = [ph.enter_context(nc.sbuf_tensor(f"xtok{b}", [128, 4 * D], BF16)) for b in range(2)]
                stg = [ph.enter_context(nc.sbuf_tensor(f"stg{b}", [128, 32 * 512], BF16)) for b in range(2)]
                psT = [ph.enter_context(nc.psum_tensor(f"psT{b}", [128, 4096], BF16)) for b in range(2)]
                te_chunk, ls_chunk, ve_chunk = {}, {}, {}
                for i in range(16):
                    b = i % 2
                    if i >= 2:
                        gp.wait_ge(TE.sem, te_chunk[i - 2])
                    for p in range(4):
                        dmaG(xtok[b][:, p * D:(p + 1) * D],
                             x_full[i * 512 + p * 128: i * 512 + (p + 1) * 128, :], par=b)
                    te.wait_ge(LG[b].sem, LG[b].n)
                    if i >= 2:
                        te.wait_ge(LS[b].sem, ls_chunk[i - 2])
                    emit_tgroups(
                        128, 4, psT,
                        lambda k, b=b: xtok[b][:, (k % 4) * D + (k // 4) * 128: (k % 4) * D + (k // 4) * 128 + 128],
                        lambda jg, gi, b=b: stg[b][:, jg * 512: jg * 512 + gi * 128].rearrange("p (a q) -> p a q", q=128))
                    te_chunk[i] = TE.n
                    sy.wait_ge(VE.sem, VE.n)
                    for j in range(32):
                        dmaS(xfm[j * 128:(j + 1) * 128, i * 512:(i + 1) * 512],
                             stg[b][:, j * 512:(j + 1) * 512], par=b)
                    ls_chunk[i] = LS[b].n
                    ve_chunk[i] = VE.n
            barrier()
            if _PHASES <= 1:
                return

            with ExitStack() as ph:
                wband = ph.enter_context(nc.sbuf_tensor("wband", [128, 32 * 1024], BF16))
                xfc = [ph.enter_context(nc.sbuf_tensor(f"xfc{b}", [128, 32 * 512], BF16)) for b in range(2)]
                ckv = [ph.enter_context(nc.sbuf_tensor(f"ckv{b}", [128, 512], F32)) for b in range(2)]
                skv = [ph.enter_context(nc.sbuf_tensor(f"skv{b}", [128, 512], F32)) for b in range(2)]
                kb = [ph.enter_context(nc.sbuf_tensor(f"kb{b}", [128, 512], BF16)) for b in range(2)]
                m1 = ph.enter_context(nc.sbuf_tensor("m1t", [128, 512], F32))
                m2 = ph.enter_context(nc.sbuf_tensor("m2t", [128, 512], F32))
                kro = [ph.enter_context(nc.sbuf_tensor(f"kro{b}", [128, 512], BF16)) for b in range(2)]
                psK = [ph.enter_context(nc.psum_tensor(f"psK{b}", [128, 512], F32)) for b in range(2)]
                psP = [ph.enter_context(nc.psum_tensor(f"psP{b}", [128, 512], F32)) for b in range(2)]

                ik = 0
                ev_psK = {}
                ve_rope = {}
                lc_st = {}
                se_kb = {}
                te_chunkb = {}
                ve_chunkb = {}
                for band in range(8):
                    is_k = band < 4
                    colbase = D + band * 1024 if is_k else 2 * D + (band - 4) * 1024
                    gp.wait_ge(TE.sem, TE.n)
                    for j in range(32):
                        dmaG(wband[:, j * 1024:(j + 1) * 1024],
                             qkv_w[j * 128:(j + 1) * 128, colbase:colbase + 1024], par=0)
                    for i in range(16):
                        b = i % 2
                        if (band, i) != (0, 0):
                            key = (band, i - 2) if i >= 2 else (band - 1, 14 + i)
                            if key in te_chunkb:
                                sy.wait_ge(TE.sem, te_chunkb[key])
                            if key in ve_chunkb:
                                sy.wait_ge(VE.sem, ve_chunkb[key])
                        dmaS(xfc[b][:].rearrange("p (j t) -> p j t", t=512),
                             xfm.rearrange("(j p) t -> p j t", p=128)[:, :, i * 512:(i + 1) * 512], par=b)
                        if is_k:
                            dmaS(ckv[b][:], cosk[:, i * 512:(i + 1) * 512], par=b)
                            dmaS(skv[b][:], sink[:, i * 512:(i + 1) * 512], par=b)
                        te.wait_ge(LG[0].sem, LG[0].n)
                        te.wait_ge(LS[b].sem, LS[b].n)
                        if is_k:
                            for oc in range(8):
                                head = band * 8 + oc
                                pb = ik % 2
                                if ik >= 2:
                                    ctr, n = ev_psK[ik - 2]
                                    te.wait_ge(ctr.sem, n)
                                for ds in range(32):
                                    mm = te.matmul(psK[pb][:],
                                                   wband[:, ds * 1024 + oc * 128: ds * 1024 + oc * 128 + 128],
                                                   xfc[b][:, ds * 512:(ds + 1) * 512],
                                                   start=(ds == 0), stop=(ds == 31))
                                t = te_mark(mm)
                                sc_.wait_ge(TE.sem, t)
                                if ik >= 2:
                                    sc_.wait_ge(VE.sem, ve_rope[ik - 2])
                                sc_.activation(out=kb[pb][:], in_=psK[pb][:], func=AF.Identity,
                                               bias=sb_qkvbT[:, 32 + head:33 + head],
                                               scale=1.0).then_inc(SE.sem, 1)
                                SE.n += 1
                                ev_psK[ik] = (SE, SE.n)
                                se_kb[ik] = SE.n
                                te.wait_ge(SE.sem, SE.n)
                                if ik >= 2:
                                    te.wait_ge(VE.sem, ve_rope[ik - 2])
                                t = te_mark(te.matmul(psP[pb][:], sb_perm[:], kb[pb][:],
                                                      start=True, stop=True))
                                ve.wait_ge(TE.sem, t)
                                vstart()
                                ve.tensor_tensor(out=m1[:], in0=kb[pb][:], in1=ckv[b][:],
                                                 op=OP.mult)
                                ve.tensor_tensor(out=m2[:], in0=psP[pb][:], in1=skv[b][:],
                                                 op=OP.mult).then_inc(VE.sem, 1)
                                VE.n += 1
                                ve.wait_ge(VE.sem, VE.n)
                                if ik >= 2:
                                    ve.wait_ge(LC[pb].sem, lc_st[ik - 2])
                                ve.tensor_tensor(out=kro[pb][:], in0=m1[:], in1=m2[:],
                                                 op=OP.add).then_inc(VE.sem, 1)
                                VE.n += 1
                                ve_rope[ik] = VE.n
                                sc_.wait_ge(VE.sem, VE.n)
                                dmaC(kscr[head * 128:(head + 1) * 128, i * 512:(i + 1) * 512],
                                     kro[pb][:], par=pb)
                                lc_st[ik] = LC[pb].n
                                ik += 1
                        else:
                            vs = (band - 4) * 2
                            for sl in range(2):
                                for p in range(4):
                                    pb = ik % 2
                                    if ik >= 2:
                                        ctr, n = ev_psK[ik - 2]
                                        te.wait_ge(ctr.sem, n)
                                    for ds in range(32):
                                        te.matmul(psK[pb][:],
                                                  xfc[b][:, ds * 512 + p * 128: ds * 512 + (p + 1) * 128],
                                                  wband[:, ds * 1024 + sl * 512:(ds * 1024) + (sl + 1) * 512],
                                                  start=(ds == 0), stop=False)
                                    t = te_mark(te.matmul(psK[pb][:], sb_ones[:, 0:128],
                                                          sb_vbrow[:, (vs + sl) * 512:(vs + sl + 1) * 512],
                                                          start=False, stop=True))
                                    sc_.wait_ge(TE.sem, t)
                                    if ik >= 2:
                                        sc_.wait_ge(LC[pb].sem, lc_st[ik - 2])
                                    sc_.activation(out=kro[pb][:], in_=psK[pb][:],
                                                   func=AF.Copy).then_inc(SE.sem, 1)
                                    SE.n += 1
                                    ev_psK[ik] = (SE, SE.n)
                                    sc_.wait_ge(SE.sem, SE.n)
                                    dmaC(vscr[i * 512 + p * 128: i * 512 + (p + 1) * 128,
                                              (vs + sl) * 512:(vs + sl + 1) * 512],
                                         kro[pb][:], par=pb)
                                    lc_st[ik] = LC[pb].n
                                    ik += 1
                        te_chunkb[(band, i)] = TE.n
                        ve_chunkb[(band, i)] = VE.n
            barrier()
            if _PHASES <= 2:
                return

            with ExitStack() as ph:
                xof = ph.enter_context(nc.sbuf_tensor("xof", [128, 32 * NOWN], BF16))
                xot = ph.enter_context(nc.sbuf_tensor("xot", [128, 4 * D], BF16))
                wq = ph.enter_context(nc.sbuf_tensor("wq", [128, 32 * 512], BF16))
                cq = ph.enter_context(nc.sbuf_tensor("cq", [128, NOWN], F32))
                sqt = ph.enter_context(nc.sbuf_tensor("sqt", [128, NOWN], F32))
                qb = [ph.enter_context(nc.sbuf_tensor(f"qb{b}", [128, 512], BF16)) for b in range(2)]
                qro = [ph.enter_context(nc.sbuf_tensor(f"qro{b}", [128, 512], BF16)) for b in range(2)]
                qm1 = ph.enter_context(nc.sbuf_tensor("qm1", [128, 512], F32))
                qm2 = ph.enter_context(nc.sbuf_tensor("qm2", [128, 512], F32))
                psQ = [ph.enter_context(nc.psum_tensor(f"psQ{b}", [128, 512], F32)) for b in range(2)]
                psQP = [ph.enter_context(nc.psum_tensor(f"psQP{b}", [128, 512], F32)) for b in range(2)]
                psXB = [ph.enter_context(nc.psum_tensor(f"psXB{b}", [128, 2048], BF16)) for b in range(2)]
                dmaS(cq[:], cosq[:])
                dmaS(sqt[:], sinq[:])
                ve.wait_ge(LS[0].sem, LS[0].n)
                for i in range(2):
                    if i == 1:
                        gp.wait_ge(TE.sem, TE.n)
                    for p in range(4):
                        dmaG(xot[:, p * D:(p + 1) * D],
                             x_own[i * 512 + p * 128: i * 512 + (p + 1) * 128, :])
                    te.wait_ge(LG[0].sem, LG[0].n)
                    te.wait_ge(LS[0].sem, LS[0].n)
                    def _xof_dst(jg, gi, i=i):
                        j = (2 * jg) // 4
                        p0 = (2 * jg) % 4
                        base = j * NOWN + i * 512 + p0 * 128
                        return xof[:, base: base + gi * 128].rearrange("p (a q) -> p a q", q=128)
                    emit_tgroups(
                        128, 2, psXB,
                        lambda k: xot[:, (k % 4) * D + (k // 4) * 128: (k % 4) * D + (k // 4) * 128 + 128],
                        _xof_dst)
                te.wait_ge(VE.sem, VE.n)
                iq = 0
                ev_psQ, ve_ropeq, lc_q, se_qb = {}, {}, {}, {}
                for quarter in range(8):
                    gp.wait_ge(TE.sem, TE.n)
                    for j in range(32):
                        dmaG(wq[:, j * 512:(j + 1) * 512],
                             qkv_w[j * 128:(j + 1) * 128, quarter * 512:(quarter + 1) * 512])
                    te.wait_ge(LG[0].sem, LG[0].n)
                    for oc in range(4):
                        head = quarter * 4 + oc
                        for qc in range(2):
                            pb = iq % 2
                            if iq >= 2:
                                ctr, n = ev_psQ[iq - 2]
                                te.wait_ge(ctr.sem, n)
                            for ds in range(32):
                                mm = te.matmul(psQ[pb][:],
                                               wq[:, ds * 512 + oc * 128: ds * 512 + oc * 128 + 128],
                                               xof[:, ds * NOWN + qc * 512: ds * NOWN + (qc + 1) * 512],
                                               start=(ds == 0), stop=(ds == 31))
                            t = te_mark(mm)
                            sc_.wait_ge(TE.sem, t)
                            if iq >= 2:
                                sc_.wait_ge(VE.sem, ve_ropeq[iq - 2])
                            sc_.activation(out=qb[pb][:], in_=psQ[pb][:], func=AF.Identity,
                                           bias=sb_qkvbT[:, head:head + 1],
                                           scale=1.0).then_inc(SE.sem, 1)
                            SE.n += 1
                            ev_psQ[iq] = (SE, SE.n)
                            te.wait_ge(SE.sem, SE.n)
                            if iq >= 2:
                                te.wait_ge(VE.sem, ve_ropeq[iq - 2])
                            t = te_mark(te.matmul(psQP[pb][:], sb_perm[:], qb[pb][:],
                                                  start=True, stop=True))
                            ve.wait_ge(TE.sem, t)
                            vstart()
                            ve.tensor_tensor(out=qm1[:], in0=qb[pb][:],
                                             in1=cq[:, qc * 512:(qc + 1) * 512], op=OP.mult)
                            ve.tensor_tensor(out=qm2[:], in0=psQP[pb][:],
                                             in1=sqt[:, qc * 512:(qc + 1) * 512],
                                             op=OP.mult).then_inc(VE.sem, 1)
                            VE.n += 1
                            ve.wait_ge(VE.sem, VE.n)
                            if iq >= 2:
                                ve.wait_ge(LC[pb].sem, lc_q[iq - 2])
                            ve.tensor_tensor(out=qro[pb][:], in0=qm1[:], in1=qm2[:],
                                             op=OP.add).then_inc(VE.sem, 1)
                            VE.n += 1
                            ve_ropeq[iq] = VE.n
                            sc_.wait_ge(VE.sem, VE.n)
                            dmaC(qscr[head * 128:(head + 1) * 128, qc * 512:(qc + 1) * 512],
                                 qro[pb][:], par=pb)
                            lc_q[iq] = LC[pb].n
                            iq += 1
            barrier()
            if _PHASES <= 3:
                return

            with ExitStack() as ph:
                masks = ph.enter_context(nc.sbuf_tensor("masks", [128, 8 * 1024], BF16))
                kh = [ph.enter_context(nc.sbuf_tensor(f"kh{b}", [128, S], BF16)) for b in range(2)]
                vh = [ph.enter_context(nc.sbuf_tensor(f"vh{b}", [128, 64 * 128], BF16)) for b in range(2)]
                qrow = [ph.enter_context(nc.sbuf_tensor(f"qrow{b}", [128, NOWN], BF16)) for b in range(2)]
                scf = ph.enter_context(nc.sbuf_tensor("scf", [128, S], F32))
                prb = [ph.enter_context(nc.sbuf_tensor(f"prb{b}", [128, S], BF16)) for b in range(2)]
                ptr = ph.enter_context(nc.sbuf_tensor("ptr", [128, S], BF16))
                ctxo = ph.enter_context(nc.sbuf_tensor("ctxo", [128, 128], BF16))
                nmax = ph.enter_context(nc.sbuf_tensor("nmax", [128, 2], F32))
                ssum = ph.enter_context(nc.sbuf_tensor("ssum", [128, 2], F32))
                rsum = ph.enter_context(nc.sbuf_tensor("rsum", [128, 2], F32))
                psSc = [ph.enter_context(nc.psum_tensor(f"psSc{b}", [128, 512], F32)) for b in range(2)]
                psPT = [ph.enter_context(nc.psum_tensor(f"psPT{b}", [128, 2048], BF16)) for b in range(2)]
                psCtx = ph.enter_context(nc.psum_tensor("psCtx", [128, 128], F32))

                dmaS(masks[:].rearrange("p (j t) -> p j t", t=1024),
                     maskin.transpose([1, 0, 2]))

                def load_head(h, b):
                    dmaS(kh[b][:], kscr[h * 128:(h + 1) * 128, :], par=b)
                    dmaS(vh[b][:].rearrange("p (m d) -> p m d", d=128),
                         vscr.rearrange("(m p) (hh d) -> p m hh d", p=128, d=128)[:, :, h, :], par=b)
                    dmaS(qrow[b][:], qscr[h * 128:(h + 1) * 128, :], par=b)
                    return LS[b].n

                hd_ls = {0: load_head(0, 0)}
                te_head = {}
                se_ctx_prev = 0
                te_prb = {}
                ev_sc = {}
                isc = 0
                ve_resc = {}
                se_evacs = {}
                pt_state = {0: 0, 1: 0}

                def stage_scores(h, b, j):
                    nonlocal isc
                    E = 1024 * (j + 1)
                    nch = E // 512
                    sc_.wait_ge(SE.sem, SE.n)
                    for c in range(nch):
                        pb = isc % 2
                        if isc >= 2:
                            ctr, n = ev_sc[isc - 2]
                            te.wait_ge(ctr.sem, n)
                        t = te_mark(te.matmul(psSc[pb][:], qrow[b][:, j * 128:(j + 1) * 128],
                                              kh[b][:, c * 512:(c + 1) * 512],
                                              start=True, stop=True))
                        if c < nch - 2:
                            sc_.wait_ge(TE.sem, t)
                            sc_.activation(out=scf[:, c * 512:(c + 1) * 512],
                                           in_=psSc[pb][:], func=AF.Copy,
                                           scale=SCALE).then_inc(SE.sem, 1)
                            SE.n += 1
                            ev_sc[isc] = (SE, SE.n)
                        else:
                            mi = c - (nch - 2)
                            ve.wait_ge(TE.sem, t)
                            if j >= 1:
                                pass
                            vstart()
                            ve.scalar_tensor_tensor(
                                out=scf[:, c * 512:(c + 1) * 512], in0=psSc[pb][:],
                                scalar=SCALE,
                                in1=masks[:, j * 1024 + mi * 512: j * 1024 + (mi + 1) * 512],
                                op0=OP.mult, op1=OP.add).then_inc(VE.sem, 1)
                            VE.n += 1
                            ev_sc[isc] = (VE, VE.n)
                        isc += 1
                    se_evacs[j] = SE.n

                def stage_softmax(h, j):
                    E = 1024 * (j + 1)
                    sl = j % 2
                    ve.wait_ge(SE.sem, se_evacs[j])
                    vstart()
                    ve.reduce_max(out=nmax[:, sl:sl + 1], in_=scf[:, 0:E], axis=AX,
                                  negate=True).then_inc(VE.sem, 1)
                    VE.n += 1
                    nm_ve = VE.n
                    sc_.wait_ge(VE.sem, nm_ve)
                    sc_.wait_ge(TE.sem, te_prb.get(h * 8 + j - 2, 0))
                    sc_.activation(out=prb[sl][:, 0:E], in_=scf[:, 0:E], func=AF.Exp,
                                   bias=nmax[:, sl:sl + 1], scale=1.0,
                                   accum_out=ssum[:, sl:sl + 1]).then_inc(SE.sem, 1)
                    SE.n += 1
                    ve.wait_ge(SE.sem, SE.n)
                    vstart()
                    ve.reciprocal(out=rsum[:, sl:sl + 1], in_=ssum[:, sl:sl + 1]).then_inc(VE.sem, 1)
                    VE.n += 1
                    ve.wait_ge(VE.sem, VE.n)
                    ve.tensor_scalar(out=prb[sl][:, 0:E], in0=prb[sl][:, 0:E],
                                     scalar1=rsum[:, sl:sl + 1], scalar2=None,
                                     op0=OP.mult).then_inc(VE.sem, 1)
                    VE.n += 1
                    ve_resc[j] = VE.n

                def stage_ctx(h, b, j):
                    nonlocal se_ctx_prev
                    E = 1024 * (j + 1)
                    sl = j % 2
                    nkt = E // 128
                    te.wait_ge(VE.sem, ve_resc[j])
                    ngr = nkt // 2
                    for jg in range(ngr):
                        bank = jg % 2
                        if pt_state[bank]:
                            te.wait_ge(VE.sem, pt_state[bank])
                        ti = None
                        for jj in range(2):
                            kt = jg * 2 + jj
                            ti = te.transpose(psPT[bank][:, jj * 1024: jj * 1024 + 128],
                                              prb[sl][:, kt * 128:(kt + 1) * 128], sb_id[:])
                        t = te_mark(ti)
                        ve.wait_ge(TE.sem, t)
                        vstart()
                        ve.tensor_copy(
                            out=ptr[:, jg * 256: jg * 256 + 256].rearrange("p (a q) -> p a q", q=128),
                            in_=psPT[bank][:].rearrange("p (a q) -> p a q", q=1024)[:, 0:2, 0:128],
                        ).then_inc(VE.sem, 1)
                        VE.n += 1
                        pt_state[bank] = VE.n
                    te_prb[h * 8 + j] = TE.n
                    te.wait_ge(VE.sem, VE.n)
                    te.wait_ge(SE.sem, se_ctx_prev)
                    mm = None
                    for kt in range(nkt):
                        mm = te.matmul(psCtx[:], ptr[:, kt * 128:(kt + 1) * 128],
                                       vh[b][:, kt * 128:(kt + 1) * 128],
                                       start=(kt == 0), stop=(kt == nkt - 1))
                    t = te_mark(mm)
                    sc_.wait_ge(TE.sem, t)
                    sc_.wait_ge(LC[0].sem, LC[0].n)
                    sc_.activation(out=ctxo[:], in_=psCtx[:],
                                   func=AF.Copy).then_inc(SE.sem, 1)
                    SE.n += 1
                    se_ctx_prev = SE.n
                    sc_.wait_ge(SE.sem, SE.n)
                    dmaC(ctxscr[j * 128:(j + 1) * 128, h * 128:(h + 1) * 128], ctxo[:])

                for h in range(H):
                    b = h % 2
                    if h + 1 < H:
                        if h >= 1:
                            sy.wait_ge(TE.sem, te_head[h - 1])
                        hd_ls[h + 1] = load_head(h + 1, (h + 1) % 2)
                    te.wait_ge(LS[b].sem, hd_ls[h])
                    for j in range(8):
                        stage_scores(h, b, j)
                        stage_softmax(h, j)
                        if j >= 1:
                            stage_ctx(h, b, j - 1)
                    stage_ctx(h, b, 7)
                    te_head[h] = TE.n
            barrier()

            with ExitStack() as ph:
                wo = ph.enter_context(nc.sbuf_tensor("wo", [128, 32 * 512], BF16))
                ctt = ph.enter_context(nc.sbuf_tensor("ctt", [128, D], BF16))
                cfm = ph.enter_context(nc.sbuf_tensor("cfm", [128, 32 * 128], BF16))
                xos = [ph.enter_context(nc.sbuf_tensor(f"xos{b}", [128, 512], F32)) for b in range(2)]
                r1o = [ph.enter_context(nc.sbuf_tensor(f"r1o{b}", [128, 512], F32)) for b in range(2)]
                psO = [ph.enter_context(nc.psum_tensor(f"psO{b}", [128, 512], F32)) for b in range(2)]
                psCT = [ph.enter_context(nc.psum_tensor(f"psCT{b}", [128, 2048], BF16)) for b in range(2)]
                io = 0
                ve_o, lc_o = {}, {}
                for os_ in range(8):
                    gp.wait_ge(TE.sem, TE.n)
                    for j in range(32):
                        dmaG(wo[:, j * 512:(j + 1) * 512],
                             out_w[j * 128:(j + 1) * 128, os_ * 512:(os_ + 1) * 512], par=0)
                    for tt in range(8):
                        sy.wait_ge(TE.sem, TE.n)
                        dmaS(ctt[:], ctxscr[tt * 128:(tt + 1) * 128, :], par=tt % 2)
                        if io >= 2:
                            gp.wait_ge(VE.sem, ve_o[io - 2])
                        dmaG(xos[io % 2][:],
                             x_own[tt * 128:(tt + 1) * 128, os_ * 512:(os_ + 1) * 512],
                             par=io % 2)
                        te.wait_ge(LS[tt % 2].sem, LS[tt % 2].n)
                        te.wait_ge(LG[0].sem, LG[0].n)
                        emit_tgroups(
                            32, 2, psCT,
                            lambda k: ctt[:, k * 128:(k + 1) * 128],
                            lambda jg, gi: cfm[:, jg * 256: jg * 256 + gi * 128].rearrange("p (a q) -> p a q", q=128))
                        te.wait_ge(VE.sem, VE.n)
                        pb = io % 2
                        if io >= 2:
                            te.wait_ge(VE.sem, ve_o[io - 2])
                        for ds in range(32):
                            te.matmul(psO[pb][:], cfm[:, ds * 128:(ds + 1) * 128],
                                      wo[:, ds * 512:(ds + 1) * 512],
                                      start=(ds == 0), stop=False)
                        t = te_mark(te.matmul(psO[pb][:], sb_ones[:, 0:128],
                                              sb_obrow[:, os_ * 512:(os_ + 1) * 512],
                                              start=False, stop=True))
                        ve.wait_ge(TE.sem, t)
                        ve.wait_ge(LG[pb].sem, LG[pb].n)
                        vstart()
                        if io >= 2:
                            ve.wait_ge(LC[pb].sem, lc_o[io - 2])
                        ve.tensor_tensor(out=r1o[pb][:], in0=psO[pb][:], in1=xos[pb][:],
                                         op=OP.add).then_inc(VE.sem, 1)
                        VE.n += 1
                        ve_o[io] = VE.n
                        sc_.wait_ge(VE.sem, VE.n)
                        dmaC(r1scr[tt * 128:(tt + 1) * 128, os_ * 512:(os_ + 1) * 512],
                             r1o[pb][:], par=pb)
                        lc_o[io] = LC[pb].n
                        io += 1
            barrier()
            if _PHASES <= 5:
                return

            x2fm_es = ExitStack()
            x2fm = x2fm_es.enter_context(nc.sbuf_tensor("x2fm", [128, 32 * NOWN], BF16))

            def ln_rows(src, dst, ph):
                rrow, xc, sqr, x2r = ph["rrow"], ph["xc"], ph["sqr"], ph["x2r"]
                st1, st2 = ph["st1"], ph["st2"]
                sy.wait_ge(VE.sem, VE.n)
                dmaS(rrow[:], src)
                ve.wait_ge(LS[0].sem, LS[0].n)
                vstart()
                ve.reduce_sum(out=st1[:], in_=rrow[:], axis=AX).then_inc(VE.sem, 1)
                VE.n += 1
                ve.wait_ge(VE.sem, VE.n)
                ve.tensor_scalar(out=st1[:], in0=st1[:], scalar1=1.0 / D, scalar2=None,
                                 op0=OP.mult).then_inc(VE.sem, 1)
                VE.n += 1
                ve.wait_ge(VE.sem, VE.n)
                ve.tensor_scalar(out=xc[:], in0=rrow[:], scalar1=st1[:, 0:1], scalar2=None,
                                 op0=OP.subtract).then_inc(VE.sem, 1)
                VE.n += 1
                sc_.wait_ge(VE.sem, VE.n)
                sc_.activation(out=sqr[:], in_=xc[:], func=AF.Square).then_inc(SE.sem, 1)
                SE.n += 1
                ve.wait_ge(SE.sem, SE.n)
                vstart()
                ve.reduce_sum(out=st2[:], in_=sqr[:], axis=AX).then_inc(VE.sem, 1)
                VE.n += 1
                ve.wait_ge(VE.sem, VE.n)
                ve.tensor_scalar(out=st2[:], in0=st2[:], scalar1=1.0 / D, scalar2=float(EPS),
                                 op0=OP.mult, op1=OP.add).then_inc(VE.sem, 1)
                VE.n += 1
                sc_.wait_ge(VE.sem, VE.n)
                sc_.activation(out=st2[:], in_=st2[:], func=AF.Sqrt).then_inc(SE.sem, 1)
                SE.n += 1
                ve.wait_ge(SE.sem, SE.n)
                vstart()
                ve.reciprocal(out=st2[:], in_=st2[:]).then_inc(VE.sem, 1)
                VE.n += 1
                ve.wait_ge(VE.sem, VE.n)
                ve.wait_ge(LC[0].sem, LC[0].n)
                ve.tensor_scalar(out=x2r[:], in0=xc[:], scalar1=st2[:, 0:1], scalar2=None,
                                 op0=OP.mult).then_inc(VE.sem, 1)
                VE.n += 1
                sc_.wait_ge(VE.sem, VE.n)
                if dst is not None:
                    dmaC(dst, x2r[:])

            with ExitStack() as ph0:
                ph = {k: ph0.enter_context(nc.sbuf_tensor(k, [128, D], F32))
                      for k in ("rrow", "xc", "sqr", "x2r")}
                ph["st1"] = ph0.enter_context(nc.sbuf_tensor("st1", [128, 1], F32))
                ph["st2"] = ph0.enter_context(nc.sbuf_tensor("st2", [128, 1], F32))
                x2b = ph0.enter_context(nc.sbuf_tensor("x2b", [128, D], BF16))
                psX2 = [ph0.enter_context(nc.psum_tensor(f"psX2{b}", [128, 4096], BF16)) for b in range(2)]
                for tt in range(8):
                    ln_rows(r1scr[tt * 128:(tt + 1) * 128, :],
                            x2scr[tt * 128:(tt + 1) * 128, :], ph)
                    sc_.wait_ge(TE.sem, TE.n)
                    sc_.activation(out=x2b[:], in_=ph["x2r"][:], func=AF.Copy).then_inc(SE.sem, 1)
                    SE.n += 1
                    te.wait_ge(SE.sem, SE.n)
                    x2v = x2fm[:].rearrange("p (j t) -> p j t", t=NOWN)
                    emit_tgroups(
                        32, 4, psX2,
                        lambda k: x2b[:, k * 128:(k + 1) * 128],
                        lambda jg, gi, tt=tt: x2v[:, jg * 4: jg * 4 + gi, tt * 128:(tt + 1) * 128])
            barrier()
            if _PHASES <= 6:
                return

            with ExitStack() as ph:
                pk1 = [ph.enter_context(nc.sbuf_tensor(f"pk1{b}", [128, 2048], I32)) for b in range(2)]
                pk3 = [ph.enter_context(nc.sbuf_tensor(f"pk3{b}", [128, 2048], I32)) for b in range(2)]
                sc1 = [ph.enter_context(nc.sbuf_tensor(f"sc1{b}", [128, 64], F32)) for b in range(2)]
                sc3 = [ph.enter_context(nc.sbuf_tensor(f"sc3{b}", [128, 64], F32)) for b in range(2)]
                msb = ph.enter_context(nc.sbuf_tensor("msb", [128, 2048], I32))
                lsb = ph.enter_context(nc.sbuf_tensor("lsb", [128, 2048], I32))
                nat = [ph.enter_context(nc.sbuf_tensor(f"nat{b}", [128, D], BF16)) for b in range(2)]
                w1t = [ph.enter_context(nc.sbuf_tensor(f"w1t{b}", [128, D], BF16)) for b in range(2)]
                w3t = [ph.enter_context(nc.sbuf_tensor(f"w3t{b}", [128, D], BF16)) for b in range(2)]
                sil = [ph.enter_context(nc.sbuf_tensor(f"sil{b}", [128, 512], F32)) for b in range(2)]
                hao = [ph.enter_context(nc.sbuf_tensor(f"hao{b}", [128, 512], BF16)) for b in range(2)]
                psA = [ph.enter_context(nc.psum_tensor(f"psA{b}", [128, 512], F32)) for b in range(2)]
                psB = [ph.enter_context(nc.psum_tensor(f"psB{b}", [128, 512], F32)) for b in range(2)]
                psW = [ph.enter_context(nc.psum_tensor(f"psW{b}", [128, 2048], BF16)) for b in range(2)]

                w1r = w1p.rearrange("(h r) b -> h (r b)", r=32)
                w3r = w3p.rearrange("(h r) b -> h (r b)", r=32)
                s1r = s1v.rearrange("(h g) -> h g", g=64)
                s3r = s3v.rearrange("(h g) -> h g", g=64)

                def deq_tile(pk, scv, outT, slot, par):
                    ve.wait_ge(LS[par].sem, LS[par].n)
                    vstart()
                    ve.tensor_scalar(out=msb[:], in0=pk[:], scalar1=4, scalar2=None,
                                     op0=OP.arith_shift_right)
                    ve.tensor_scalar(out=lsb[:], in0=pk[:], scalar1=28, scalar2=28,
                                     op0=OP.arith_shift_left,
                                     op1=OP.arith_shift_right).then_inc(VE.sem, 1)
                    VE.n += 1
                    ve.wait_ge(VE.sem, VE.n)
                    nb = nat[slot]
                    natv = nb[:].rearrange("p (m z c) -> p z m c", z=2, c=64)
                    mv = msb[:].rearrange("p (m c) -> p m c", c=64)
                    lv = lsb[:].rearrange("p (m c) -> p m c", c=64)
                    sev = scv[:].rearrange("p (m z) -> p z m", z=2)[:, 0].unsqueeze(2).broadcast_to([128, 32, 64])
                    sov = scv[:].rearrange("p (m z) -> p z m", z=2)[:, 1].unsqueeze(2).broadcast_to([128, 32, 64])
                    ve.tensor_tensor(out=natv[:, 0], in0=mv, in1=sev, op=OP.mult)
                    ve.tensor_tensor(out=natv[:, 1], in0=lv, in1=sov, op=OP.mult).then_inc(VE.sem, 1)
                    VE.n += 1
                    te.wait_ge(VE.sem, VE.n)
                    emit_tgroups(
                        32, 2, psW,
                        lambda k, nb=nb: nb[:, k * 128:(k + 1) * 128],
                        lambda jg, gi, outT=outT: outT[:, jg * 256: jg * 256 + gi * 128].rearrange("p (a q) -> p a q", q=128))

                te_ht, se_ab, ve_ab, lc_ab = {}, {}, {}, {}
                for ht in range(NHT):
                    b = ht % 2
                    if ht >= 2:
                        sy.wait_ge(VE.sem, ve_ab.get(2 * (ht - 2) + 1, 0))
                    dmaS(pk1[b][:], w1r[ht * 128:(ht + 1) * 128, :], par=b)
                    dmaS(sc1[b][:], s1r[ht * 128:(ht + 1) * 128, :], par=b)
                    dmaS(pk3[b][:], w3r[ht * 128:(ht + 1) * 128, :], par=b)
                    dmaS(sc3[b][:], s3r[ht * 128:(ht + 1) * 128, :], par=b)
                    if ht >= 2:
                        ve.wait_ge(TE.sem, te_ht[ht - 2])
                    deq_tile(pk1[b], sc1[b], w1t[b], 0, b)
                    deq_tile(pk3[b], sc3[b], w3t[b], 1, b)
                    te.wait_ge(VE.sem, VE.n)
                    for c in range(2):
                        it = ht * 2 + c
                        pb = c
                        if it >= 2:
                            te.wait_ge(SE.sem, se_ab[it - 2])
                            te.wait_ge(VE.sem, ve_ab[it - 2])
                        for ds in range(32):
                            te.matmul(psA[pb][:], w1t[b][:, ds * 128:(ds + 1) * 128],
                                      x2fm[:, ds * NOWN + c * 512: ds * NOWN + (c + 1) * 512],
                                      start=(ds == 0), stop=(ds == 31))
                        for ds in range(32):
                            mm = te.matmul(psB[pb][:], w3t[b][:, ds * 128:(ds + 1) * 128],
                                           x2fm[:, ds * NOWN + c * 512: ds * NOWN + (c + 1) * 512],
                                           start=(ds == 0), stop=(ds == 31))
                        t = te_mark(mm)
                        sc_.wait_ge(TE.sem, t)
                        if it >= 2:
                            sc_.wait_ge(VE.sem, ve_ab[it - 2])
                        sc_.activation(out=sil[pb][:], in_=psA[pb][:],
                                       func=AF.Silu).then_inc(SE.sem, 1)
                        SE.n += 1
                        se_ab[it] = SE.n
                        ve.wait_ge(SE.sem, SE.n)
                        vstart()
                        if it >= 2:
                            ve.wait_ge(LC[pb].sem, lc_ab[it - 2])
                        ve.tensor_tensor(out=hao[pb][:], in0=sil[pb][:], in1=psB[pb][:],
                                         op=OP.mult).then_inc(VE.sem, 1)
                        VE.n += 1
                        ve_ab[it] = VE.n
                        sc_.wait_ge(VE.sem, VE.n)
                        dmaC(hactscr[ht * 128:(ht + 1) * 128, c * 512:(c + 1) * 512],
                             hao[pb][:], par=pb)
                        lc_ab[it] = LC[pb].n
                    te_ht[ht] = TE.n
            barrier()
            x2fm_es.close()
            if _PHASES <= 7:
                return

            with ExitStack() as ph:
                natS = ph.enter_context(nc.sbuf_tensor("natS", [128, 32 * 1024], BF16))
                pk2 = [ph.enter_context(nc.sbuf_tensor(f"pk2{b}", [128, 512], I32)) for b in range(2)]
                sc2 = [ph.enter_context(nc.sbuf_tensor(f"sc2{b}", [128, 172], F32)) for b in range(2)]
                msb2 = ph.enter_context(nc.sbuf_tensor("msb2", [128, 512], I32))
                lsb2 = ph.enter_context(nc.sbuf_tensor("lsb2", [128, 512], I32))
                w2st = ph.enter_context(nc.sbuf_tensor("w2st", [128, D], BF16))
                psW2 = [ph.enter_context(nc.psum_tensor(f"psW2{b}", [128, 4096], BF16)) for b in range(2)]
                w2r = w2p.rearrange("(d r) b -> d (r b)", r=86)
                s2r = s2v.rearrange("(d g) -> d g", g=172)
                ve_w2 = {}
                te_sup = {}
                for sup in range(11):
                    HS = 1024 if sup < 10 else 768
                    NM = HS // 128
                    for dt in range(32):
                        it = sup * 32 + dt
                        b = it % 2
                        if it >= 2:
                            sy.wait_ge(VE.sem, ve_w2[it - 2])
                        dmaS(pk2[b][:, 0:NM * 64],
                             w2r[dt * 128:(dt + 1) * 128, sup * 512: sup * 512 + NM * 64], par=b)
                        dmaS(sc2[b][:], s2r[dt * 128:(dt + 1) * 128, :], par=b)
                        ve.wait_ge(LS[b].sem, LS[b].n)
                        if dt == 0 and sup >= 1:
                            ve.wait_ge(TE.sem, te_sup[sup - 1])
                        vstart()
                        ve.tensor_scalar(out=msb2[:, 0:NM * 64], in0=pk2[b][:, 0:NM * 64],
                                         scalar1=4, scalar2=None,
                                         op0=OP.arith_shift_right)
                        ve.tensor_scalar(out=lsb2[:, 0:NM * 64], in0=pk2[b][:, 0:NM * 64],
                                         scalar1=28, scalar2=28,
                                         op0=OP.arith_shift_left,
                                         op1=OP.arith_shift_right).then_inc(VE.sem, 1)
                        VE.n += 1
                        ve.wait_ge(VE.sem, VE.n)
                        natv = natS[:, dt * 1024: dt * 1024 + HS].rearrange(
                            "p (m z c) -> p z m c", z=2, c=64)
                        mv = msb2[:, 0:NM * 64].rearrange("p (m c) -> p m c", c=64)
                        lv = lsb2[:, 0:NM * 64].rearrange("p (m c) -> p m c", c=64)
                        sall = sc2[b][:, 16 * sup: 16 * sup + 2 * NM]
                        sev = sall.rearrange("p (m z) -> p z m", z=2)[:, 0].unsqueeze(2).broadcast_to([128, NM, 64])
                        sov = sall.rearrange("p (m z) -> p z m", z=2)[:, 1].unsqueeze(2).broadcast_to([128, NM, 64])
                        ve.tensor_tensor(out=natv[:, 0], in0=mv, in1=sev, op=OP.mult)
                        ve.tensor_tensor(out=natv[:, 1], in0=lv, in1=sov, op=OP.mult).then_inc(VE.sem, 1)
                        VE.n += 1
                        ve_w2[it] = VE.n
                    te.wait_ge(VE.sem, VE.n)
                    for hb in range(NM):
                        ve.wait_ge(LC[0].sem, LC[0].n)
                        emit_tgroups(
                            32, 4, psW2,
                            lambda k, hb=hb: natS[:, k * 1024 + hb * 128: k * 1024 + (hb + 1) * 128],
                            lambda jg, gi: w2st[:, jg * 512: jg * 512 + gi * 128].rearrange("p (a q) -> p a q", q=128))
                        sc_.wait_ge(VE.sem, VE.n)
                        dmaC(w2tscr[sup * 1024 + hb * 128: sup * 1024 + (hb + 1) * 128, :],
                             w2st[:])
                    te_sup[sup] = TE.n
            barrier()
            if _PHASES <= 8:
                return

            with ExitStack() as ph:
                hact_sb = ph.enter_context(nc.sbuf_tensor("hact_sb", [128, NHT * NOWN], BF16))
                w2b = [ph.enter_context(nc.sbuf_tensor(f"w2b{b}", [128, 512], BF16)) for b in range(2)]
                x2s = ph.enter_context(nc.sbuf_tensor("x2s", [128, 512], F32))
                r2o = ph.enter_context(nc.sbuf_tensor("r2o", [128, 512], F32))
                psF = [ph.enter_context(nc.psum_tensor(f"psF{t}", [128, 512], F32)) for t in range(8)]
                for hs in range(NHT):
                    dmaG(hact_sb[:, hs * NOWN:(hs + 1) * NOWN],
                         hactscr[hs * 128:(hs + 1) * 128, :], par=1)
                te.wait_ge(LG[1].sem, LG[1].n)
                te_f = {}
                ve_ds = 0
                for ds in range(8):
                    for hs in range(NHT):
                        it = ds * NHT + hs
                        b = it % 2
                        if it >= 2:
                            sy.wait_ge(TE.sem, te_f[it - 2])
                        dmaS(w2b[b][:], w2tscr[hs * 128:(hs + 1) * 128, ds * 512:(ds + 1) * 512], par=b)
                        te.wait_ge(LS[b].sem, LS[b].n)
                        if hs == 0:
                            te.wait_ge(VE.sem, ve_ds)
                        for tt in range(8):
                            mm = te.matmul(psF[tt][:],
                                           hact_sb[:, hs * NOWN + tt * 128: hs * NOWN + (tt + 1) * 128],
                                           w2b[b][:], start=(hs == 0), stop=(hs == NHT - 1))
                        te_f[it] = te_mark(mm)
                    for tt in range(8):
                        gp.wait_ge(VE.sem, VE.n)
                        dmaG(x2s[:], x2scr[tt * 128:(tt + 1) * 128, ds * 512:(ds + 1) * 512])
                        ve.wait_ge(TE.sem, TE.n)
                        ve.wait_ge(LG[0].sem, LG[0].n)
                        vstart()
                        ve.wait_ge(LC[0].sem, LC[0].n)
                        ve.tensor_tensor(out=r2o[:], in0=psF[tt][:], in1=x2s[:],
                                         op=OP.add).then_inc(VE.sem, 1)
                        VE.n += 1
                        sc_.wait_ge(VE.sem, VE.n)
                        dmaC(r2scr[tt * 128:(tt + 1) * 128, ds * 512:(ds + 1) * 512],
                             r2o[:])
                    ve_ds = VE.n
            barrier()
            if _PHASES <= 9:
                return

            with ExitStack() as ph0:
                ph = {k: ph0.enter_context(nc.sbuf_tensor(k + "b", [128, D], F32))
                      for k in ("rrow", "xc", "sqr", "x2r")}
                ph["st1"] = ph0.enter_context(nc.sbuf_tensor("st1b", [128, 1], F32))
                ph["st2"] = ph0.enter_context(nc.sbuf_tensor("st2b", [128, 1], F32))
                for tt in range(8):
                    ln_rows(r2scr[tt * 128:(tt + 1) * 128, :],
                            out_own[tt * 128:(tt + 1) * 128, :], ph)
                for c in LC:
                    sy.wait_ge(c.sem, c.n)
                for c in LS:
                    sy.wait_ge(c.sem, c.n)
                for c in LG:
                    sy.wait_ge(c.sem, c.n)

    return nc


def _host_inputs(inputs, core):
    x = np.asarray(inputs["L0_attn_hidden_states"]).reshape(S, D)
    qkv_b = np.asarray(inputs["L0_attn_qkv_b"])
    out_b = np.asarray(inputs["L0_attn_out_b"])

    own = np.concatenate([np.arange(128 * (8 * j + core), 128 * (8 * j + core) + 128)
                          for j in range(8)])
    mask = np.zeros((8, 128, 1024), np.float32)
    for j in range(8):
        q = 128 * (8 * j + core) + np.arange(128)[:, None]
        k = 1024 * j + np.arange(1024)[None, :]
        mask[j] = np.where(k <= q, 0.0, np.float32(NEG))
    inv = 1.0 / (10000.0 ** (np.arange(64, dtype=np.float64) / 64.0))
    pos = np.arange(S, dtype=np.float64)
    ang = pos[None, :] * inv[:, None]
    cosk_ = np.cos(ang).astype(np.float32)
    sink_ = np.sin(ang).astype(np.float32)
    cosk_ = np.concatenate([cosk_, cosk_], axis=0)
    sink_ = np.concatenate([sink_, sink_], axis=0)

    perm = np.zeros((128, 128), np.float32)
    for p in range(64):
        perm[p, p + 64] = -1.0
        perm[p + 64, p] = 1.0

    return {
        "x_full": x,
        "x_own": np.ascontiguousarray(x[own]),
        "qkv_w": np.asarray(inputs["L0_attn_qkv_w"]),
        "qkv_bT": np.ascontiguousarray(qkv_b.reshape(96, 128).T),
        "vb_row": np.ascontiguousarray(qkv_b[8192:].reshape(1, D)),
        "out_w": np.asarray(inputs["L0_attn_out_w"]),
        "ob_row": np.ascontiguousarray(out_b.reshape(1, D)),
        "w1p": np.asarray(inputs["L0_ffn_w1"]),
        "s1v": np.asarray(inputs["L0_ffn_s1"]),
        "w2p": np.asarray(inputs["L0_ffn_w2"]),
        "s2v": np.asarray(inputs["L0_ffn_s2"]),
        "w3p": np.asarray(inputs["L0_ffn_w3"]),
        "s3v": np.asarray(inputs["L0_ffn_s3"]),
        "maskin": mask.astype(ml_dtypes.bfloat16),
        "cosk": np.ascontiguousarray(cosk_),
        "sink": np.ascontiguousarray(sink_),
        "cosq": np.ascontiguousarray(cosk_[:, own]),
        "sinq": np.ascontiguousarray(sink_[:, own]),
        "identb": np.eye(128).astype(ml_dtypes.bfloat16),
        "permT": np.ascontiguousarray(perm.T).astype(ml_dtypes.bfloat16),
    }, own


def kernel(L0_attn_hidden_states, L0_attn_qkv_w, L0_attn_qkv_b, L0_attn_out_w,
           L0_attn_out_b, L0_ffn_w1, L0_ffn_s1, L0_ffn_w2, L0_ffn_s2,
           L0_ffn_w3, L0_ffn_s3):
    inputs = dict(
        L0_attn_hidden_states=L0_attn_hidden_states,
        L0_attn_qkv_w=L0_attn_qkv_w, L0_attn_qkv_b=L0_attn_qkv_b,
        L0_attn_out_w=L0_attn_out_w, L0_attn_out_b=L0_attn_out_b,
        L0_ffn_w1=L0_ffn_w1, L0_ffn_s1=L0_ffn_s1,
        L0_ffn_w2=L0_ffn_w2, L0_ffn_s2=L0_ffn_s2,
        L0_ffn_w3=L0_ffn_w3, L0_ffn_s3=L0_ffn_s3,
    )
    nc = _build()
    in_maps, owns = [], []
    for c in range(8):
        m, own = _host_inputs(inputs, c)
        in_maps.append(m)
        owns.append(own)
    res = run_bass_kernel_spmd(nc, in_maps, core_ids=list(range(8)))
    out = np.empty((1, S, D), np.float32)
    for c in range(8):
        out[0, owns[c]] = res.results[c]["out_own"]
    return out


# version 03
# speedup vs baseline: 1.2932x; 1.0467x over previous
import numpy as np
import ml_dtypes
import concourse.bass as bass
import concourse.mybir as mybir
from concourse.bass_utils import run_bass_kernel_spmd
from concourse.alu_op_type import AluOpType
from contextlib import ExitStack

F32 = mybir.dt.float32
BF16 = mybir.dt.bfloat16
I32 = mybir.dt.int32
AF = mybir.ActivationFunctionType
OP = AluOpType
AX = mybir.AxisListType.X

S = 8192
D = 4096
H = 32
NOWN = 1024
HID = 11008
NHT = HID // 128
SCALE = float(1.0 / np.sqrt(np.float32(128.0)))
EPS = 1e-5
NEG = -66504.0

_DBG_KIND = {}
_PHASES = 99


class Ctr:
    def __init__(self, sem):
        self.sem = sem
        self.n = 0


def _build():
    nc = bass.Bass("TRN2", target_bir_lowering=False)
    ap = lambda t: t.ap()

    x_full = ap(nc.dram_tensor("x_full", [S, D], F32, kind="ExternalInput"))
    x_own = ap(nc.dram_tensor("x_own", [NOWN, D], F32, kind="ExternalInput"))
    qkv_w = ap(nc.dram_tensor("qkv_w", [D, 3 * D], F32, kind="ExternalInput"))
    qkv_bT = ap(nc.dram_tensor("qkv_bT", [128, 96], F32, kind="ExternalInput"))
    vb_row = ap(nc.dram_tensor("vb_row", [1, D], F32, kind="ExternalInput"))
    out_w = ap(nc.dram_tensor("out_w", [D, D], F32, kind="ExternalInput"))
    ob_row = ap(nc.dram_tensor("ob_row", [1, D], F32, kind="ExternalInput"))
    w1p = ap(nc.dram_tensor("w1p", [352256, 64], I32, kind="ExternalInput"))
    s1v = ap(nc.dram_tensor("s1v", [704512], F32, kind="ExternalInput"))
    w2p = ap(nc.dram_tensor("w2p", [352256, 64], I32, kind="ExternalInput"))
    s2v = ap(nc.dram_tensor("s2v", [704512], F32, kind="ExternalInput"))
    w3p = ap(nc.dram_tensor("w3p", [352256, 64], I32, kind="ExternalInput"))
    s3v = ap(nc.dram_tensor("s3v", [704512], F32, kind="ExternalInput"))
    maskin = ap(nc.dram_tensor("maskin", [8, 128, 1024], BF16, kind="ExternalInput"))
    cosk = ap(nc.dram_tensor("cosk", [128, S], F32, kind="ExternalInput"))
    sink = ap(nc.dram_tensor("sink", [128, S], F32, kind="ExternalInput"))
    cosq = ap(nc.dram_tensor("cosq", [128, NOWN], F32, kind="ExternalInput"))
    sinq = ap(nc.dram_tensor("sinq", [128, NOWN], F32, kind="ExternalInput"))
    identb = ap(nc.dram_tensor("identb", [128, 128], BF16, kind="ExternalInput"))
    permT = ap(nc.dram_tensor("permT", [128, 128], BF16, kind="ExternalInput"))

    out_own = ap(nc.dram_tensor("out_own", [NOWN, D], F32, kind="ExternalOutput"))

    def scr(name, shape, dt):
        return ap(nc.dram_tensor(name, shape, dt, kind=_DBG_KIND.get(name, "Internal")))

    xfm = scr("xfm", [D, S], BF16)
    kscr = scr("kscr", [D, S], BF16)
    vscr = scr("vscr", [S, D], BF16)
    qscr = scr("qscr", [D, NOWN], BF16)
    ctxscr = scr("ctxscr", [NOWN, D], BF16)
    x2scr = scr("x2scr", [NOWN, D], F32)
    r1scr = scr("r1scr", [NOWN, D], F32)
    r2scr = scr("r2scr", [NOWN, D], F32)
    hactscr = scr("hactscr", [HID, NOWN], BF16)
    w2tscr = scr("w2tscr", [HID, D], BF16)

    es = ExitStack()
    with es:
        LS = [Ctr(es.enter_context(nc.semaphore(f"LS{i}"))) for i in range(2)]
        LC = [Ctr(es.enter_context(nc.semaphore(f"LC{i}"))) for i in range(2)]
        LG = [Ctr(es.enter_context(nc.semaphore(f"LG{i}"))) for i in range(2)]
        TE = Ctr(es.enter_context(nc.semaphore("TEs")))
        VE = Ctr(es.enter_context(nc.semaphore("VEs")))
        SE = Ctr(es.enter_context(nc.semaphore("SEs")))
        BAR = Ctr(es.enter_context(nc.semaphore("BARs")))

        sb_qkvbT = es.enter_context(nc.sbuf_tensor("sb_qkvbT", [128, 96], F32))
        sb_vbrow = es.enter_context(nc.sbuf_tensor("sb_vbrow", [1, D], BF16))
        sb_obrow = es.enter_context(nc.sbuf_tensor("sb_obrow", [1, D], BF16))
        sb_id = es.enter_context(nc.sbuf_tensor("sb_id", [128, 128], BF16))
        sb_perm = es.enter_context(nc.sbuf_tensor("sb_perm", [128, 128], BF16))
        sb_ones = es.enter_context(nc.sbuf_tensor("sb_ones", [1, 512], BF16))

        blk = es.enter_context(nc.Block())

        @blk.gpsimd
        def _(_g):
            sy, sc_, gp, ve, te = nc.sync, nc.scalar, nc.gpsimd, nc.vector, nc.tensor

            def dmaS(out, in_, par=0):
                sy.dma_start(out=out, in_=in_).then_inc(LS[par].sem, 16)
                LS[par].n += 16
                return (LS[par], LS[par].n)

            def dmaC(out, in_, par=0):
                sc_.dma_start(out=out, in_=in_).then_inc(LC[par].sem, 16)
                LC[par].n += 16
                return (LC[par], LC[par].n)

            def dmaG(out, in_, par=0):
                gp.dma_start(out=out, in_=in_).then_inc(LG[par].sem, 16)
                LG[par].n += 16
                return (LG[par], LG[par].n)

            def wq_(eng, tok):
                eng.wait_ge(tok[0].sem, tok[1])

            def te_mark(instr):
                instr.then_inc(TE.sem, 1)
                TE.n += 1
                return TE.n

            def vstart():
                ve.wait_ge(VE.sem, VE.n)

            tg_state = {}

            def emit_tgroups(nblk, g, psl, src_fn, dst_fn, evac="ve"):
                st = tg_state.setdefault(psl[0].name, {0: None, 1: None})
                ngr = (nblk + g - 1) // g
                for jg in range(ngr):
                    gi = min(g, nblk - jg * g)
                    bank = jg % 2
                    if st[bank] is not None:
                        ctr, n = st[bank]
                        te.wait_ge(ctr.sem, n)
                    ti = None
                    for jj in range(gi):
                        ti = te.transpose(psl[bank][:, jj * 1024: jj * 1024 + 128],
                                          src_fn(jg * g + jj), sb_id[:])
                    t = te_mark(ti)
                    srcv = psl[bank][:].rearrange("p (a q) -> p a q", q=1024)[:, 0:gi, 0:128]
                    if evac == "se":
                        sc_.wait_ge(TE.sem, t)
                        sc_.activation(out=dst_fn(jg, gi), in_=srcv,
                                       func=AF.Copy).then_inc(SE.sem, 1)
                        SE.n += 1
                        st[bank] = (SE, SE.n)
                    else:
                        ve.wait_ge(TE.sem, t)
                        vstart()
                        ve.tensor_copy(out=dst_fn(jg, gi), in_=srcv).then_inc(VE.sem, 1)
                        VE.n += 1
                        st[bank] = (VE, VE.n)

            def barrier():
                for c in LS:
                    sy.wait_ge(c.sem, c.n)
                for c in LC:
                    sc_.wait_ge(c.sem, c.n)
                for c in LG:
                    gp.wait_ge(c.sem, c.n)
                te.wait_ge(TE.sem, TE.n)
                ve.wait_ge(VE.sem, VE.n)
                sc_.wait_ge(SE.sem, SE.n)
                for e in (te, ve, sc_, gp, sy):
                    e.sem_inc(BAR.sem, 1)
                BAR.n += 5
                for e in (te, ve, sc_, gp, sy):
                    e.wait_ge(BAR.sem, BAR.n)

            with ExitStack() as p0:
                sb_vbf = p0.enter_context(nc.sbuf_tensor("sb_vbf", [1, D], F32))
                sb_obf = p0.enter_context(nc.sbuf_tensor("sb_obf", [1, D], F32))
                dmaS(sb_qkvbT[:], qkv_bT[:])
                dmaS(sb_vbf[:], vb_row[:])
                dmaS(sb_obf[:], ob_row[:])
                dmaS(sb_id[:], identb[:])
                dmaS(sb_perm[:], permT[:])
                ve.wait_ge(LS[0].sem, LS[0].n)
                ve.memset(sb_ones[:], 1.0)
                ve.tensor_copy(out=sb_vbrow[:], in_=sb_vbf[:]).then_inc(VE.sem, 1)
                VE.n += 1
                ve.tensor_copy(out=sb_obrow[:], in_=sb_obf[:]).then_inc(VE.sem, 1)
                VE.n += 1
                barrier()
            if _PHASES <= 0:
                return

            with ExitStack() as ph:
                xtok = [ph.enter_context(nc.sbuf_tensor(f"xtok{b}", [128, 4 * D], BF16)) for b in range(2)]
                stg = [ph.enter_context(nc.sbuf_tensor(f"stg{b}", [128, 32 * 512], BF16)) for b in range(2)]
                psT = [ph.enter_context(nc.psum_tensor(f"psT{b}", [128, 4096], BF16)) for b in range(2)]
                te_chunk, ls_chunk, ve_chunk = {}, {}, {}
                for i in range(16):
                    b = i % 2
                    if i >= 2:
                        gp.wait_ge(TE.sem, te_chunk[i - 2])
                    for p in range(4):
                        dmaG(xtok[b][:, p * D:(p + 1) * D],
                             x_full[i * 512 + p * 128: i * 512 + (p + 1) * 128, :], par=b)
                    te.wait_ge(LG[b].sem, LG[b].n)
                    if i >= 2:
                        te.wait_ge(LS[b].sem, ls_chunk[i - 2])
                    emit_tgroups(
                        128, 4, psT,
                        lambda k, b=b: xtok[b][:, (k % 4) * D + (k // 4) * 128: (k % 4) * D + (k // 4) * 128 + 128],
                        lambda jg, gi, b=b: stg[b][:, jg * 512: jg * 512 + gi * 128].rearrange("p (a q) -> p a q", q=128))
                    te_chunk[i] = TE.n
                    sy.wait_ge(VE.sem, VE.n)
                    for j in range(32):
                        dmaS(xfm[j * 128:(j + 1) * 128, i * 512:(i + 1) * 512],
                             stg[b][:, j * 512:(j + 1) * 512], par=b)
                    ls_chunk[i] = LS[b].n
                    ve_chunk[i] = VE.n
            barrier()
            if _PHASES <= 1:
                return

            with ExitStack() as ph:
                wband = ph.enter_context(nc.sbuf_tensor("wband", [128, 32 * 1024], BF16))
                xfc = [ph.enter_context(nc.sbuf_tensor(f"xfc{b}", [128, 32 * 512], BF16)) for b in range(2)]
                ckv = [ph.enter_context(nc.sbuf_tensor(f"ckv{b}", [128, 512], F32)) for b in range(2)]
                skv = [ph.enter_context(nc.sbuf_tensor(f"skv{b}", [128, 512], F32)) for b in range(2)]
                kb = [ph.enter_context(nc.sbuf_tensor(f"kb{b}", [128, 512], BF16)) for b in range(2)]
                m1 = ph.enter_context(nc.sbuf_tensor("m1t", [128, 512], F32))
                m2 = ph.enter_context(nc.sbuf_tensor("m2t", [128, 512], F32))
                kro = [ph.enter_context(nc.sbuf_tensor(f"kro{b}", [128, 512], BF16)) for b in range(2)]
                psK = [ph.enter_context(nc.psum_tensor(f"psK{b}", [128, 512], F32)) for b in range(2)]
                psP = [ph.enter_context(nc.psum_tensor(f"psP{b}", [128, 512], F32)) for b in range(2)]

                ik = 0
                ev_psK = {}
                ve_rope = {}
                lc_st = {}
                se_kb = {}
                te_chunkb = {}
                ve_chunkb = {}
                for band in range(8):
                    is_k = band < 4
                    colbase = D + band * 1024 if is_k else 2 * D + (band - 4) * 1024
                    gp.wait_ge(TE.sem, TE.n)
                    for j in range(32):
                        dmaG(wband[:, j * 1024:(j + 1) * 1024],
                             qkv_w[j * 128:(j + 1) * 128, colbase:colbase + 1024], par=0)
                    for i in range(16):
                        b = i % 2
                        if (band, i) != (0, 0):
                            key = (band, i - 2) if i >= 2 else (band - 1, 14 + i)
                            if key in te_chunkb:
                                sy.wait_ge(TE.sem, te_chunkb[key])
                            if key in ve_chunkb:
                                sy.wait_ge(VE.sem, ve_chunkb[key])
                        dmaS(xfc[b][:].rearrange("p (j t) -> p j t", t=512),
                             xfm.rearrange("(j p) t -> p j t", p=128)[:, :, i * 512:(i + 1) * 512], par=b)
                        if is_k:
                            dmaS(ckv[b][:], cosk[:, i * 512:(i + 1) * 512], par=b)
                            dmaS(skv[b][:], sink[:, i * 512:(i + 1) * 512], par=b)
                        te.wait_ge(LG[0].sem, LG[0].n)
                        te.wait_ge(LS[b].sem, LS[b].n)
                        if is_k:
                            for oc in range(8):
                                head = band * 8 + oc
                                pb = ik % 2
                                if ik >= 2:
                                    ctr, n = ev_psK[ik - 2]
                                    te.wait_ge(ctr.sem, n)
                                for ds in range(32):
                                    mm = te.matmul(psK[pb][:],
                                                   wband[:, ds * 1024 + oc * 128: ds * 1024 + oc * 128 + 128],
                                                   xfc[b][:, ds * 512:(ds + 1) * 512],
                                                   start=(ds == 0), stop=(ds == 31))
                                t = te_mark(mm)
                                sc_.wait_ge(TE.sem, t)
                                if ik >= 2:
                                    sc_.wait_ge(VE.sem, ve_rope[ik - 2])
                                sc_.activation(out=kb[pb][:], in_=psK[pb][:], func=AF.Identity,
                                               bias=sb_qkvbT[:, 32 + head:33 + head],
                                               scale=1.0).then_inc(SE.sem, 1)
                                SE.n += 1
                                ev_psK[ik] = (SE, SE.n)
                                se_kb[ik] = SE.n
                                te.wait_ge(SE.sem, SE.n)
                                if ik >= 2:
                                    te.wait_ge(VE.sem, ve_rope[ik - 2])
                                t = te_mark(te.matmul(psP[pb][:], sb_perm[:], kb[pb][:],
                                                      start=True, stop=True))
                                ve.wait_ge(TE.sem, t)
                                vstart()
                                ve.tensor_tensor(out=m1[:], in0=kb[pb][:], in1=ckv[b][:],
                                                 op=OP.mult)
                                ve.tensor_tensor(out=m2[:], in0=psP[pb][:], in1=skv[b][:],
                                                 op=OP.mult).then_inc(VE.sem, 1)
                                VE.n += 1
                                ve.wait_ge(VE.sem, VE.n)
                                if ik >= 2:
                                    ve.wait_ge(LC[pb].sem, lc_st[ik - 2])
                                ve.tensor_tensor(out=kro[pb][:], in0=m1[:], in1=m2[:],
                                                 op=OP.add).then_inc(VE.sem, 1)
                                VE.n += 1
                                ve_rope[ik] = VE.n
                                sc_.wait_ge(VE.sem, VE.n)
                                dmaC(kscr[head * 128:(head + 1) * 128, i * 512:(i + 1) * 512],
                                     kro[pb][:], par=pb)
                                lc_st[ik] = LC[pb].n
                                ik += 1
                        else:
                            vs = (band - 4) * 2
                            for sl in range(2):
                                for p in range(4):
                                    pb = ik % 2
                                    if ik >= 2:
                                        ctr, n = ev_psK[ik - 2]
                                        te.wait_ge(ctr.sem, n)
                                    for ds in range(32):
                                        te.matmul(psK[pb][:],
                                                  xfc[b][:, ds * 512 + p * 128: ds * 512 + (p + 1) * 128],
                                                  wband[:, ds * 1024 + sl * 512:(ds * 1024) + (sl + 1) * 512],
                                                  start=(ds == 0), stop=False)
                                    t = te_mark(te.matmul(psK[pb][:], sb_ones[:, 0:128],
                                                          sb_vbrow[:, (vs + sl) * 512:(vs + sl + 1) * 512],
                                                          start=False, stop=True))
                                    sc_.wait_ge(TE.sem, t)
                                    if ik >= 2:
                                        sc_.wait_ge(LC[pb].sem, lc_st[ik - 2])
                                    sc_.activation(out=kro[pb][:], in_=psK[pb][:],
                                                   func=AF.Copy).then_inc(SE.sem, 1)
                                    SE.n += 1
                                    ev_psK[ik] = (SE, SE.n)
                                    sc_.wait_ge(SE.sem, SE.n)
                                    dmaC(vscr[i * 512 + p * 128: i * 512 + (p + 1) * 128,
                                              (vs + sl) * 512:(vs + sl + 1) * 512],
                                         kro[pb][:], par=pb)
                                    lc_st[ik] = LC[pb].n
                                    ik += 1
                        te_chunkb[(band, i)] = TE.n
                        ve_chunkb[(band, i)] = VE.n
            barrier()
            if _PHASES <= 2:
                return

            with ExitStack() as ph:
                xof = ph.enter_context(nc.sbuf_tensor("xof", [128, 32 * NOWN], BF16))
                xot = ph.enter_context(nc.sbuf_tensor("xot", [128, 4 * D], BF16))
                wq = ph.enter_context(nc.sbuf_tensor("wq", [128, 32 * 512], BF16))
                cq = ph.enter_context(nc.sbuf_tensor("cq", [128, NOWN], F32))
                sqt = ph.enter_context(nc.sbuf_tensor("sqt", [128, NOWN], F32))
                qb = [ph.enter_context(nc.sbuf_tensor(f"qb{b}", [128, 512], BF16)) for b in range(2)]
                qro = [ph.enter_context(nc.sbuf_tensor(f"qro{b}", [128, 512], BF16)) for b in range(2)]
                qm1 = ph.enter_context(nc.sbuf_tensor("qm1", [128, 512], F32))
                qm2 = ph.enter_context(nc.sbuf_tensor("qm2", [128, 512], F32))
                psQ = [ph.enter_context(nc.psum_tensor(f"psQ{b}", [128, 512], F32)) for b in range(2)]
                psQP = [ph.enter_context(nc.psum_tensor(f"psQP{b}", [128, 512], F32)) for b in range(2)]
                psXB = [ph.enter_context(nc.psum_tensor(f"psXB{b}", [128, 2048], BF16)) for b in range(2)]
                dmaS(cq[:], cosq[:])
                dmaS(sqt[:], sinq[:])
                ve.wait_ge(LS[0].sem, LS[0].n)
                for i in range(2):
                    if i == 1:
                        gp.wait_ge(TE.sem, TE.n)
                    for p in range(4):
                        dmaG(xot[:, p * D:(p + 1) * D],
                             x_own[i * 512 + p * 128: i * 512 + (p + 1) * 128, :])
                    te.wait_ge(LG[0].sem, LG[0].n)
                    te.wait_ge(LS[0].sem, LS[0].n)
                    def _xof_dst(jg, gi, i=i):
                        j = (2 * jg) // 4
                        p0 = (2 * jg) % 4
                        base = j * NOWN + i * 512 + p0 * 128
                        return xof[:, base: base + gi * 128].rearrange("p (a q) -> p a q", q=128)
                    emit_tgroups(
                        128, 2, psXB,
                        lambda k: xot[:, (k % 4) * D + (k // 4) * 128: (k % 4) * D + (k // 4) * 128 + 128],
                        _xof_dst)
                te.wait_ge(VE.sem, VE.n)
                iq = 0
                ev_psQ, ve_ropeq, lc_q, se_qb = {}, {}, {}, {}
                for quarter in range(8):
                    gp.wait_ge(TE.sem, TE.n)
                    for j in range(32):
                        dmaG(wq[:, j * 512:(j + 1) * 512],
                             qkv_w[j * 128:(j + 1) * 128, quarter * 512:(quarter + 1) * 512])
                    te.wait_ge(LG[0].sem, LG[0].n)
                    for oc in range(4):
                        head = quarter * 4 + oc
                        for qc in range(2):
                            pb = iq % 2
                            if iq >= 2:
                                ctr, n = ev_psQ[iq - 2]
                                te.wait_ge(ctr.sem, n)
                            for ds in range(32):
                                mm = te.matmul(psQ[pb][:],
                                               wq[:, ds * 512 + oc * 128: ds * 512 + oc * 128 + 128],
                                               xof[:, ds * NOWN + qc * 512: ds * NOWN + (qc + 1) * 512],
                                               start=(ds == 0), stop=(ds == 31))
                            t = te_mark(mm)
                            sc_.wait_ge(TE.sem, t)
                            if iq >= 2:
                                sc_.wait_ge(VE.sem, ve_ropeq[iq - 2])
                            sc_.activation(out=qb[pb][:], in_=psQ[pb][:], func=AF.Identity,
                                           bias=sb_qkvbT[:, head:head + 1],
                                           scale=1.0).then_inc(SE.sem, 1)
                            SE.n += 1
                            ev_psQ[iq] = (SE, SE.n)
                            te.wait_ge(SE.sem, SE.n)
                            if iq >= 2:
                                te.wait_ge(VE.sem, ve_ropeq[iq - 2])
                            t = te_mark(te.matmul(psQP[pb][:], sb_perm[:], qb[pb][:],
                                                  start=True, stop=True))
                            ve.wait_ge(TE.sem, t)
                            vstart()
                            ve.tensor_tensor(out=qm1[:], in0=qb[pb][:],
                                             in1=cq[:, qc * 512:(qc + 1) * 512], op=OP.mult)
                            ve.tensor_tensor(out=qm2[:], in0=psQP[pb][:],
                                             in1=sqt[:, qc * 512:(qc + 1) * 512],
                                             op=OP.mult).then_inc(VE.sem, 1)
                            VE.n += 1
                            ve.wait_ge(VE.sem, VE.n)
                            if iq >= 2:
                                ve.wait_ge(LC[pb].sem, lc_q[iq - 2])
                            ve.tensor_tensor(out=qro[pb][:], in0=qm1[:], in1=qm2[:],
                                             op=OP.add).then_inc(VE.sem, 1)
                            VE.n += 1
                            ve_ropeq[iq] = VE.n
                            sc_.wait_ge(VE.sem, VE.n)
                            dmaC(qscr[head * 128:(head + 1) * 128, qc * 512:(qc + 1) * 512],
                                 qro[pb][:], par=pb)
                            lc_q[iq] = LC[pb].n
                            iq += 1
            barrier()
            if _PHASES <= 3:
                return

            with ExitStack() as ph:
                masks = ph.enter_context(nc.sbuf_tensor("masks", [128, 8 * 1024], BF16))
                kh = [ph.enter_context(nc.sbuf_tensor(f"kh{b}", [128, S], BF16)) for b in range(2)]
                vh = [ph.enter_context(nc.sbuf_tensor(f"vh{b}", [128, 64 * 128], BF16)) for b in range(2)]
                qrow = [ph.enter_context(nc.sbuf_tensor(f"qrow{b}", [128, NOWN], BF16)) for b in range(2)]
                scf = ph.enter_context(nc.sbuf_tensor("scf", [128, S], F32))
                prb = [ph.enter_context(nc.sbuf_tensor(f"prb{b}", [128, S], BF16)) for b in range(2)]
                ptr = ph.enter_context(nc.sbuf_tensor("ptr", [128, S], BF16))
                ctxo = ph.enter_context(nc.sbuf_tensor("ctxo", [128, 128], BF16))
                nmax = ph.enter_context(nc.sbuf_tensor("nmax", [128, 2], F32))
                ssum = ph.enter_context(nc.sbuf_tensor("ssum", [128, 2], F32))
                rsum = ph.enter_context(nc.sbuf_tensor("rsum", [128, 2], F32))
                psSc = [ph.enter_context(nc.psum_tensor(f"psSc{b}", [128, 512], F32)) for b in range(2)]
                psPT = [ph.enter_context(nc.psum_tensor(f"psPT{b}", [128, 2048], BF16)) for b in range(2)]
                psCtx = ph.enter_context(nc.psum_tensor("psCtx", [128, 128], F32))

                dmaS(masks[:].rearrange("p (j t) -> p j t", t=1024),
                     maskin.transpose([1, 0, 2]))

                def load_head(h, b):
                    dmaS(kh[b][:], kscr[h * 128:(h + 1) * 128, :], par=b)
                    dmaS(vh[b][:].rearrange("p (m d) -> p m d", d=128),
                         vscr.rearrange("(m p) (hh d) -> p m hh d", p=128, d=128)[:, :, h, :], par=b)
                    dmaS(qrow[b][:], qscr[h * 128:(h + 1) * 128, :], par=b)
                    return LS[b].n

                hd_ls = {0: load_head(0, 0)}
                te_head = {}
                se_ctx_prev = 0
                te_prb = {}
                ev_sc = {}
                isc = 0
                ve_resc = {}
                se_evacs = {}
                pt_state = {0: 0, 1: 0}

                def stage_scores(h, b, j):
                    nonlocal isc
                    E = 1024 * (j + 1)
                    nch = E // 512
                    sc_.wait_ge(SE.sem, SE.n)
                    for c in range(nch):
                        pb = isc % 2
                        if isc >= 2:
                            ctr, n = ev_sc[isc - 2]
                            te.wait_ge(ctr.sem, n)
                        t = te_mark(te.matmul(psSc[pb][:], qrow[b][:, j * 128:(j + 1) * 128],
                                              kh[b][:, c * 512:(c + 1) * 512],
                                              start=True, stop=True))
                        if c < nch - 2:
                            sc_.wait_ge(TE.sem, t)
                            sc_.activation(out=scf[:, c * 512:(c + 1) * 512],
                                           in_=psSc[pb][:], func=AF.Copy,
                                           scale=SCALE).then_inc(SE.sem, 1)
                            SE.n += 1
                            ev_sc[isc] = (SE, SE.n)
                        else:
                            mi = c - (nch - 2)
                            ve.wait_ge(TE.sem, t)
                            if j >= 1:
                                pass
                            vstart()
                            ve.scalar_tensor_tensor(
                                out=scf[:, c * 512:(c + 1) * 512], in0=psSc[pb][:],
                                scalar=SCALE,
                                in1=masks[:, j * 1024 + mi * 512: j * 1024 + (mi + 1) * 512],
                                op0=OP.mult, op1=OP.add).then_inc(VE.sem, 1)
                            VE.n += 1
                            ev_sc[isc] = (VE, VE.n)
                        isc += 1
                    se_evacs[j] = SE.n

                def stage_softmax(h, j):
                    E = 1024 * (j + 1)
                    sl = j % 2
                    ve.wait_ge(SE.sem, se_evacs[j])
                    vstart()
                    ve.reduce_max(out=nmax[:, sl:sl + 1], in_=scf[:, 0:E], axis=AX,
                                  negate=True).then_inc(VE.sem, 1)
                    VE.n += 1
                    nm_ve = VE.n
                    sc_.wait_ge(VE.sem, nm_ve)
                    sc_.wait_ge(TE.sem, te_prb.get(h * 8 + j - 2, 0))
                    sc_.activation(out=prb[sl][:, 0:E], in_=scf[:, 0:E], func=AF.Exp,
                                   bias=nmax[:, sl:sl + 1], scale=1.0,
                                   accum_out=ssum[:, sl:sl + 1]).then_inc(SE.sem, 1)
                    SE.n += 1
                    ve.wait_ge(SE.sem, SE.n)
                    vstart()
                    ve.reciprocal(out=rsum[:, sl:sl + 1], in_=ssum[:, sl:sl + 1]).then_inc(VE.sem, 1)
                    VE.n += 1
                    ve_resc[j] = (SE.n, VE.n)

                def stage_ctx(h, b, j):
                    nonlocal se_ctx_prev
                    E = 1024 * (j + 1)
                    sl = j % 2
                    nkt = E // 128
                    te.wait_ge(SE.sem, ve_resc[j][0])
                    ngr = nkt // 2
                    for jg in range(ngr):
                        bank = jg % 2
                        if pt_state[bank]:
                            te.wait_ge(VE.sem, pt_state[bank])
                        ti = None
                        for jj in range(2):
                            kt = jg * 2 + jj
                            ti = te.transpose(psPT[bank][:, jj * 1024: jj * 1024 + 128],
                                              prb[sl][:, kt * 128:(kt + 1) * 128], sb_id[:])
                        t = te_mark(ti)
                        ve.wait_ge(TE.sem, t)
                        vstart()
                        ve.tensor_copy(
                            out=ptr[:, jg * 256: jg * 256 + 256].rearrange("p (a q) -> p a q", q=128),
                            in_=psPT[bank][:].rearrange("p (a q) -> p a q", q=1024)[:, 0:2, 0:128],
                        ).then_inc(VE.sem, 1)
                        VE.n += 1
                        pt_state[bank] = VE.n
                    te_prb[h * 8 + j] = TE.n
                    te.wait_ge(VE.sem, VE.n)
                    te.wait_ge(VE.sem, se_ctx_prev)
                    mm = None
                    for kt in range(nkt):
                        mm = te.matmul(psCtx[:], ptr[:, kt * 128:(kt + 1) * 128],
                                       vh[b][:, kt * 128:(kt + 1) * 128],
                                       start=(kt == 0), stop=(kt == nkt - 1))
                    t = te_mark(mm)
                    ve.wait_ge(TE.sem, t)
                    ve.wait_ge(VE.sem, ve_resc[j][1])
                    ve.wait_ge(LC[0].sem, LC[0].n)
                    ve.tensor_scalar(out=ctxo[:], in0=psCtx[:],
                                     scalar1=rsum[:, sl:sl + 1], scalar2=None,
                                     op0=OP.mult).then_inc(VE.sem, 1)
                    VE.n += 1
                    se_ctx_prev = VE.n
                    sc_.wait_ge(VE.sem, VE.n)
                    dmaC(ctxscr[j * 128:(j + 1) * 128, h * 128:(h + 1) * 128], ctxo[:])

                for h in range(H):
                    b = h % 2
                    if h + 1 < H:
                        if h >= 1:
                            sy.wait_ge(TE.sem, te_head[h - 1])
                        hd_ls[h + 1] = load_head(h + 1, (h + 1) % 2)
                    te.wait_ge(LS[b].sem, hd_ls[h])
                    for j in range(8):
                        stage_scores(h, b, j)
                        stage_softmax(h, j)
                        if j >= 1:
                            stage_ctx(h, b, j - 1)
                    stage_ctx(h, b, 7)
                    te_head[h] = TE.n
            barrier()

            with ExitStack() as ph:
                cfm_all = ph.enter_context(nc.sbuf_tensor("cfm_all", [128, 8 * 4096], BF16))
                wo = ph.enter_context(nc.sbuf_tensor("wo", [128, 32 * 512], BF16))
                ctt = [ph.enter_context(nc.sbuf_tensor(f"ctt{b}", [128, D], BF16)) for b in range(2)]
                xos = [ph.enter_context(nc.sbuf_tensor(f"xos{b}", [128, 512], F32)) for b in range(2)]
                r1o = [ph.enter_context(nc.sbuf_tensor(f"r1o{b}", [128, 512], F32)) for b in range(2)]
                psO = [ph.enter_context(nc.psum_tensor(f"psO{b}", [128, 512], F32)) for b in range(2)]
                psCT = [ph.enter_context(nc.psum_tensor(f"psCT{b}", [128, 2048], BF16)) for b in range(2)]
                te_ct = {}
                for tt in range(8):
                    b = tt % 2
                    if tt >= 2:
                        sy.wait_ge(TE.sem, te_ct[tt - 2])
                    dmaS(ctt[b][:], ctxscr[tt * 128:(tt + 1) * 128, :], par=b)
                    te.wait_ge(LS[b].sem, LS[b].n)
                    emit_tgroups(
                        32, 2, psCT,
                        lambda k, b=b: ctt[b][:, k * 128:(k + 1) * 128],
                        lambda jg, gi, tt=tt: cfm_all[:, tt * 4096 + jg * 256: tt * 4096 + jg * 256 + gi * 128].rearrange("p (a q) -> p a q", q=128))
                    te_ct[tt] = TE.n
                te.wait_ge(VE.sem, VE.n)
                sy.wait_ge(LS[0].sem, LS[0].n)
                sy.wait_ge(LS[1].sem, LS[1].n)
                io = 0
                ve_o, lc_o = {}, {}
                for os_ in range(8):
                    gp.wait_ge(TE.sem, TE.n)
                    for j in range(32):
                        dmaG(wo[:, j * 512:(j + 1) * 512],
                             out_w[j * 128:(j + 1) * 128, os_ * 512:(os_ + 1) * 512], par=0)
                    te.wait_ge(LG[0].sem, LG[0].n)
                    for tt in range(8):
                        pb = io % 2
                        if io >= 2:
                            sy.wait_ge(VE.sem, ve_o[io - 2])
                        dmaS(xos[pb][:],
                             x_own[tt * 128:(tt + 1) * 128, os_ * 512:(os_ + 1) * 512],
                             par=pb)
                        if io >= 2:
                            te.wait_ge(VE.sem, ve_o[io - 2])
                        for ds in range(32):
                            te.matmul(psO[pb][:],
                                      cfm_all[:, tt * 4096 + ds * 128:(tt * 4096) + (ds + 1) * 128],
                                      wo[:, ds * 512:(ds + 1) * 512],
                                      start=(ds == 0), stop=False)
                        t = te_mark(te.matmul(psO[pb][:], sb_ones[:, 0:128],
                                              sb_obrow[:, os_ * 512:(os_ + 1) * 512],
                                              start=False, stop=True))
                        ve.wait_ge(TE.sem, t)
                        ve.wait_ge(LS[pb].sem, LS[pb].n)
                        vstart()
                        if io >= 2:
                            ve.wait_ge(LC[pb].sem, lc_o[io - 2])
                        ve.tensor_tensor(out=r1o[pb][:], in0=psO[pb][:], in1=xos[pb][:],
                                         op=OP.add).then_inc(VE.sem, 1)
                        VE.n += 1
                        ve_o[io] = VE.n
                        sc_.wait_ge(VE.sem, VE.n)
                        dmaC(r1scr[tt * 128:(tt + 1) * 128, os_ * 512:(os_ + 1) * 512],
                             r1o[pb][:], par=pb)
                        lc_o[io] = LC[pb].n
                        io += 1
            barrier()
            if _PHASES <= 4:
                return

            x2fm_es = ExitStack()
            x2fm = x2fm_es.enter_context(nc.sbuf_tensor("x2fm", [128, 32 * NOWN], BF16))

            def ln_rows(src, dst, ph):
                rrow, xc, sqr, x2r = ph["rrow"], ph["xc"], ph["sqr"], ph["x2r"]
                st1, st2 = ph["st1"], ph["st2"]
                sy.wait_ge(VE.sem, VE.n)
                dmaS(rrow[:], src)
                ve.wait_ge(LS[0].sem, LS[0].n)
                vstart()
                ve.reduce_sum(out=st1[:], in_=rrow[:], axis=AX).then_inc(VE.sem, 1)
                VE.n += 1
                ve.wait_ge(VE.sem, VE.n)
                ve.tensor_scalar(out=st1[:], in0=st1[:], scalar1=1.0 / D, scalar2=None,
                                 op0=OP.mult).then_inc(VE.sem, 1)
                VE.n += 1
                ve.wait_ge(VE.sem, VE.n)
                ve.tensor_scalar(out=xc[:], in0=rrow[:], scalar1=st1[:, 0:1], scalar2=None,
                                 op0=OP.subtract).then_inc(VE.sem, 1)
                VE.n += 1
                sc_.wait_ge(VE.sem, VE.n)
                sc_.activation(out=sqr[:], in_=xc[:], func=AF.Square).then_inc(SE.sem, 1)
                SE.n += 1
                ve.wait_ge(SE.sem, SE.n)
                vstart()
                ve.reduce_sum(out=st2[:], in_=sqr[:], axis=AX).then_inc(VE.sem, 1)
                VE.n += 1
                ve.wait_ge(VE.sem, VE.n)
                ve.tensor_scalar(out=st2[:], in0=st2[:], scalar1=1.0 / D, scalar2=float(EPS),
                                 op0=OP.mult, op1=OP.add).then_inc(VE.sem, 1)
                VE.n += 1
                sc_.wait_ge(VE.sem, VE.n)
                sc_.activation(out=st2[:], in_=st2[:], func=AF.Sqrt).then_inc(SE.sem, 1)
                SE.n += 1
                ve.wait_ge(SE.sem, SE.n)
                vstart()
                ve.reciprocal(out=st2[:], in_=st2[:]).then_inc(VE.sem, 1)
                VE.n += 1
                ve.wait_ge(VE.sem, VE.n)
                ve.wait_ge(LC[0].sem, LC[0].n)
                ve.tensor_scalar(out=x2r[:], in0=xc[:], scalar1=st2[:, 0:1], scalar2=None,
                                 op0=OP.mult).then_inc(VE.sem, 1)
                VE.n += 1
                sc_.wait_ge(VE.sem, VE.n)
                if dst is not None:
                    dmaC(dst, x2r[:])

            with ExitStack() as ph0:
                ph = {k: ph0.enter_context(nc.sbuf_tensor(k, [128, D], F32))
                      for k in ("rrow", "xc", "sqr", "x2r")}
                ph["st1"] = ph0.enter_context(nc.sbuf_tensor("st1", [128, 1], F32))
                ph["st2"] = ph0.enter_context(nc.sbuf_tensor("st2", [128, 1], F32))
                x2b = ph0.enter_context(nc.sbuf_tensor("x2b", [128, D], BF16))
                psX2 = [ph0.enter_context(nc.psum_tensor(f"psX2{b}", [128, 4096], BF16)) for b in range(2)]
                for tt in range(8):
                    ln_rows(r1scr[tt * 128:(tt + 1) * 128, :],
                            x2scr[tt * 128:(tt + 1) * 128, :], ph)
                    sc_.wait_ge(TE.sem, TE.n)
                    sc_.activation(out=x2b[:], in_=ph["x2r"][:], func=AF.Copy).then_inc(SE.sem, 1)
                    SE.n += 1
                    te.wait_ge(SE.sem, SE.n)
                    x2v = x2fm[:].rearrange("p (j t) -> p j t", t=NOWN)
                    emit_tgroups(
                        32, 4, psX2,
                        lambda k: x2b[:, k * 128:(k + 1) * 128],
                        lambda jg, gi, tt=tt: x2v[:, jg * 4: jg * 4 + gi, tt * 128:(tt + 1) * 128])
            barrier()
            if _PHASES <= 6:
                return

            with ExitStack() as ph:
                pk1 = [ph.enter_context(nc.sbuf_tensor(f"pk1{b}", [128, 2048], I32)) for b in range(2)]
                pk3 = [ph.enter_context(nc.sbuf_tensor(f"pk3{b}", [128, 2048], I32)) for b in range(2)]
                sc1 = [ph.enter_context(nc.sbuf_tensor(f"sc1{b}", [128, 64], F32)) for b in range(2)]
                sc3 = [ph.enter_context(nc.sbuf_tensor(f"sc3{b}", [128, 64], F32)) for b in range(2)]
                msb = ph.enter_context(nc.sbuf_tensor("msb", [128, 2048], I32))
                lsb = ph.enter_context(nc.sbuf_tensor("lsb", [128, 2048], I32))
                nat = [ph.enter_context(nc.sbuf_tensor(f"nat{b}", [128, D], BF16)) for b in range(2)]
                w1t = [ph.enter_context(nc.sbuf_tensor(f"w1t{b}", [128, D], BF16)) for b in range(2)]
                w3t = [ph.enter_context(nc.sbuf_tensor(f"w3t{b}", [128, D], BF16)) for b in range(2)]
                sil = [ph.enter_context(nc.sbuf_tensor(f"sil{b}", [128, 512], F32)) for b in range(2)]
                hao = [ph.enter_context(nc.sbuf_tensor(f"hao{b}", [128, 512], BF16)) for b in range(2)]
                psA = [ph.enter_context(nc.psum_tensor(f"psA{b}", [128, 512], F32)) for b in range(2)]
                psB = [ph.enter_context(nc.psum_tensor(f"psB{b}", [128, 512], F32)) for b in range(2)]
                psW = [ph.enter_context(nc.psum_tensor(f"psW{b}", [128, 2048], BF16)) for b in range(2)]

                w1r = w1p.rearrange("(h r) b -> h (r b)", r=32)
                w3r = w3p.rearrange("(h r) b -> h (r b)", r=32)
                s1r = s1v.rearrange("(h g) -> h g", g=64)
                s3r = s3v.rearrange("(h g) -> h g", g=64)

                def deq_tile(pk, scv, outT, slot, par):
                    ve.wait_ge(LS[par].sem, LS[par].n)
                    vstart()
                    ve.tensor_scalar(out=msb[:], in0=pk[:], scalar1=4, scalar2=None,
                                     op0=OP.arith_shift_right)
                    ve.tensor_scalar(out=lsb[:], in0=pk[:], scalar1=28, scalar2=28,
                                     op0=OP.arith_shift_left,
                                     op1=OP.arith_shift_right).then_inc(VE.sem, 1)
                    VE.n += 1
                    ve.wait_ge(VE.sem, VE.n)
                    nb = nat[slot]
                    natv = nb[:].rearrange("p (m z c) -> p z m c", z=2, c=64)
                    mv = msb[:].rearrange("p (m c) -> p m c", c=64)
                    lv = lsb[:].rearrange("p (m c) -> p m c", c=64)
                    sev = scv[:].rearrange("p (m z) -> p z m", z=2)[:, 0].unsqueeze(2).broadcast_to([128, 32, 64])
                    sov = scv[:].rearrange("p (m z) -> p z m", z=2)[:, 1].unsqueeze(2).broadcast_to([128, 32, 64])
                    ve.tensor_tensor(out=natv[:, 0], in0=mv, in1=sev, op=OP.mult)
                    ve.tensor_tensor(out=natv[:, 1], in0=lv, in1=sov, op=OP.mult).then_inc(VE.sem, 1)
                    VE.n += 1
                    te.wait_ge(VE.sem, VE.n)
                    emit_tgroups(
                        32, 2, psW,
                        lambda k, nb=nb: nb[:, k * 128:(k + 1) * 128],
                        lambda jg, gi, outT=outT: outT[:, jg * 256: jg * 256 + gi * 128].rearrange("p (a q) -> p a q", q=128),
                        evac="se")

                te_ht, se_ab, ve_ab, lc_ab = {}, {}, {}, {}
                for ht in range(NHT):
                    b = ht % 2
                    if ht >= 2:
                        sy.wait_ge(VE.sem, ve_ab.get(2 * (ht - 2) + 1, 0))
                    dmaS(pk1[b][:], w1r[ht * 128:(ht + 1) * 128, :], par=b)
                    dmaS(sc1[b][:], s1r[ht * 128:(ht + 1) * 128, :], par=b)
                    dmaS(pk3[b][:], w3r[ht * 128:(ht + 1) * 128, :], par=b)
                    dmaS(sc3[b][:], s3r[ht * 128:(ht + 1) * 128, :], par=b)
                    if ht >= 2:
                        ve.wait_ge(TE.sem, te_ht[ht - 2])
                    deq_tile(pk1[b], sc1[b], w1t[b], 0, b)
                    deq_tile(pk3[b], sc3[b], w3t[b], 1, b)
                    te.wait_ge(VE.sem, VE.n)
                    te.wait_ge(SE.sem, SE.n)
                    for c in range(2):
                        it = ht * 2 + c
                        pb = c
                        if it >= 2:
                            te.wait_ge(SE.sem, se_ab[it - 2])
                            te.wait_ge(VE.sem, ve_ab[it - 2])
                        for ds in range(32):
                            te.matmul(psA[pb][:], w1t[b][:, ds * 128:(ds + 1) * 128],
                                      x2fm[:, ds * NOWN + c * 512: ds * NOWN + (c + 1) * 512],
                                      start=(ds == 0), stop=(ds == 31))
                        for ds in range(32):
                            mm = te.matmul(psB[pb][:], w3t[b][:, ds * 128:(ds + 1) * 128],
                                           x2fm[:, ds * NOWN + c * 512: ds * NOWN + (c + 1) * 512],
                                           start=(ds == 0), stop=(ds == 31))
                        t = te_mark(mm)
                        sc_.wait_ge(TE.sem, t)
                        if it >= 2:
                            sc_.wait_ge(VE.sem, ve_ab[it - 2])
                        sc_.activation(out=sil[pb][:], in_=psA[pb][:],
                                       func=AF.Silu).then_inc(SE.sem, 1)
                        SE.n += 1
                        se_ab[it] = SE.n
                        ve.wait_ge(SE.sem, SE.n)
                        vstart()
                        if it >= 2:
                            ve.wait_ge(LC[pb].sem, lc_ab[it - 2])
                        ve.tensor_tensor(out=hao[pb][:], in0=sil[pb][:], in1=psB[pb][:],
                                         op=OP.mult).then_inc(VE.sem, 1)
                        VE.n += 1
                        ve_ab[it] = VE.n
                        sc_.wait_ge(VE.sem, VE.n)
                        dmaC(hactscr[ht * 128:(ht + 1) * 128, c * 512:(c + 1) * 512],
                             hao[pb][:], par=pb)
                        lc_ab[it] = LC[pb].n
                    te_ht[ht] = TE.n
            barrier()
            x2fm_es.close()
            if _PHASES <= 7:
                return

            with ExitStack() as ph:
                natS = ph.enter_context(nc.sbuf_tensor("natS", [128, 32 * 1024], BF16))
                pk2 = [ph.enter_context(nc.sbuf_tensor(f"pk2{b}", [128, 512], I32)) for b in range(2)]
                sc2 = [ph.enter_context(nc.sbuf_tensor(f"sc2{b}", [128, 172], F32)) for b in range(2)]
                msb2 = ph.enter_context(nc.sbuf_tensor("msb2", [128, 512], I32))
                lsb2 = ph.enter_context(nc.sbuf_tensor("lsb2", [128, 512], I32))
                w2st = ph.enter_context(nc.sbuf_tensor("w2st", [128, D], BF16))
                psW2 = [ph.enter_context(nc.psum_tensor(f"psW2{b}", [128, 4096], BF16)) for b in range(2)]
                w2r = w2p.rearrange("(d r) b -> d (r b)", r=86)
                s2r = s2v.rearrange("(d g) -> d g", g=172)
                ve_w2 = {}
                te_sup = {}
                for sup in range(11):
                    HS = 1024 if sup < 10 else 768
                    NM = HS // 128
                    for dt in range(32):
                        it = sup * 32 + dt
                        b = it % 2
                        if it >= 2:
                            sy.wait_ge(VE.sem, ve_w2[it - 2])
                        dmaS(pk2[b][:, 0:NM * 64],
                             w2r[dt * 128:(dt + 1) * 128, sup * 512: sup * 512 + NM * 64], par=b)
                        dmaS(sc2[b][:], s2r[dt * 128:(dt + 1) * 128, :], par=b)
                        ve.wait_ge(LS[b].sem, LS[b].n)
                        if dt == 0 and sup >= 1:
                            ve.wait_ge(TE.sem, te_sup[sup - 1])
                        vstart()
                        ve.tensor_scalar(out=msb2[:, 0:NM * 64], in0=pk2[b][:, 0:NM * 64],
                                         scalar1=4, scalar2=None,
                                         op0=OP.arith_shift_right)
                        ve.tensor_scalar(out=lsb2[:, 0:NM * 64], in0=pk2[b][:, 0:NM * 64],
                                         scalar1=28, scalar2=28,
                                         op0=OP.arith_shift_left,
                                         op1=OP.arith_shift_right).then_inc(VE.sem, 1)
                        VE.n += 1
                        ve.wait_ge(VE.sem, VE.n)
                        natv = natS[:, dt * 1024: dt * 1024 + HS].rearrange(
                            "p (m z c) -> p z m c", z=2, c=64)
                        mv = msb2[:, 0:NM * 64].rearrange("p (m c) -> p m c", c=64)
                        lv = lsb2[:, 0:NM * 64].rearrange("p (m c) -> p m c", c=64)
                        sall = sc2[b][:, 16 * sup: 16 * sup + 2 * NM]
                        sev = sall.rearrange("p (m z) -> p z m", z=2)[:, 0].unsqueeze(2).broadcast_to([128, NM, 64])
                        sov = sall.rearrange("p (m z) -> p z m", z=2)[:, 1].unsqueeze(2).broadcast_to([128, NM, 64])
                        ve.tensor_tensor(out=natv[:, 0], in0=mv, in1=sev, op=OP.mult)
                        ve.tensor_tensor(out=natv[:, 1], in0=lv, in1=sov, op=OP.mult).then_inc(VE.sem, 1)
                        VE.n += 1
                        ve_w2[it] = VE.n
                    te.wait_ge(VE.sem, VE.n)
                    for hb in range(NM):
                        sc_.wait_ge(LC[0].sem, LC[0].n)
                        emit_tgroups(
                            32, 4, psW2,
                            lambda k, hb=hb: natS[:, k * 1024 + hb * 128: k * 1024 + (hb + 1) * 128],
                            lambda jg, gi: w2st[:, jg * 512: jg * 512 + gi * 128].rearrange("p (a q) -> p a q", q=128),
                            evac="se")
                        sc_.wait_ge(SE.sem, SE.n)
                        dmaC(w2tscr[sup * 1024 + hb * 128: sup * 1024 + (hb + 1) * 128, :],
                             w2st[:])
                    te_sup[sup] = TE.n
            barrier()
            if _PHASES <= 8:
                return

            with ExitStack() as ph:
                hact_sb = ph.enter_context(nc.sbuf_tensor("hact_sb", [128, NHT * NOWN], BF16))
                w2b = [ph.enter_context(nc.sbuf_tensor(f"w2b{b}", [128, 512], BF16)) for b in range(2)]
                x2s = ph.enter_context(nc.sbuf_tensor("x2s", [128, 512], F32))
                r2o = ph.enter_context(nc.sbuf_tensor("r2o", [128, 512], F32))
                psF = [ph.enter_context(nc.psum_tensor(f"psF{t}", [128, 512], F32)) for t in range(8)]
                for hs in range(NHT):
                    dmaG(hact_sb[:, hs * NOWN:(hs + 1) * NOWN],
                         hactscr[hs * 128:(hs + 1) * 128, :], par=1)
                te.wait_ge(LG[1].sem, LG[1].n)
                te_f = {}
                ve_ds = 0
                for ds in range(8):
                    for hs in range(NHT):
                        it = ds * NHT + hs
                        b = it % 2
                        if it >= 2:
                            sy.wait_ge(TE.sem, te_f[it - 2])
                        dmaS(w2b[b][:], w2tscr[hs * 128:(hs + 1) * 128, ds * 512:(ds + 1) * 512], par=b)
                        te.wait_ge(LS[b].sem, LS[b].n)
                        if hs == 0:
                            te.wait_ge(VE.sem, ve_ds)
                        for tt in range(8):
                            mm = te.matmul(psF[tt][:],
                                           hact_sb[:, hs * NOWN + tt * 128: hs * NOWN + (tt + 1) * 128],
                                           w2b[b][:], start=(hs == 0), stop=(hs == NHT - 1))
                        te_f[it] = te_mark(mm)
                    for tt in range(8):
                        gp.wait_ge(VE.sem, VE.n)
                        dmaG(x2s[:], x2scr[tt * 128:(tt + 1) * 128, ds * 512:(ds + 1) * 512])
                        ve.wait_ge(TE.sem, TE.n)
                        ve.wait_ge(LG[0].sem, LG[0].n)
                        vstart()
                        ve.wait_ge(LC[0].sem, LC[0].n)
                        ve.tensor_tensor(out=r2o[:], in0=psF[tt][:], in1=x2s[:],
                                         op=OP.add).then_inc(VE.sem, 1)
                        VE.n += 1
                        sc_.wait_ge(VE.sem, VE.n)
                        dmaC(r2scr[tt * 128:(tt + 1) * 128, ds * 512:(ds + 1) * 512],
                             r2o[:])
                    ve_ds = VE.n
            barrier()
            if _PHASES <= 9:
                return

            with ExitStack() as ph0:
                ph = {k: ph0.enter_context(nc.sbuf_tensor(k + "b", [128, D], F32))
                      for k in ("rrow", "xc", "sqr", "x2r")}
                ph["st1"] = ph0.enter_context(nc.sbuf_tensor("st1b", [128, 1], F32))
                ph["st2"] = ph0.enter_context(nc.sbuf_tensor("st2b", [128, 1], F32))
                for tt in range(8):
                    ln_rows(r2scr[tt * 128:(tt + 1) * 128, :],
                            out_own[tt * 128:(tt + 1) * 128, :], ph)
                for c in LC:
                    sy.wait_ge(c.sem, c.n)
                for c in LS:
                    sy.wait_ge(c.sem, c.n)
                for c in LG:
                    sy.wait_ge(c.sem, c.n)

    return nc


def _host_inputs(inputs, core):
    x = np.asarray(inputs["L0_attn_hidden_states"]).reshape(S, D)
    qkv_b = np.asarray(inputs["L0_attn_qkv_b"])
    out_b = np.asarray(inputs["L0_attn_out_b"])

    own = np.concatenate([np.arange(128 * (8 * j + core), 128 * (8 * j + core) + 128)
                          for j in range(8)])
    mask = np.zeros((8, 128, 1024), np.float32)
    for j in range(8):
        q = 128 * (8 * j + core) + np.arange(128)[:, None]
        k = 1024 * j + np.arange(1024)[None, :]
        mask[j] = np.where(k <= q, 0.0, np.float32(NEG))
    inv = 1.0 / (10000.0 ** (np.arange(64, dtype=np.float64) / 64.0))
    pos = np.arange(S, dtype=np.float64)
    ang = pos[None, :] * inv[:, None]
    cosk_ = np.cos(ang).astype(np.float32)
    sink_ = np.sin(ang).astype(np.float32)
    cosk_ = np.concatenate([cosk_, cosk_], axis=0)
    sink_ = np.concatenate([sink_, sink_], axis=0)

    perm = np.zeros((128, 128), np.float32)
    for p in range(64):
        perm[p, p + 64] = -1.0
        perm[p + 64, p] = 1.0

    return {
        "x_full": x,
        "x_own": np.ascontiguousarray(x[own]),
        "qkv_w": np.asarray(inputs["L0_attn_qkv_w"]),
        "qkv_bT": np.ascontiguousarray(qkv_b.reshape(96, 128).T),
        "vb_row": np.ascontiguousarray(qkv_b[8192:].reshape(1, D)),
        "out_w": np.asarray(inputs["L0_attn_out_w"]),
        "ob_row": np.ascontiguousarray(out_b.reshape(1, D)),
        "w1p": np.asarray(inputs["L0_ffn_w1"]),
        "s1v": np.asarray(inputs["L0_ffn_s1"]),
        "w2p": np.asarray(inputs["L0_ffn_w2"]),
        "s2v": np.asarray(inputs["L0_ffn_s2"]),
        "w3p": np.asarray(inputs["L0_ffn_w3"]),
        "s3v": np.asarray(inputs["L0_ffn_s3"]),
        "maskin": mask.astype(ml_dtypes.bfloat16),
        "cosk": np.ascontiguousarray(cosk_),
        "sink": np.ascontiguousarray(sink_),
        "cosq": np.ascontiguousarray(cosk_[:, own]),
        "sinq": np.ascontiguousarray(sink_[:, own]),
        "identb": np.eye(128).astype(ml_dtypes.bfloat16),
        "permT": np.ascontiguousarray(perm.T).astype(ml_dtypes.bfloat16),
    }, own


def kernel(L0_attn_hidden_states, L0_attn_qkv_w, L0_attn_qkv_b, L0_attn_out_w,
           L0_attn_out_b, L0_ffn_w1, L0_ffn_s1, L0_ffn_w2, L0_ffn_s2,
           L0_ffn_w3, L0_ffn_s3):
    inputs = dict(
        L0_attn_hidden_states=L0_attn_hidden_states,
        L0_attn_qkv_w=L0_attn_qkv_w, L0_attn_qkv_b=L0_attn_qkv_b,
        L0_attn_out_w=L0_attn_out_w, L0_attn_out_b=L0_attn_out_b,
        L0_ffn_w1=L0_ffn_w1, L0_ffn_s1=L0_ffn_s1,
        L0_ffn_w2=L0_ffn_w2, L0_ffn_s2=L0_ffn_s2,
        L0_ffn_w3=L0_ffn_w3, L0_ffn_s3=L0_ffn_s3,
    )
    nc = _build()
    in_maps, owns = [], []
    for c in range(8):
        m, own = _host_inputs(inputs, c)
        in_maps.append(m)
        owns.append(own)
    res = run_bass_kernel_spmd(nc, in_maps, core_ids=list(range(8)))
    out = np.empty((1, S, D), np.float32)
    for c in range(8):
        out[0, owns[c]] = res.results[c]["out_own"]
    return out


# version 04 (final)
# speedup vs baseline: 1.3096x; 1.0127x over previous
import numpy as np
import ml_dtypes
import concourse.bass as bass
import concourse.mybir as mybir
from concourse.bass_utils import run_bass_kernel_spmd
from concourse.alu_op_type import AluOpType
from contextlib import ExitStack

F32 = mybir.dt.float32
BF16 = mybir.dt.bfloat16
I32 = mybir.dt.int32
AF = mybir.ActivationFunctionType
OP = AluOpType
AX = mybir.AxisListType.X

S = 8192
D = 4096
H = 32
NOWN = 1024
HID = 11008
NHT = HID // 128
SCALE = float(1.0 / np.sqrt(np.float32(128.0)))
EPS = 1e-5
NEG = -66504.0

_DBG_KIND = {}
_PHASES = 99


class Ctr:
    def __init__(self, sem):
        self.sem = sem
        self.n = 0


def _build():
    nc = bass.Bass("TRN2", target_bir_lowering=False)
    ap = lambda t: t.ap()

    x_full = ap(nc.dram_tensor("x_full", [S, D], F32, kind="ExternalInput"))
    x_own = ap(nc.dram_tensor("x_own", [NOWN, D], F32, kind="ExternalInput"))
    qkv_w = ap(nc.dram_tensor("qkv_w", [D, 3 * D], F32, kind="ExternalInput"))
    qkv_bT = ap(nc.dram_tensor("qkv_bT", [128, 96], F32, kind="ExternalInput"))
    vb_row = ap(nc.dram_tensor("vb_row", [1, D], F32, kind="ExternalInput"))
    out_w = ap(nc.dram_tensor("out_w", [D, D], F32, kind="ExternalInput"))
    ob_row = ap(nc.dram_tensor("ob_row", [1, D], F32, kind="ExternalInput"))
    w1p = ap(nc.dram_tensor("w1p", [352256, 64], I32, kind="ExternalInput"))
    s1v = ap(nc.dram_tensor("s1v", [704512], F32, kind="ExternalInput"))
    w2p = ap(nc.dram_tensor("w2p", [352256, 64], I32, kind="ExternalInput"))
    s2v = ap(nc.dram_tensor("s2v", [704512], F32, kind="ExternalInput"))
    w3p = ap(nc.dram_tensor("w3p", [352256, 64], I32, kind="ExternalInput"))
    s3v = ap(nc.dram_tensor("s3v", [704512], F32, kind="ExternalInput"))
    maskin = ap(nc.dram_tensor("maskin", [8, 128, 1024], BF16, kind="ExternalInput"))
    cosk = ap(nc.dram_tensor("cosk", [128, S], F32, kind="ExternalInput"))
    sink = ap(nc.dram_tensor("sink", [128, S], F32, kind="ExternalInput"))
    cosq = ap(nc.dram_tensor("cosq", [128, NOWN], F32, kind="ExternalInput"))
    sinq = ap(nc.dram_tensor("sinq", [128, NOWN], F32, kind="ExternalInput"))
    identb = ap(nc.dram_tensor("identb", [128, 128], BF16, kind="ExternalInput"))
    permT = ap(nc.dram_tensor("permT", [128, 128], BF16, kind="ExternalInput"))

    out_own = ap(nc.dram_tensor("out_own", [NOWN, D], F32, kind="ExternalOutput"))

    def scr(name, shape, dt):
        return ap(nc.dram_tensor(name, shape, dt, kind=_DBG_KIND.get(name, "Internal")))

    xfm = scr("xfm", [D, S], BF16)
    kscr = scr("kscr", [D, S], BF16)
    vscr = scr("vscr", [S, D], BF16)
    qscr = scr("qscr", [D, NOWN], BF16)
    ctxscr = scr("ctxscr", [NOWN, D], BF16)
    x2scr = scr("x2scr", [NOWN, D], F32)
    r1scr = scr("r1scr", [NOWN, D], F32)
    r2scr = scr("r2scr", [NOWN, D], F32)
    hactscr = scr("hactscr", [HID, NOWN], BF16)
    w2tscr = scr("w2tscr", [HID, D], BF16)

    es = ExitStack()
    with es:
        LS = [Ctr(es.enter_context(nc.semaphore(f"LS{i}"))) for i in range(2)]
        LC = [Ctr(es.enter_context(nc.semaphore(f"LC{i}"))) for i in range(2)]
        LG = [Ctr(es.enter_context(nc.semaphore(f"LG{i}"))) for i in range(2)]
        TE = Ctr(es.enter_context(nc.semaphore("TEs")))
        VE = Ctr(es.enter_context(nc.semaphore("VEs")))
        SE = Ctr(es.enter_context(nc.semaphore("SEs")))
        BAR = Ctr(es.enter_context(nc.semaphore("BARs")))

        sb_qkvbT = es.enter_context(nc.sbuf_tensor("sb_qkvbT", [128, 96], F32))
        sb_vbrow = es.enter_context(nc.sbuf_tensor("sb_vbrow", [1, D], BF16))
        sb_obrow = es.enter_context(nc.sbuf_tensor("sb_obrow", [1, D], BF16))
        sb_id = es.enter_context(nc.sbuf_tensor("sb_id", [128, 128], BF16))
        sb_perm = es.enter_context(nc.sbuf_tensor("sb_perm", [128, 128], BF16))
        sb_ones = es.enter_context(nc.sbuf_tensor("sb_ones", [1, 512], BF16))

        blk = es.enter_context(nc.Block())

        @blk.gpsimd
        def _(_g):
            sy, sc_, gp, ve, te = nc.sync, nc.scalar, nc.gpsimd, nc.vector, nc.tensor

            def dmaS(out, in_, par=0):
                sy.dma_start(out=out, in_=in_).then_inc(LS[par].sem, 16)
                LS[par].n += 16
                return (LS[par], LS[par].n)

            def dmaC(out, in_, par=0):
                sc_.dma_start(out=out, in_=in_).then_inc(LC[par].sem, 16)
                LC[par].n += 16
                return (LC[par], LC[par].n)

            def dmaG(out, in_, par=0):
                gp.dma_start(out=out, in_=in_).then_inc(LG[par].sem, 16)
                LG[par].n += 16
                return (LG[par], LG[par].n)

            def wq_(eng, tok):
                eng.wait_ge(tok[0].sem, tok[1])

            def te_mark(instr):
                instr.then_inc(TE.sem, 1)
                TE.n += 1
                return TE.n

            def vstart():
                ve.wait_ge(VE.sem, VE.n)

            tg_state = {}

            def emit_tgroups(nblk, g, psl, src_fn, dst_fn, evac="ve"):
                st = tg_state.setdefault(psl[0].name, {0: None, 1: None})
                ngr = (nblk + g - 1) // g
                for jg in range(ngr):
                    gi = min(g, nblk - jg * g)
                    bank = jg % 2
                    if st[bank] is not None:
                        ctr, n = st[bank]
                        te.wait_ge(ctr.sem, n)
                    ti = None
                    for jj in range(gi):
                        ti = te.transpose(psl[bank][:, jj * 1024: jj * 1024 + 128],
                                          src_fn(jg * g + jj), sb_id[:])
                    t = te_mark(ti)
                    srcv = psl[bank][:].rearrange("p (a q) -> p a q", q=1024)[:, 0:gi, 0:128]
                    if evac == "se":
                        sc_.wait_ge(TE.sem, t)
                        sc_.activation(out=dst_fn(jg, gi), in_=srcv,
                                       func=AF.Copy).then_inc(SE.sem, 1)
                        SE.n += 1
                        st[bank] = (SE, SE.n)
                    else:
                        ve.wait_ge(TE.sem, t)
                        vstart()
                        ve.tensor_copy(out=dst_fn(jg, gi), in_=srcv).then_inc(VE.sem, 1)
                        VE.n += 1
                        st[bank] = (VE, VE.n)

            def barrier():
                for c in LS:
                    sy.wait_ge(c.sem, c.n)
                for c in LC:
                    sc_.wait_ge(c.sem, c.n)
                for c in LG:
                    gp.wait_ge(c.sem, c.n)
                te.wait_ge(TE.sem, TE.n)
                ve.wait_ge(VE.sem, VE.n)
                sc_.wait_ge(SE.sem, SE.n)
                for e in (te, ve, sc_, gp, sy):
                    e.sem_inc(BAR.sem, 1)
                BAR.n += 5
                for e in (te, ve, sc_, gp, sy):
                    e.wait_ge(BAR.sem, BAR.n)

            with ExitStack() as p0:
                sb_vbf = p0.enter_context(nc.sbuf_tensor("sb_vbf", [1, D], F32))
                sb_obf = p0.enter_context(nc.sbuf_tensor("sb_obf", [1, D], F32))
                dmaS(sb_qkvbT[:], qkv_bT[:])
                dmaS(sb_vbf[:], vb_row[:])
                dmaS(sb_obf[:], ob_row[:])
                dmaS(sb_id[:], identb[:])
                dmaS(sb_perm[:], permT[:])
                ve.wait_ge(LS[0].sem, LS[0].n)
                ve.memset(sb_ones[:], 1.0)
                ve.tensor_copy(out=sb_vbrow[:], in_=sb_vbf[:]).then_inc(VE.sem, 1)
                VE.n += 1
                ve.tensor_copy(out=sb_obrow[:], in_=sb_obf[:]).then_inc(VE.sem, 1)
                VE.n += 1
                barrier()
            if _PHASES <= 0:
                return

            with ExitStack() as ph:
                xtok = [ph.enter_context(nc.sbuf_tensor(f"xtok{b}", [128, 4 * D], BF16)) for b in range(2)]
                stg = [ph.enter_context(nc.sbuf_tensor(f"stg{b}", [128, 32 * 512], BF16)) for b in range(2)]
                psT = [ph.enter_context(nc.psum_tensor(f"psT{b}", [128, 4096], BF16)) for b in range(2)]
                te_chunk, ls_chunk, ve_chunk = {}, {}, {}
                for i in range(16):
                    b = i % 2
                    if i >= 2:
                        gp.wait_ge(TE.sem, te_chunk[i - 2])
                    for p in range(4):
                        dmaG(xtok[b][:, p * D:(p + 1) * D],
                             x_full[i * 512 + p * 128: i * 512 + (p + 1) * 128, :], par=b)
                    te.wait_ge(LG[b].sem, LG[b].n)
                    if i >= 2:
                        te.wait_ge(LS[b].sem, ls_chunk[i - 2])
                    emit_tgroups(
                        128, 4, psT,
                        lambda k, b=b: xtok[b][:, (k % 4) * D + (k // 4) * 128: (k % 4) * D + (k // 4) * 128 + 128],
                        lambda jg, gi, b=b: stg[b][:, jg * 512: jg * 512 + gi * 128].rearrange("p (a q) -> p a q", q=128))
                    te_chunk[i] = TE.n
                    sy.wait_ge(VE.sem, VE.n)
                    for j in range(32):
                        dmaS(xfm[j * 128:(j + 1) * 128, i * 512:(i + 1) * 512],
                             stg[b][:, j * 512:(j + 1) * 512], par=b)
                    ls_chunk[i] = LS[b].n
                    ve_chunk[i] = VE.n
            barrier()
            if _PHASES <= 1:
                return

            with ExitStack() as ph:
                wband = ph.enter_context(nc.sbuf_tensor("wband", [128, 32 * 1024], BF16))
                xfc = [ph.enter_context(nc.sbuf_tensor(f"xfc{b}", [128, 32 * 512], BF16)) for b in range(2)]
                ckv = [ph.enter_context(nc.sbuf_tensor(f"ckv{b}", [128, 512], F32)) for b in range(2)]
                skv = [ph.enter_context(nc.sbuf_tensor(f"skv{b}", [128, 512], F32)) for b in range(2)]
                kb = [ph.enter_context(nc.sbuf_tensor(f"kb{b}", [128, 512], BF16)) for b in range(2)]
                m1 = ph.enter_context(nc.sbuf_tensor("m1t", [128, 512], F32))
                m2 = ph.enter_context(nc.sbuf_tensor("m2t", [128, 512], F32))
                kro = [ph.enter_context(nc.sbuf_tensor(f"kro{b}", [128, 512], BF16)) for b in range(2)]
                psK = [ph.enter_context(nc.psum_tensor(f"psK{b}", [128, 512], F32)) for b in range(2)]
                psP = [ph.enter_context(nc.psum_tensor(f"psP{b}", [128, 512], F32)) for b in range(2)]

                ik = 0
                ev_psK = {}
                ve_rope = {}
                lc_st = {}
                se_kb = {}
                te_chunkb = {}
                ve_chunkb = {}
                for band in range(8):
                    is_k = band < 4
                    colbase = D + band * 1024 if is_k else 2 * D + (band - 4) * 1024
                    gp.wait_ge(TE.sem, TE.n)
                    for j in range(32):
                        dmaG(wband[:, j * 1024:(j + 1) * 1024],
                             qkv_w[j * 128:(j + 1) * 128, colbase:colbase + 1024], par=0)
                    for i in range(16):
                        b = i % 2
                        if (band, i) != (0, 0):
                            key = (band, i - 2) if i >= 2 else (band - 1, 14 + i)
                            if key in te_chunkb:
                                sy.wait_ge(TE.sem, te_chunkb[key])
                            if key in ve_chunkb:
                                sy.wait_ge(VE.sem, ve_chunkb[key])
                        dmaS(xfc[b][:].rearrange("p (j t) -> p j t", t=512),
                             xfm.rearrange("(j p) t -> p j t", p=128)[:, :, i * 512:(i + 1) * 512], par=b)
                        if is_k:
                            dmaS(ckv[b][:], cosk[:, i * 512:(i + 1) * 512], par=b)
                            dmaS(skv[b][:], sink[:, i * 512:(i + 1) * 512], par=b)
                        te.wait_ge(LG[0].sem, LG[0].n)
                        te.wait_ge(LS[b].sem, LS[b].n)
                        if is_k:
                            for oc in range(8):
                                head = band * 8 + oc
                                pb = ik % 2
                                if ik >= 2:
                                    ctr, n = ev_psK[ik - 2]
                                    te.wait_ge(ctr.sem, n)
                                for ds in range(32):
                                    mm = te.matmul(psK[pb][:],
                                                   wband[:, ds * 1024 + oc * 128: ds * 1024 + oc * 128 + 128],
                                                   xfc[b][:, ds * 512:(ds + 1) * 512],
                                                   start=(ds == 0), stop=(ds == 31))
                                t = te_mark(mm)
                                sc_.wait_ge(TE.sem, t)
                                if ik >= 2:
                                    sc_.wait_ge(VE.sem, ve_rope[ik - 2])
                                sc_.activation(out=kb[pb][:], in_=psK[pb][:], func=AF.Identity,
                                               bias=sb_qkvbT[:, 32 + head:33 + head],
                                               scale=1.0).then_inc(SE.sem, 1)
                                SE.n += 1
                                ev_psK[ik] = (SE, SE.n)
                                se_kb[ik] = SE.n
                                te.wait_ge(SE.sem, SE.n)
                                if ik >= 2:
                                    te.wait_ge(VE.sem, ve_rope[ik - 2])
                                t = te_mark(te.matmul(psP[pb][:], sb_perm[:], kb[pb][:],
                                                      start=True, stop=True))
                                ve.wait_ge(TE.sem, t)
                                vstart()
                                ve.tensor_tensor(out=m1[:], in0=kb[pb][:], in1=ckv[b][:],
                                                 op=OP.mult)
                                ve.tensor_tensor(out=m2[:], in0=psP[pb][:], in1=skv[b][:],
                                                 op=OP.mult).then_inc(VE.sem, 1)
                                VE.n += 1
                                ve.wait_ge(VE.sem, VE.n)
                                if ik >= 2:
                                    ve.wait_ge(LC[pb].sem, lc_st[ik - 2])
                                ve.tensor_tensor(out=kro[pb][:], in0=m1[:], in1=m2[:],
                                                 op=OP.add).then_inc(VE.sem, 1)
                                VE.n += 1
                                ve_rope[ik] = VE.n
                                sc_.wait_ge(VE.sem, VE.n)
                                dmaC(kscr[head * 128:(head + 1) * 128, i * 512:(i + 1) * 512],
                                     kro[pb][:], par=pb)
                                lc_st[ik] = LC[pb].n
                                ik += 1
                        else:
                            vs = (band - 4) * 2
                            for sl in range(2):
                                for p in range(4):
                                    pb = ik % 2
                                    if ik >= 2:
                                        ctr, n = ev_psK[ik - 2]
                                        te.wait_ge(ctr.sem, n)
                                    for ds in range(32):
                                        te.matmul(psK[pb][:],
                                                  xfc[b][:, ds * 512 + p * 128: ds * 512 + (p + 1) * 128],
                                                  wband[:, ds * 1024 + sl * 512:(ds * 1024) + (sl + 1) * 512],
                                                  start=(ds == 0), stop=False)
                                    t = te_mark(te.matmul(psK[pb][:], sb_ones[:, 0:128],
                                                          sb_vbrow[:, (vs + sl) * 512:(vs + sl + 1) * 512],
                                                          start=False, stop=True))
                                    sc_.wait_ge(TE.sem, t)
                                    if ik >= 2:
                                        sc_.wait_ge(LC[pb].sem, lc_st[ik - 2])
                                    sc_.activation(out=kro[pb][:], in_=psK[pb][:],
                                                   func=AF.Copy).then_inc(SE.sem, 1)
                                    SE.n += 1
                                    ev_psK[ik] = (SE, SE.n)
                                    sc_.wait_ge(SE.sem, SE.n)
                                    dmaC(vscr[i * 512 + p * 128: i * 512 + (p + 1) * 128,
                                              (vs + sl) * 512:(vs + sl + 1) * 512],
                                         kro[pb][:], par=pb)
                                    lc_st[ik] = LC[pb].n
                                    ik += 1
                        te_chunkb[(band, i)] = TE.n
                        ve_chunkb[(band, i)] = VE.n
            barrier()
            if _PHASES <= 2:
                return

            with ExitStack() as ph:
                xof = ph.enter_context(nc.sbuf_tensor("xof", [128, 32 * NOWN], BF16))
                xot = ph.enter_context(nc.sbuf_tensor("xot", [128, 4 * D], BF16))
                wq = ph.enter_context(nc.sbuf_tensor("wq", [128, 32 * 512], BF16))
                cq = ph.enter_context(nc.sbuf_tensor("cq", [128, NOWN], F32))
                sqt = ph.enter_context(nc.sbuf_tensor("sqt", [128, NOWN], F32))
                qb = [ph.enter_context(nc.sbuf_tensor(f"qb{b}", [128, 512], BF16)) for b in range(2)]
                qro = [ph.enter_context(nc.sbuf_tensor(f"qro{b}", [128, 512], BF16)) for b in range(2)]
                qm1 = ph.enter_context(nc.sbuf_tensor("qm1", [128, 512], F32))
                qm2 = ph.enter_context(nc.sbuf_tensor("qm2", [128, 512], F32))
                psQ = [ph.enter_context(nc.psum_tensor(f"psQ{b}", [128, 512], F32)) for b in range(2)]
                psQP = [ph.enter_context(nc.psum_tensor(f"psQP{b}", [128, 512], F32)) for b in range(2)]
                psXB = [ph.enter_context(nc.psum_tensor(f"psXB{b}", [128, 2048], BF16)) for b in range(2)]
                dmaS(cq[:], cosq[:])
                dmaS(sqt[:], sinq[:])
                ve.wait_ge(LS[0].sem, LS[0].n)
                for i in range(2):
                    if i == 1:
                        gp.wait_ge(TE.sem, TE.n)
                    for p in range(4):
                        dmaG(xot[:, p * D:(p + 1) * D],
                             x_own[i * 512 + p * 128: i * 512 + (p + 1) * 128, :])
                    te.wait_ge(LG[0].sem, LG[0].n)
                    te.wait_ge(LS[0].sem, LS[0].n)
                    def _xof_dst(jg, gi, i=i):
                        j = (2 * jg) // 4
                        p0 = (2 * jg) % 4
                        base = j * NOWN + i * 512 + p0 * 128
                        return xof[:, base: base + gi * 128].rearrange("p (a q) -> p a q", q=128)
                    emit_tgroups(
                        128, 2, psXB,
                        lambda k: xot[:, (k % 4) * D + (k // 4) * 128: (k % 4) * D + (k // 4) * 128 + 128],
                        _xof_dst)
                te.wait_ge(VE.sem, VE.n)
                iq = 0
                ev_psQ, ve_ropeq, lc_q, se_qb = {}, {}, {}, {}
                for quarter in range(8):
                    gp.wait_ge(TE.sem, TE.n)
                    for j in range(32):
                        dmaG(wq[:, j * 512:(j + 1) * 512],
                             qkv_w[j * 128:(j + 1) * 128, quarter * 512:(quarter + 1) * 512])
                    te.wait_ge(LG[0].sem, LG[0].n)
                    for oc in range(4):
                        head = quarter * 4 + oc
                        for qc in range(2):
                            pb = iq % 2
                            if iq >= 2:
                                ctr, n = ev_psQ[iq - 2]
                                te.wait_ge(ctr.sem, n)
                            for ds in range(32):
                                mm = te.matmul(psQ[pb][:],
                                               wq[:, ds * 512 + oc * 128: ds * 512 + oc * 128 + 128],
                                               xof[:, ds * NOWN + qc * 512: ds * NOWN + (qc + 1) * 512],
                                               start=(ds == 0), stop=(ds == 31))
                            t = te_mark(mm)
                            sc_.wait_ge(TE.sem, t)
                            if iq >= 2:
                                sc_.wait_ge(VE.sem, ve_ropeq[iq - 2])
                            sc_.activation(out=qb[pb][:], in_=psQ[pb][:], func=AF.Identity,
                                           bias=sb_qkvbT[:, head:head + 1],
                                           scale=1.0).then_inc(SE.sem, 1)
                            SE.n += 1
                            ev_psQ[iq] = (SE, SE.n)
                            te.wait_ge(SE.sem, SE.n)
                            if iq >= 2:
                                te.wait_ge(VE.sem, ve_ropeq[iq - 2])
                            t = te_mark(te.matmul(psQP[pb][:], sb_perm[:], qb[pb][:],
                                                  start=True, stop=True))
                            ve.wait_ge(TE.sem, t)
                            vstart()
                            ve.tensor_tensor(out=qm1[:], in0=qb[pb][:],
                                             in1=cq[:, qc * 512:(qc + 1) * 512], op=OP.mult)
                            ve.tensor_tensor(out=qm2[:], in0=psQP[pb][:],
                                             in1=sqt[:, qc * 512:(qc + 1) * 512],
                                             op=OP.mult).then_inc(VE.sem, 1)
                            VE.n += 1
                            ve.wait_ge(VE.sem, VE.n)
                            if iq >= 2:
                                ve.wait_ge(LC[pb].sem, lc_q[iq - 2])
                            ve.tensor_tensor(out=qro[pb][:], in0=qm1[:], in1=qm2[:],
                                             op=OP.add).then_inc(VE.sem, 1)
                            VE.n += 1
                            ve_ropeq[iq] = VE.n
                            sc_.wait_ge(VE.sem, VE.n)
                            dmaC(qscr[head * 128:(head + 1) * 128, qc * 512:(qc + 1) * 512],
                                 qro[pb][:], par=pb)
                            lc_q[iq] = LC[pb].n
                            iq += 1
            barrier()
            if _PHASES <= 3:
                return

            with ExitStack() as ph:
                mks = [ph.enter_context(nc.sbuf_tensor(f"mks{b}", [128, 1024], BF16)) for b in range(2)]
                kh = [ph.enter_context(nc.sbuf_tensor(f"kh{b}", [128, S], BF16)) for b in range(2)]
                vh = [ph.enter_context(nc.sbuf_tensor(f"vh{b}", [128, 64 * 128], BF16)) for b in range(2)]
                qrow = [ph.enter_context(nc.sbuf_tensor(f"qrow{b}", [128, NOWN], BF16)) for b in range(2)]
                scf = [ph.enter_context(nc.sbuf_tensor(f"scf{b}", [128, S], F32)) for b in range(2)]
                prb = [ph.enter_context(nc.sbuf_tensor(f"prb{b}", [128, S], BF16)) for b in range(2)]
                ptr = ph.enter_context(nc.sbuf_tensor("ptr", [128, S], BF16))
                ctxo = ph.enter_context(nc.sbuf_tensor("ctxo", [128, 128], BF16))
                nmax = ph.enter_context(nc.sbuf_tensor("nmax", [128, 2], F32))
                ssum = ph.enter_context(nc.sbuf_tensor("ssum", [128, 2], F32))
                rsum = ph.enter_context(nc.sbuf_tensor("rsum", [128, 2], F32))
                psSc = [ph.enter_context(nc.psum_tensor(f"psSc{b}", [128, 512], F32)) for b in range(2)]
                psPT = [ph.enter_context(nc.psum_tensor(f"psPT{b}", [128, 2048], BF16)) for b in range(2)]
                psCtx = ph.enter_context(nc.psum_tensor("psCtx", [128, 128], F32))

                def load_head(h, b):
                    dmaS(kh[b][:], kscr[h * 128:(h + 1) * 128, :], par=b)
                    dmaS(vh[b][:].rearrange("p (m d) -> p m d", d=128),
                         vscr.rearrange("(m p) (hh d) -> p m hh d", p=128, d=128)[:, :, h, :], par=b)
                    dmaS(qrow[b][:], qscr[h * 128:(h + 1) * 128, :], par=b)
                    return LS[b].n

                hd_ls = {0: load_head(0, 0)}
                te_head = {}
                ve_ctx_prev = 0
                te_prb = {}
                ev_sc = {}
                isc = 0
                se_exp = {}
                ve_nm = {}
                ve_recip = {}
                se_evacs = {}
                ve_mask = {}
                pt_state = {0: 0, 1: 0}

                def stage_scores(g):
                    nonlocal isc
                    h, j = g // 8, g % 8
                    b = (g // 8) % 2
                    E = 1024 * (j + 1)
                    nch = E // 512
                    sl = j % 2
                    if g >= 2:
                        gp.wait_ge(VE.sem, ve_mask.get(g - 2, 0))
                    dmaG(mks[sl][:], maskin[j], par=sl)
                    sc_.wait_ge(SE.sem, SE.n)
                    for c in range(nch):
                        pb = isc % 2
                        if isc >= 2:
                            ctr, n = ev_sc[isc - 2]
                            te.wait_ge(ctr.sem, n)
                        t = te_mark(te.matmul(psSc[pb][:], qrow[b][:, j * 128:(j + 1) * 128],
                                              kh[b][:, c * 512:(c + 1) * 512],
                                              start=True, stop=True))
                        if c < nch - 2:
                            sc_.wait_ge(TE.sem, t)
                            sc_.activation(out=scf[sl][:, c * 512:(c + 1) * 512],
                                           in_=psSc[pb][:], func=AF.Copy,
                                           scale=SCALE).then_inc(SE.sem, 1)
                            SE.n += 1
                            ev_sc[isc] = (SE, SE.n)
                        else:
                            mi = c - (nch - 2)
                            ve.wait_ge(TE.sem, t)
                            ve.wait_ge(LG[sl].sem, LG[sl].n)
                            ve.wait_ge(SE.sem, se_exp.get(g - 2, 0))
                            vstart()
                            ve.scalar_tensor_tensor(
                                out=scf[sl][:, c * 512:(c + 1) * 512], in0=psSc[pb][:],
                                scalar=SCALE,
                                in1=mks[sl][:, mi * 512:(mi + 1) * 512],
                                op0=OP.mult, op1=OP.add).then_inc(VE.sem, 1)
                            VE.n += 1
                            ev_sc[isc] = (VE, VE.n)
                        isc += 1
                    ve_mask[g] = VE.n
                    se_evacs[g] = SE.n

                def stage_softmax(g):
                    h, j = g // 8, g % 8
                    E = 1024 * (j + 1)
                    sl = j % 2
                    ve.wait_ge(SE.sem, se_evacs[g])
                    vstart()
                    ve.reduce_max(out=nmax[:, sl:sl + 1], in_=scf[sl][:, 0:E], axis=AX,
                                  negate=True).then_inc(VE.sem, 1)
                    VE.n += 1
                    ve_nm[g] = VE.n
                    sc_.wait_ge(VE.sem, VE.n)
                    sc_.wait_ge(TE.sem, te_prb.get(g - 2, 0))
                    sc_.activation(out=prb[sl][:, 0:E], in_=scf[sl][:, 0:E], func=AF.Exp,
                                   bias=nmax[:, sl:sl + 1], scale=1.0,
                                   accum_out=ssum[:, sl:sl + 1]).then_inc(SE.sem, 1)
                    SE.n += 1
                    se_exp[g] = SE.n
                    ve.wait_ge(SE.sem, SE.n)
                    vstart()
                    ve.reciprocal(out=rsum[:, sl:sl + 1], in_=ssum[:, sl:sl + 1]).then_inc(VE.sem, 1)
                    VE.n += 1
                    ve_recip[g] = VE.n

                def stage_ctx(g):
                    nonlocal ve_ctx_prev
                    h, j = g // 8, g % 8
                    b = (g // 8) % 2
                    E = 1024 * (j + 1)
                    sl = j % 2
                    nkt = E // 128
                    te.wait_ge(SE.sem, se_exp[g])
                    ngr = nkt // 2
                    for jg in range(ngr):
                        bank = jg % 2
                        if pt_state[bank]:
                            te.wait_ge(VE.sem, pt_state[bank])
                        ti = None
                        for jj in range(2):
                            kt = jg * 2 + jj
                            ti = te.transpose(psPT[bank][:, jj * 1024: jj * 1024 + 128],
                                              prb[sl][:, kt * 128:(kt + 1) * 128], sb_id[:])
                        t = te_mark(ti)
                        ve.wait_ge(TE.sem, t)
                        vstart()
                        ve.tensor_copy(
                            out=ptr[:, jg * 256: jg * 256 + 256].rearrange("p (a q) -> p a q", q=128),
                            in_=psPT[bank][:].rearrange("p (a q) -> p a q", q=1024)[:, 0:2, 0:128],
                        ).then_inc(VE.sem, 1)
                        VE.n += 1
                        pt_state[bank] = VE.n
                    te_prb[g] = TE.n
                    te.wait_ge(VE.sem, VE.n)
                    te.wait_ge(VE.sem, ve_ctx_prev)
                    mm = None
                    for kt in range(nkt):
                        mm = te.matmul(psCtx[:], ptr[:, kt * 128:(kt + 1) * 128],
                                       vh[b][:, kt * 128:(kt + 1) * 128],
                                       start=(kt == 0), stop=(kt == nkt - 1))
                    t = te_mark(mm)
                    ve.wait_ge(TE.sem, t)
                    ve.wait_ge(VE.sem, ve_recip[g])
                    ve.wait_ge(LC[0].sem, LC[0].n)
                    vstart()
                    ve.tensor_scalar(out=ctxo[:], in0=psCtx[:],
                                     scalar1=rsum[:, sl:sl + 1], scalar2=None,
                                     op0=OP.mult).then_inc(VE.sem, 1)
                    VE.n += 1
                    ve_ctx_prev = VE.n
                    sc_.wait_ge(VE.sem, VE.n)
                    dmaC(ctxscr[j * 128:(j + 1) * 128, h * 128:(h + 1) * 128], ctxo[:])

                NG = H * 8
                for g in range(NG + 2):
                    if g < NG:
                        h, j = g // 8, g % 8
                        if j == 0:
                            te.wait_ge(LS[(g // 8) % 2].sem, hd_ls[h])
                        if j == 2 and h + 1 < H:
                            sy.wait_ge(TE.sem, te_head.get(h - 1, 0))
                            hd_ls[h + 1] = load_head(h + 1, (h + 1) % 2)
                        stage_scores(g)
                    if 0 <= g - 1 < NG:
                        stage_softmax(g - 1)
                    if 0 <= g - 2 < NG:
                        stage_ctx(g - 2)
                        if (g - 2) % 8 == 7:
                            te_head[(g - 2) // 8] = TE.n
            barrier()

            with ExitStack() as ph:
                cfm_all = ph.enter_context(nc.sbuf_tensor("cfm_all", [128, 8 * 4096], BF16))
                wo = ph.enter_context(nc.sbuf_tensor("wo", [128, 32 * 512], BF16))
                ctt = [ph.enter_context(nc.sbuf_tensor(f"ctt{b}", [128, D], BF16)) for b in range(2)]
                xos = [ph.enter_context(nc.sbuf_tensor(f"xos{b}", [128, 512], F32)) for b in range(2)]
                r1o = [ph.enter_context(nc.sbuf_tensor(f"r1o{b}", [128, 512], F32)) for b in range(2)]
                psO = [ph.enter_context(nc.psum_tensor(f"psO{b}", [128, 512], F32)) for b in range(2)]
                psCT = [ph.enter_context(nc.psum_tensor(f"psCT{b}", [128, 2048], BF16)) for b in range(2)]
                te_ct = {}
                for tt in range(8):
                    b = tt % 2
                    if tt >= 2:
                        sy.wait_ge(TE.sem, te_ct[tt - 2])
                    dmaS(ctt[b][:], ctxscr[tt * 128:(tt + 1) * 128, :], par=b)
                    te.wait_ge(LS[b].sem, LS[b].n)
                    emit_tgroups(
                        32, 2, psCT,
                        lambda k, b=b: ctt[b][:, k * 128:(k + 1) * 128],
                        lambda jg, gi, tt=tt: cfm_all[:, tt * 4096 + jg * 256: tt * 4096 + jg * 256 + gi * 128].rearrange("p (a q) -> p a q", q=128))
                    te_ct[tt] = TE.n
                te.wait_ge(VE.sem, VE.n)
                sy.wait_ge(LS[0].sem, LS[0].n)
                sy.wait_ge(LS[1].sem, LS[1].n)
                io = 0
                ve_o, lc_o = {}, {}
                for os_ in range(8):
                    gp.wait_ge(TE.sem, TE.n)
                    for j in range(32):
                        dmaG(wo[:, j * 512:(j + 1) * 512],
                             out_w[j * 128:(j + 1) * 128, os_ * 512:(os_ + 1) * 512], par=0)
                    te.wait_ge(LG[0].sem, LG[0].n)
                    for tt in range(8):
                        pb = io % 2
                        if io >= 2:
                            sy.wait_ge(VE.sem, ve_o[io - 2])
                        dmaS(xos[pb][:],
                             x_own[tt * 128:(tt + 1) * 128, os_ * 512:(os_ + 1) * 512],
                             par=pb)
                        if io >= 2:
                            te.wait_ge(VE.sem, ve_o[io - 2])
                        for ds in range(32):
                            te.matmul(psO[pb][:],
                                      cfm_all[:, tt * 4096 + ds * 128:(tt * 4096) + (ds + 1) * 128],
                                      wo[:, ds * 512:(ds + 1) * 512],
                                      start=(ds == 0), stop=False)
                        t = te_mark(te.matmul(psO[pb][:], sb_ones[:, 0:128],
                                              sb_obrow[:, os_ * 512:(os_ + 1) * 512],
                                              start=False, stop=True))
                        ve.wait_ge(TE.sem, t)
                        ve.wait_ge(LS[pb].sem, LS[pb].n)
                        vstart()
                        if io >= 2:
                            ve.wait_ge(LC[pb].sem, lc_o[io - 2])
                        ve.tensor_tensor(out=r1o[pb][:], in0=psO[pb][:], in1=xos[pb][:],
                                         op=OP.add).then_inc(VE.sem, 1)
                        VE.n += 1
                        ve_o[io] = VE.n
                        sc_.wait_ge(VE.sem, VE.n)
                        dmaC(r1scr[tt * 128:(tt + 1) * 128, os_ * 512:(os_ + 1) * 512],
                             r1o[pb][:], par=pb)
                        lc_o[io] = LC[pb].n
                        io += 1
            barrier()
            if _PHASES <= 4:
                return

            x2fm_es = ExitStack()
            x2fm = x2fm_es.enter_context(nc.sbuf_tensor("x2fm", [128, 32 * NOWN], BF16))

            def ln_rows(src, dst, ph):
                rrow, xc, sqr, x2r = ph["rrow"], ph["xc"], ph["sqr"], ph["x2r"]
                st1, st2 = ph["st1"], ph["st2"]
                sy.wait_ge(VE.sem, VE.n)
                dmaS(rrow[:], src)
                ve.wait_ge(LS[0].sem, LS[0].n)
                vstart()
                ve.reduce_sum(out=st1[:], in_=rrow[:], axis=AX).then_inc(VE.sem, 1)
                VE.n += 1
                ve.wait_ge(VE.sem, VE.n)
                ve.tensor_scalar(out=st1[:], in0=st1[:], scalar1=1.0 / D, scalar2=None,
                                 op0=OP.mult).then_inc(VE.sem, 1)
                VE.n += 1
                ve.wait_ge(VE.sem, VE.n)
                ve.tensor_scalar(out=xc[:], in0=rrow[:], scalar1=st1[:, 0:1], scalar2=None,
                                 op0=OP.subtract).then_inc(VE.sem, 1)
                VE.n += 1
                sc_.wait_ge(VE.sem, VE.n)
                sc_.activation(out=sqr[:], in_=xc[:], func=AF.Square).then_inc(SE.sem, 1)
                SE.n += 1
                ve.wait_ge(SE.sem, SE.n)
                vstart()
                ve.reduce_sum(out=st2[:], in_=sqr[:], axis=AX).then_inc(VE.sem, 1)
                VE.n += 1
                ve.wait_ge(VE.sem, VE.n)
                ve.tensor_scalar(out=st2[:], in0=st2[:], scalar1=1.0 / D, scalar2=float(EPS),
                                 op0=OP.mult, op1=OP.add).then_inc(VE.sem, 1)
                VE.n += 1
                sc_.wait_ge(VE.sem, VE.n)
                sc_.activation(out=st2[:], in_=st2[:], func=AF.Sqrt).then_inc(SE.sem, 1)
                SE.n += 1
                ve.wait_ge(SE.sem, SE.n)
                vstart()
                ve.reciprocal(out=st2[:], in_=st2[:]).then_inc(VE.sem, 1)
                VE.n += 1
                ve.wait_ge(VE.sem, VE.n)
                ve.wait_ge(LC[0].sem, LC[0].n)
                ve.tensor_scalar(out=x2r[:], in0=xc[:], scalar1=st2[:, 0:1], scalar2=None,
                                 op0=OP.mult).then_inc(VE.sem, 1)
                VE.n += 1
                sc_.wait_ge(VE.sem, VE.n)
                if dst is not None:
                    dmaC(dst, x2r[:])

            with ExitStack() as ph0:
                ph = {k: ph0.enter_context(nc.sbuf_tensor(k, [128, D], F32))
                      for k in ("rrow", "xc", "sqr", "x2r")}
                ph["st1"] = ph0.enter_context(nc.sbuf_tensor("st1", [128, 1], F32))
                ph["st2"] = ph0.enter_context(nc.sbuf_tensor("st2", [128, 1], F32))
                x2b = ph0.enter_context(nc.sbuf_tensor("x2b", [128, D], BF16))
                psX2 = [ph0.enter_context(nc.psum_tensor(f"psX2{b}", [128, 4096], BF16)) for b in range(2)]
                for tt in range(8):
                    ln_rows(r1scr[tt * 128:(tt + 1) * 128, :],
                            x2scr[tt * 128:(tt + 1) * 128, :], ph)
                    sc_.wait_ge(TE.sem, TE.n)
                    sc_.activation(out=x2b[:], in_=ph["x2r"][:], func=AF.Copy).then_inc(SE.sem, 1)
                    SE.n += 1
                    te.wait_ge(SE.sem, SE.n)
                    x2v = x2fm[:].rearrange("p (j t) -> p j t", t=NOWN)
                    emit_tgroups(
                        32, 4, psX2,
                        lambda k: x2b[:, k * 128:(k + 1) * 128],
                        lambda jg, gi, tt=tt: x2v[:, jg * 4: jg * 4 + gi, tt * 128:(tt + 1) * 128])
            barrier()
            if _PHASES <= 6:
                return

            with ExitStack() as ph:
                pk1 = [ph.enter_context(nc.sbuf_tensor(f"pk1{b}", [128, 2048], I32)) for b in range(2)]
                pk3 = [ph.enter_context(nc.sbuf_tensor(f"pk3{b}", [128, 2048], I32)) for b in range(2)]
                sc1 = [ph.enter_context(nc.sbuf_tensor(f"sc1{b}", [128, 64], F32)) for b in range(2)]
                sc3 = [ph.enter_context(nc.sbuf_tensor(f"sc3{b}", [128, 64], F32)) for b in range(2)]
                msb = ph.enter_context(nc.sbuf_tensor("msb", [128, 2048], I32))
                lsb = ph.enter_context(nc.sbuf_tensor("lsb", [128, 2048], I32))
                nat = [ph.enter_context(nc.sbuf_tensor(f"nat{b}", [128, D], BF16)) for b in range(2)]
                w1t = [ph.enter_context(nc.sbuf_tensor(f"w1t{b}", [128, D], BF16)) for b in range(2)]
                w3t = [ph.enter_context(nc.sbuf_tensor(f"w3t{b}", [128, D], BF16)) for b in range(2)]
                sil = [ph.enter_context(nc.sbuf_tensor(f"sil{b}", [128, 512], F32)) for b in range(2)]
                hao = [ph.enter_context(nc.sbuf_tensor(f"hao{b}", [128, 512], BF16)) for b in range(2)]
                psA = [ph.enter_context(nc.psum_tensor(f"psA{b}", [128, 512], F32)) for b in range(2)]
                psB = [ph.enter_context(nc.psum_tensor(f"psB{b}", [128, 512], F32)) for b in range(2)]
                psW = [ph.enter_context(nc.psum_tensor(f"psW{b}", [128, 2048], BF16)) for b in range(2)]

                w1r = w1p.rearrange("(h r) b -> h (r b)", r=32)
                w3r = w3p.rearrange("(h r) b -> h (r b)", r=32)
                s1r = s1v.rearrange("(h g) -> h g", g=64)
                s3r = s3v.rearrange("(h g) -> h g", g=64)

                def deq_tile(pk, scv, outT, slot, par):
                    ve.wait_ge(LS[par].sem, LS[par].n)
                    vstart()
                    ve.tensor_scalar(out=msb[:], in0=pk[:], scalar1=4, scalar2=None,
                                     op0=OP.arith_shift_right)
                    ve.tensor_scalar(out=lsb[:], in0=pk[:], scalar1=28, scalar2=28,
                                     op0=OP.arith_shift_left,
                                     op1=OP.arith_shift_right).then_inc(VE.sem, 1)
                    VE.n += 1
                    ve.wait_ge(VE.sem, VE.n)
                    nb = nat[slot]
                    natv = nb[:].rearrange("p (m z c) -> p z m c", z=2, c=64)
                    mv = msb[:].rearrange("p (m c) -> p m c", c=64)
                    lv = lsb[:].rearrange("p (m c) -> p m c", c=64)
                    sev = scv[:].rearrange("p (m z) -> p z m", z=2)[:, 0].unsqueeze(2).broadcast_to([128, 32, 64])
                    sov = scv[:].rearrange("p (m z) -> p z m", z=2)[:, 1].unsqueeze(2).broadcast_to([128, 32, 64])
                    ve.tensor_tensor(out=natv[:, 0], in0=mv, in1=sev, op=OP.mult)
                    ve.tensor_tensor(out=natv[:, 1], in0=lv, in1=sov, op=OP.mult).then_inc(VE.sem, 1)
                    VE.n += 1
                    te.wait_ge(VE.sem, VE.n)
                    emit_tgroups(
                        32, 2, psW,
                        lambda k, nb=nb: nb[:, k * 128:(k + 1) * 128],
                        lambda jg, gi, outT=outT: outT[:, jg * 256: jg * 256 + gi * 128].rearrange("p (a q) -> p a q", q=128),
                        evac="se")

                te_ht, se_ab, ve_ab, lc_ab = {}, {}, {}, {}
                for ht in range(NHT):
                    b = ht % 2
                    if ht >= 2:
                        sy.wait_ge(VE.sem, ve_ab.get(2 * (ht - 2) + 1, 0))
                    dmaS(pk1[b][:], w1r[ht * 128:(ht + 1) * 128, :], par=b)
                    dmaS(sc1[b][:], s1r[ht * 128:(ht + 1) * 128, :], par=b)
                    dmaS(pk3[b][:], w3r[ht * 128:(ht + 1) * 128, :], par=b)
                    dmaS(sc3[b][:], s3r[ht * 128:(ht + 1) * 128, :], par=b)
                    if ht >= 2:
                        ve.wait_ge(TE.sem, te_ht[ht - 2])
                    deq_tile(pk1[b], sc1[b], w1t[b], 0, b)
                    deq_tile(pk3[b], sc3[b], w3t[b], 1, b)
                    te.wait_ge(VE.sem, VE.n)
                    te.wait_ge(SE.sem, SE.n)
                    for c in range(2):
                        it = ht * 2 + c
                        pb = c
                        if it >= 2:
                            te.wait_ge(SE.sem, se_ab[it - 2])
                            te.wait_ge(VE.sem, ve_ab[it - 2])
                        for ds in range(32):
                            te.matmul(psA[pb][:], w1t[b][:, ds * 128:(ds + 1) * 128],
                                      x2fm[:, ds * NOWN + c * 512: ds * NOWN + (c + 1) * 512],
                                      start=(ds == 0), stop=(ds == 31))
                        for ds in range(32):
                            mm = te.matmul(psB[pb][:], w3t[b][:, ds * 128:(ds + 1) * 128],
                                           x2fm[:, ds * NOWN + c * 512: ds * NOWN + (c + 1) * 512],
                                           start=(ds == 0), stop=(ds == 31))
                        t = te_mark(mm)
                        sc_.wait_ge(TE.sem, t)
                        if it >= 2:
                            sc_.wait_ge(VE.sem, ve_ab[it - 2])
                        sc_.activation(out=sil[pb][:], in_=psA[pb][:],
                                       func=AF.Silu).then_inc(SE.sem, 1)
                        SE.n += 1
                        se_ab[it] = SE.n
                        ve.wait_ge(SE.sem, SE.n)
                        vstart()
                        if it >= 2:
                            ve.wait_ge(LC[pb].sem, lc_ab[it - 2])
                        ve.tensor_tensor(out=hao[pb][:], in0=sil[pb][:], in1=psB[pb][:],
                                         op=OP.mult).then_inc(VE.sem, 1)
                        VE.n += 1
                        ve_ab[it] = VE.n
                        sc_.wait_ge(VE.sem, VE.n)
                        dmaC(hactscr[ht * 128:(ht + 1) * 128, c * 512:(c + 1) * 512],
                             hao[pb][:], par=pb)
                        lc_ab[it] = LC[pb].n
                    te_ht[ht] = TE.n
            barrier()
            x2fm_es.close()
            if _PHASES <= 7:
                return

            with ExitStack() as ph:
                natS = ph.enter_context(nc.sbuf_tensor("natS", [128, 32 * 1024], BF16))
                pk2 = [ph.enter_context(nc.sbuf_tensor(f"pk2{b}", [128, 512], I32)) for b in range(2)]
                sc2 = [ph.enter_context(nc.sbuf_tensor(f"sc2{b}", [128, 172], F32)) for b in range(2)]
                msb2 = ph.enter_context(nc.sbuf_tensor("msb2", [128, 512], I32))
                lsb2 = ph.enter_context(nc.sbuf_tensor("lsb2", [128, 512], I32))
                w2st = ph.enter_context(nc.sbuf_tensor("w2st", [128, D], BF16))
                psW2 = [ph.enter_context(nc.psum_tensor(f"psW2{b}", [128, 4096], BF16)) for b in range(2)]
                w2r = w2p.rearrange("(d r) b -> d (r b)", r=86)
                s2r = s2v.rearrange("(d g) -> d g", g=172)
                ve_w2 = {}
                te_sup = {}
                for sup in range(11):
                    HS = 1024 if sup < 10 else 768
                    NM = HS // 128
                    for dt in range(32):
                        it = sup * 32 + dt
                        b = it % 2
                        if it >= 2:
                            sy.wait_ge(VE.sem, ve_w2[it - 2])
                        dmaS(pk2[b][:, 0:NM * 64],
                             w2r[dt * 128:(dt + 1) * 128, sup * 512: sup * 512 + NM * 64], par=b)
                        dmaS(sc2[b][:], s2r[dt * 128:(dt + 1) * 128, :], par=b)
                        ve.wait_ge(LS[b].sem, LS[b].n)
                        if dt == 0 and sup >= 1:
                            ve.wait_ge(TE.sem, te_sup[sup - 1])
                        vstart()
                        ve.tensor_scalar(out=msb2[:, 0:NM * 64], in0=pk2[b][:, 0:NM * 64],
                                         scalar1=4, scalar2=None,
                                         op0=OP.arith_shift_right)
                        ve.tensor_scalar(out=lsb2[:, 0:NM * 64], in0=pk2[b][:, 0:NM * 64],
                                         scalar1=28, scalar2=28,
                                         op0=OP.arith_shift_left,
                                         op1=OP.arith_shift_right).then_inc(VE.sem, 1)
                        VE.n += 1
                        ve.wait_ge(VE.sem, VE.n)
                        natv = natS[:, dt * 1024: dt * 1024 + HS].rearrange(
                            "p (m z c) -> p z m c", z=2, c=64)
                        mv = msb2[:, 0:NM * 64].rearrange("p (m c) -> p m c", c=64)
                        lv = lsb2[:, 0:NM * 64].rearrange("p (m c) -> p m c", c=64)
                        sall = sc2[b][:, 16 * sup: 16 * sup + 2 * NM]
                        sev = sall.rearrange("p (m z) -> p z m", z=2)[:, 0].unsqueeze(2).broadcast_to([128, NM, 64])
                        sov = sall.rearrange("p (m z) -> p z m", z=2)[:, 1].unsqueeze(2).broadcast_to([128, NM, 64])
                        ve.tensor_tensor(out=natv[:, 0], in0=mv, in1=sev, op=OP.mult)
                        ve.tensor_tensor(out=natv[:, 1], in0=lv, in1=sov, op=OP.mult).then_inc(VE.sem, 1)
                        VE.n += 1
                        ve_w2[it] = VE.n
                    te.wait_ge(VE.sem, VE.n)
                    for hb in range(NM):
                        sc_.wait_ge(LC[0].sem, LC[0].n)
                        emit_tgroups(
                            32, 4, psW2,
                            lambda k, hb=hb: natS[:, k * 1024 + hb * 128: k * 1024 + (hb + 1) * 128],
                            lambda jg, gi: w2st[:, jg * 512: jg * 512 + gi * 128].rearrange("p (a q) -> p a q", q=128),
                            evac="se")
                        sc_.wait_ge(SE.sem, SE.n)
                        dmaC(w2tscr[sup * 1024 + hb * 128: sup * 1024 + (hb + 1) * 128, :],
                             w2st[:])
                    te_sup[sup] = TE.n
            barrier()
            if _PHASES <= 8:
                return

            with ExitStack() as ph:
                hact_sb = ph.enter_context(nc.sbuf_tensor("hact_sb", [128, NHT * NOWN], BF16))
                w2b = [ph.enter_context(nc.sbuf_tensor(f"w2b{b}", [128, 512], BF16)) for b in range(2)]
                x2s = ph.enter_context(nc.sbuf_tensor("x2s", [128, 512], F32))
                r2o = ph.enter_context(nc.sbuf_tensor("r2o", [128, 512], F32))
                psF = [ph.enter_context(nc.psum_tensor(f"psF{t}", [128, 512], F32)) for t in range(8)]
                for hs in range(NHT):
                    dmaG(hact_sb[:, hs * NOWN:(hs + 1) * NOWN],
                         hactscr[hs * 128:(hs + 1) * 128, :], par=1)
                te.wait_ge(LG[1].sem, LG[1].n)
                te_f = {}
                ve_ds = 0
                for ds in range(8):
                    for hs in range(NHT):
                        it = ds * NHT + hs
                        b = it % 2
                        if it >= 2:
                            sy.wait_ge(TE.sem, te_f[it - 2])
                        dmaS(w2b[b][:], w2tscr[hs * 128:(hs + 1) * 128, ds * 512:(ds + 1) * 512], par=b)
                        te.wait_ge(LS[b].sem, LS[b].n)
                        if hs == 0:
                            te.wait_ge(VE.sem, ve_ds)
                        for tt in range(8):
                            mm = te.matmul(psF[tt][:],
                                           hact_sb[:, hs * NOWN + tt * 128: hs * NOWN + (tt + 1) * 128],
                                           w2b[b][:], start=(hs == 0), stop=(hs == NHT - 1))
                        te_f[it] = te_mark(mm)
                    for tt in range(8):
                        gp.wait_ge(VE.sem, VE.n)
                        dmaG(x2s[:], x2scr[tt * 128:(tt + 1) * 128, ds * 512:(ds + 1) * 512])
                        ve.wait_ge(TE.sem, TE.n)
                        ve.wait_ge(LG[0].sem, LG[0].n)
                        vstart()
                        ve.wait_ge(LC[0].sem, LC[0].n)
                        ve.tensor_tensor(out=r2o[:], in0=psF[tt][:], in1=x2s[:],
                                         op=OP.add).then_inc(VE.sem, 1)
                        VE.n += 1
                        sc_.wait_ge(VE.sem, VE.n)
                        dmaC(r2scr[tt * 128:(tt + 1) * 128, ds * 512:(ds + 1) * 512],
                             r2o[:])
                    ve_ds = VE.n
            barrier()
            if _PHASES <= 9:
                return

            with ExitStack() as ph0:
                ph = {k: ph0.enter_context(nc.sbuf_tensor(k + "b", [128, D], F32))
                      for k in ("rrow", "xc", "sqr", "x2r")}
                ph["st1"] = ph0.enter_context(nc.sbuf_tensor("st1b", [128, 1], F32))
                ph["st2"] = ph0.enter_context(nc.sbuf_tensor("st2b", [128, 1], F32))
                for tt in range(8):
                    ln_rows(r2scr[tt * 128:(tt + 1) * 128, :],
                            out_own[tt * 128:(tt + 1) * 128, :], ph)
                for c in LC:
                    sy.wait_ge(c.sem, c.n)
                for c in LS:
                    sy.wait_ge(c.sem, c.n)
                for c in LG:
                    sy.wait_ge(c.sem, c.n)

    return nc


def _host_inputs(inputs, core):
    x = np.asarray(inputs["L0_attn_hidden_states"]).reshape(S, D)
    qkv_b = np.asarray(inputs["L0_attn_qkv_b"])
    out_b = np.asarray(inputs["L0_attn_out_b"])

    own = np.concatenate([np.arange(128 * (8 * j + core), 128 * (8 * j + core) + 128)
                          for j in range(8)])
    mask = np.zeros((8, 128, 1024), np.float32)
    for j in range(8):
        q = 128 * (8 * j + core) + np.arange(128)[:, None]
        k = 1024 * j + np.arange(1024)[None, :]
        mask[j] = np.where(k <= q, 0.0, np.float32(NEG))
    inv = 1.0 / (10000.0 ** (np.arange(64, dtype=np.float64) / 64.0))
    pos = np.arange(S, dtype=np.float64)
    ang = pos[None, :] * inv[:, None]
    cosk_ = np.cos(ang).astype(np.float32)
    sink_ = np.sin(ang).astype(np.float32)
    cosk_ = np.concatenate([cosk_, cosk_], axis=0)
    sink_ = np.concatenate([sink_, sink_], axis=0)

    perm = np.zeros((128, 128), np.float32)
    for p in range(64):
        perm[p, p + 64] = -1.0
        perm[p + 64, p] = 1.0

    return {
        "x_full": x,
        "x_own": np.ascontiguousarray(x[own]),
        "qkv_w": np.asarray(inputs["L0_attn_qkv_w"]),
        "qkv_bT": np.ascontiguousarray(qkv_b.reshape(96, 128).T),
        "vb_row": np.ascontiguousarray(qkv_b[8192:].reshape(1, D)),
        "out_w": np.asarray(inputs["L0_attn_out_w"]),
        "ob_row": np.ascontiguousarray(out_b.reshape(1, D)),
        "w1p": np.asarray(inputs["L0_ffn_w1"]),
        "s1v": np.asarray(inputs["L0_ffn_s1"]),
        "w2p": np.asarray(inputs["L0_ffn_w2"]),
        "s2v": np.asarray(inputs["L0_ffn_s2"]),
        "w3p": np.asarray(inputs["L0_ffn_w3"]),
        "s3v": np.asarray(inputs["L0_ffn_s3"]),
        "maskin": mask.astype(ml_dtypes.bfloat16),
        "cosk": np.ascontiguousarray(cosk_),
        "sink": np.ascontiguousarray(sink_),
        "cosq": np.ascontiguousarray(cosk_[:, own]),
        "sinq": np.ascontiguousarray(sink_[:, own]),
        "identb": np.eye(128).astype(ml_dtypes.bfloat16),
        "permT": np.ascontiguousarray(perm.T).astype(ml_dtypes.bfloat16),
    }, own


def kernel(L0_attn_hidden_states, L0_attn_qkv_w, L0_attn_qkv_b, L0_attn_out_w,
           L0_attn_out_b, L0_ffn_w1, L0_ffn_s1, L0_ffn_w2, L0_ffn_s2,
           L0_ffn_w3, L0_ffn_s3):
    inputs = dict(
        L0_attn_hidden_states=L0_attn_hidden_states,
        L0_attn_qkv_w=L0_attn_qkv_w, L0_attn_qkv_b=L0_attn_qkv_b,
        L0_attn_out_w=L0_attn_out_w, L0_attn_out_b=L0_attn_out_b,
        L0_ffn_w1=L0_ffn_w1, L0_ffn_s1=L0_ffn_s1,
        L0_ffn_w2=L0_ffn_w2, L0_ffn_s2=L0_ffn_s2,
        L0_ffn_w3=L0_ffn_w3, L0_ffn_s3=L0_ffn_s3,
    )
    nc = _build()
    in_maps, owns = [], []
    for c in range(8):
        m, own = _host_inputs(inputs, c)
        in_maps.append(m)
        owns.append(own)
    res = run_bass_kernel_spmd(nc, in_maps, core_ids=list(range(8)))
    out = np.empty((1, S, D), np.float32)
    for c in range(8):
        out[0, owns[c]] = res.results[c]["out_own"]
    return out
